# version 18
# baseline (speedup 1.0000x reference)
import contextlib
import numpy as np
import ml_dtypes

import concourse.bass as bass
import concourse.mybir as mybir
from concourse.bass_utils import run_bass_kernel_spmd

F32 = mybir.dt.float32
BF16 = mybir.dt.bfloat16
AF = mybir.ActivationFunctionType
ALU = mybir.AluOpType
AX = mybir.AxisListType

BIG = 1.0e30
EPS = 1.0e-5

REAL = dict(n_cores=8, batch=2, seq=16384, d_model=1024, d_inner=2048,
            d_state=16, ct=4096, tb=512, cores_per_batch=4)


class Ctr:
    def __init__(self, sem):
        self.sem = sem
        self.v = 0


def build_program(p):
    n_cores = p["n_cores"]
    D = p["d_model"]
    E = p["d_inner"]
    NS = p["d_state"]
    CT = p["ct"]
    TB = p["tb"]
    L = 128
    ND = D // 128
    NE = E // 128
    NB = CT // TB
    NCH = TB // L
    NSS = 2 * NS + 1
    SWW = TB + 4

    nc = bass.Bass("TRN2", target_bir_lowering=False, debug=False,
                   detect_race_conditions=False)
    dp = nc.declare_dram_parameter

    x_in = dp("x", [D, CT + 4], F32, isOutput=False)
    w_inx = dp("w_inx", [D, E], F32, isOutput=False)
    w_inz = dp("w_inz", [D, E], F32, isOutput=False)
    w_xp = dp("w_xp", [E, NSS], F32, isOutput=False)
    w_out = dp("w_out", [E, D], F32, isOutput=False)
    w_r = dp("w_r", [D, D], F32, isOutput=False)
    w_k = dp("w_k", [D, D], F32, isOutput=False)
    w_v = dp("w_v", [D, D], F32, isOutput=False)
    w_cd = dp("w_cd", [E, 512], F32, isOutput=False)
    cb_col = dp("cb_col", [128, NE], F32, isOutput=False)
    d_colp = dp("d_col", [128, NE], F32, isOutput=False)
    mur_col = dp("mur_col", [128, ND], F32, isOutput=False)
    muk_col = dp("muk_col", [128, ND], F32, isOutput=False)
    alog_row = dp("alog_row", [1, NS], F32, isOutput=False)
    c_id_bf = dp("c_id_bf", [128, 128], BF16, isOutput=False)
    c_id_f32 = dp("c_id_f32", [128, 128], F32, isOutput=False)
    c_tri = dp("c_tri", [128, 128], F32, isOutput=False)
    c_mask = dp("c_mask", [128, 128], F32, isOutput=False)
    c_ones = dp("c_ones", [1, 128], F32, isOutput=False)
    c_allones = dp("c_allones", [128, 128], F32, isOutput=False)
    c_maskrow = dp("c_maskrow", [1, n_cores], F32, isOutput=False)
    c_predT = dp("c_predT", [n_cores, n_cores], F32, isOutput=False)
    c_hscale = dp("c_hscale", [1, 1], F32, isOutput=False)
    out_d = dp("out", [D, CT], F32, isOutput=True)
    DBG = bool(p.get("dbg"))
    if DBG:
        d_xdbl = dp("d_xdbl", [NSS, CT], F32, isOutput=True)
        d_hloc = dp("d_hloc", [NS, E], F32, isOutput=True)
        d_hin = dp("d_hin", [NS, E], F32, isOutput=True)
        d_rows = dp("d_rows", [1, 20], F32, isOutput=True)
        d_ln = dp("d_ln", [D, CT + 4], BF16, isOutput=True)
        d_xc = dp("d_xc", [E, CT], BF16, isOutput=True)
        d_xmid = dp("d_xmid", [D, CT], F32, isOutput=True)
        d_xcT = dp("d_xcT", [128, E], BF16, isOutput=True)
        d_etil = dp("d_etil", [128, NS], F32, isOutput=True)
        d_btil = dp("d_btil", [128, NS], BF16, isOutput=True)
        d_dec16 = dp("d_dec16", [NS, 2], F32, isOutput=True)
        d_h0 = dp("d_h0", [NS, E], F32, isOutput=True)
        d_b16 = dp("d_b16", [NS, 512], F32, isOutput=True)
        d_bdt = dp("d_bdt", [128, NS], F32, isOutput=True)

    ln1_sp = nc.dram_tensor("ln1_sp", [D, CT + 4], BF16)
    xconv_sp = nc.dram_tensor("xconv_sp", [E, CT], BF16)
    xmid_sp = nc.dram_tensor("xmid_sp", [D, CT], F32)
    xdbl_sp = nc.dram_tensor("xdbl_sp", [NSS, CT], F32)
    cc_in = nc.dram_tensor("cc_in", [NS + 1, E], F32)
    cc_out = nc.dram_tensor("cc_out", [n_cores * (NS + 1), E], F32)

    es = contextlib.ExitStack()

    def SB(name, shape, dt):
        return es.enter_context(nc.sbuf_tensor(name, shape, dt))

    def PT(name, shape, dt):
        return nc.psum_tensor(name, shape, dt, allow_name_mangling=True)

    def SBM(name, shape, dt):
        return nc.sbuf_tensor(name, shape, dt, allow_name_mangling=True)

    s_onescol_bf = SB("s_onescol_bf", [128, 1], BF16)
    s_onescol_f32 = SB("s_onescol_f32", [128, 1], F32)
    s_onesrow_bf = SB("s_onesrow_bf", [1, 128], BF16)
    s_mur = SB("s_mur", [128, ND], F32)
    s_muk = SB("s_muk", [128, ND], F32)
    s_imur = SB("s_imur", [128, ND], F32)
    s_imuk = SB("s_imuk", [128, ND], F32)
    s_ln2H = SB("s_ln2H", [128, ND], F32)
    s_rs = SB("s_rs", [1, SWW], F32)
    s_negm = SB("s_negm", [1, SWW], F32)
    s_scr = SB("s_scr", [1, SWW], F32)
    scan_es = es.enter_context(contextlib.ExitStack())

    def SC(name, shape, dt):
        return scan_es.enter_context(nc.sbuf_tensor(name, shape, dt))

    s_id_bf = SC("s_id_bf", [128, 128], BF16)
    s_id_f32 = SC("s_id_f32", [128, 128], F32)
    s_tri = SC("s_tri", [128, 128], F32)
    s_mask = SC("s_mask", [128, 128], F32)
    s_ones = SC("s_ones", [1, 128], F32)
    s_allones = SC("s_allones", [128, 128], F32)
    s_h1b = SC("s_h1b", [128, 3], F32)
    s_arow = SC("s_arow", [1, NS], F32)
    s_nAb = SC("s_nAb", [128, NS], F32)
    s_Ab = SC("s_Ab", [128, NS], F32)
    s_acol = SC("s_acol", [NS, 1], F32)
    s_dcol = SC("s_dcol", [128, NE], F32)
    s_cbcol = SC("s_cbcol", [128, NE], F32)
    s_hcur = SC("s_hcur", [NS, E], F32)
    s_hcur_bf = SC("s_hcur_bf", [NS, E], BF16)
    s_xdt = SC("s_xdt", [NSS, TB], F32)
    s_drow = SC("s_drow", [1, TB], F32)
    s_sp1 = SC("s_sp1", [1, TB], F32)
    s_sp2 = SC("s_sp2", [1, TB], F32)
    s_xdblH = SC("s_xdblH", [NSS, 1], F32)
    s_xcH_bf = SC("s_xcH_bf", [128, NE], BF16)
    s_ln1H_bf = SC("s_ln1H_bf", [128, ND * 4], BF16)
    s_zH = SC("s_zH", [128, NE], F32)
    s_ygt = SC("s_ygt", [128, NE], F32)
    s_c16h = SC("s_c16h", [NS, 1], F32)
    s_c16h_bf = SC("s_c16h_bf", [NS, 1], BF16)
    s_ygH_bf = SC("s_ygH_bf", [128, NE], BF16)
    s_xmidH = SC("s_xmidH", [128, ND], F32)
    s_xmidH_bf = SC("s_xmidH_bf", [128, ND], BF16)
    s_xmidH2 = SC("s_xmidH2", [128, ND], F32)
    s_h1r = SC("s_h1r", [1, 3], F32)
    s_hscale = SC("s_hscale", [1, 1], F32)
    s_dT4 = SC("s_dT4", [128, NCH], F32)
    s_S4 = SC("s_S4", [128, NCH], F32)
    s_Stb4 = SC("s_Stb4", [128, NCH], F32)
    s_gap4 = SC("s_gap4", [128, NCH], F32)
    s_TAin = SC("s_TAin", [128, 2 * NCH], F32)
    s_TBin = SC("s_TBin", [128, 2 * NCH], F32)
    s_lhs4 = SC("s_lhs4", [2, 512], F32)
    s_rhs4 = SC("s_rhs4", [2, 512], F32)
    s_Etil4 = SC("s_Etil4", [128, NS * NCH], F32)
    s_dec4 = SC("s_dec4", [NS, NCH], F32)
    s_BdT4 = SC("s_BdT4", [128, NS * NCH], F32)
    s_Btil4_bf = SC("s_Btil4_bf", [128, NS * NCH], BF16)
    s_Ctil4_bf = SC("s_Ctil4_bf", [NS, 512], BF16)
    s_B16 = SC("s_B16", [NS, TB], F32)
    s_C16 = SC("s_C16", [NS, TB], F32)
    s_C16_bf = SC("s_C16_bf", [NS, TB], BF16)
    s_Bd16_bf = SC("s_Bd16_bf", [NS, TB], BF16)
    s_dSTm = SC("s_dSTm", [128, 128], F32)
    s_K_bf = SC("s_K_bf", [128, 256], BF16)
    s_xcT = SC("s_xcT", [128, 2 * E], BF16)
    coll_es = scan_es.enter_context(contextlib.ExitStack())

    def CC(name, shape, dt):
        return coll_es.enter_context(nc.sbuf_tensor(name, shape, dt))

    s_hloc = CC("s_hloc", [NS, E], F32)
    s_W16 = CC("s_W16", [NS, n_cores], F32)
    s_maskrow = CC("s_maskrow", [1, n_cores], F32)
    s_predT = CC("s_predT", [n_cores, n_cores], F32)
    s_accS = CC("s_accS", [1, 1], F32)
    s_S8 = CC("s_S8", [1, n_cores], F32)
    s_c8 = CC("s_c8", [n_cores, 1], F32)
    s_gs = CC("s_gs", [n_cores, 1], F32)
    s_gr = CC("s_gr", [1, n_cores], F32)

    LN = {}

    def alloc_ln_rows(scope_es):
        LN["m"] = scope_es.enter_context(SBM("s_m", [1, SWW], F32))
        LN["v"] = scope_es.enter_context(SBM("s_v", [1, SWW], F32))
        LN["negm_bf"] = scope_es.enter_context(SBM("s_negm_bf", [1, SWW], BF16))
        LN["rs_bf"] = scope_es.enter_context(SBM("s_rs_bf", [1, SWW], BF16))
        LN["mb"] = scope_es.enter_context(SBM("s_mb", [128, SWW], F32))
        LN["rsb"] = scope_es.enter_context(SBM("s_rsb", [128, SWW], F32))

    qs = Ctr(es.enter_context(nc.semaphore("qs")))
    qg = Ctr(es.enter_context(nc.semaphore("qg")))
    st = Ctr(es.enter_context(nc.semaphore("st")))
    sv = Ctr(es.enter_context(nc.semaphore("sv")))
    ss = Ctr(es.enter_context(nc.semaphore("ss")))
    sg = Ctr(es.enter_context(nc.semaphore("sg")))
    sc = Ctr(es.enter_context(nc.semaphore("sc")))
    ENG = ("tensor", "vector", "scalar", "gpsimd", "sync")
    bars = {e: Ctr(es.enter_context(nc.semaphore("bar_" + e))) for e in ENG}
    block = es.enter_context(nc.Block())
    bar_round = [0]

    _OPS = {"matmul", "transpose", "tensor_copy", "tensor_tensor",
            "tensor_add", "tensor_sub", "tensor_mul", "tensor_max",
            "tensor_scalar", "tensor_scalar_mul", "tensor_scalar_add",
            "tensor_scalar_max", "tensor_scalar_min", "tensor_single_scalar",
            "scalar_tensor_tensor", "tensor_reduce", "reduce_sum",
            "reduce_max", "activation", "copy", "add", "mul", "memset",
            "memzero", "reciprocal", "tensor_relu", "tensor_tensor_scan"}

    class EProxy:
        def __init__(self, e):
            self._e = e
            self._last = None

        def __getattr__(self, name):
            a = getattr(self._e, name)
            if name in _OPS and callable(a):
                def wrap(*args, **kw):
                    r = a(*args, **kw)
                    self._last = r
                    return r
                return wrap
            return a

        def drain(self, *a, **kw):
            self._last = None
            return self._e.drain(*a, **kw)

        def dma_start(self, *a, **kw):
            return self._e.dma_start(*a, **kw)

        def maybe_drain_then_inc(self, update, **kw):
            if self._last is not None:
                r = self._last.then_inc(*update)
                self._last = None
                return r
            return self._e.maybe_drain_then_inc(update, **kw)

    def sec(en, f):
        getattr(block, en)(lambda eng: f(EProxy(eng)))

    def barrier():
        bar_round[0] += 1
        r = bar_round[0]

        def mk(en):
            def f(eng):
                if en == "sync":
                    eng.wait_ge(qs.sem, qs.v)
                if en == "gpsimd":
                    eng.wait_ge(qg.sem, qg.v)
                    eng.wait_ge(sc.sem, sc.v)
                eng.maybe_drain_then_inc((bars[en].sem, 1))
                for o in ENG:
                    if o != en:
                        eng.wait_ge(bars[o].sem, r)
            return f
        for en in ENG:
            sec(en, mk(en))

    def chunks_of(width, step=512):
        out = []
        o = 0
        while o < width:
            out.append((o, min(step, width - o)))
            o += step
        return out

    def mm_rounds(n_out, n_banks, emit_mms, cons, emit_evac, pre_wait=(),
                  cons_pre_wait=()):
        rounds = [list(range(r, min(r + n_banks, n_out)))
                  for r in range(0, n_out, n_banks)]
        st_base = st.v
        cctr = {"vector": sv, "scalar": ss}[cons]
        c_base = cctr.v

        def f_tens(eng):
            for c, v in pre_wait:
                eng.wait_ge(c.sem, v)
            for ri, ots in enumerate(rounds):
                if ri > 0:
                    eng.wait_ge(cctr.sem, c_base + ri)
                for i, ot in enumerate(ots):
                    emit_mms(eng, ot, i)
                eng.maybe_drain_then_inc((st.sem, 1))
        sec("tensor", f_tens)
        st.v = st_base + len(rounds)

        def f_cons(eng):
            for c, v in cons_pre_wait:
                eng.wait_ge(c.sem, v)
            for ri, ots in enumerate(rounds):
                eng.wait_ge(st.sem, st_base + ri + 1)
                for i, ot in enumerate(ots):
                    emit_evac(eng, ot, i)
                eng.maybe_drain_then_inc((cctr.sem, 1))
        sec(cons, f_cons)
        cctr.v = c_base + len(rounds)

    def emit_init():
        def f_sync(eng):
            for dst, src in ((s_id_bf, c_id_bf), (s_id_f32, c_id_f32),
                             (s_tri, c_tri), (s_mask, c_mask), (s_ones, c_ones),
                             (s_allones, c_allones),
                             (s_arow, alog_row), (s_dcol, d_colp),
                             (s_cbcol, cb_col), (s_mur, mur_col),
                             (s_muk, muk_col), (s_maskrow, c_maskrow),
                             (s_predT, c_predT), (s_hscale, c_hscale)):
                eng.dma_start(out=dst[:, :], in_=src[:, :]).then_inc(qs.sem, 16)
                qs.v += 16
        sec("sync", f_sync)
        barrier()
        with PT("pi_a", [128, NS], F32) as p_ab, \
             PT("pi_ac", [NS, 1], F32) as p_ac:
            def f_scal(eng):
                eng.activation(out=s_arow[:, :], in_=s_arow[:, :], func=AF.Exp)
                eng.maybe_drain_then_inc((ss.sem, 1))
            sec("scalar", f_scal)
            ss.v += 1

            def f_vec(eng):
                eng.memset(s_onescol_bf[:, :], 1.0)
                eng.memset(s_onescol_f32[:, :], 1.0)
                eng.memset(s_onesrow_bf[:, :], 1.0)
                eng.memset(s_TAin.rearrange("p (c two) -> p c two",
                                             two=2)[:, :, 0:1], 1.0)
                eng.memset(s_TBin.rearrange("p (c two) -> p c two",
                                            two=2)[:, :, 1:2], -1.0)
                eng.memset(s_accS[:, :], 0.0)
                eng.memset(s_hloc[:, :], 0.0)
                eng.memset(s_hcur[:, :], 0.0)
                eng.tensor_scalar(out=s_imur[:, :], in0=s_mur[:, :], scalar1=-1.0,
                                  scalar2=1.0, op0=ALU.mult, op1=ALU.add)
                eng.tensor_scalar(out=s_imuk[:, :], in0=s_muk[:, :], scalar1=-1.0,
                                  scalar2=1.0, op0=ALU.mult, op1=ALU.add)
                eng.maybe_drain_then_inc((sv.sem, 1))
            sec("vector", f_vec)
            sv.v += 1

            def f_tens(eng):
                eng.wait_ge(ss.sem, ss.v)
                eng.matmul(out=p_ab[:, :], lhsT=s_ones[0:1, :], rhs=s_arow[:, :],
                           start=True, stop=True)
                eng.transpose(out=p_ac[:, :], in_=s_arow[:, :],
                              identity=s_id_f32[0:1, 0:1])
                eng.maybe_drain_then_inc((st.sem, 1))
            sec("tensor", f_tens)
            st.v += 1

            def f_vec2(eng):
                eng.wait_ge(st.sem, st.v)
                eng.tensor_copy(out=s_nAb[:, :], in_=p_ab[:, :])
                eng.tensor_scalar_mul(out=s_Ab[:, :], in0=p_ab[:, :], scalar1=-1.0)
                eng.tensor_scalar_mul(out=s_acol[:, :], in0=p_ac[:, :], scalar1=-1.0)
                eng.tensor_scalar_mul(out=s_arow[:, :], in0=s_arow[:, :],
                                      scalar1=-1.0)
                eng.maybe_drain_then_inc((sv.sem, 1))
            sec("vector", f_vec2)
            sv.v += 1
            barrier()

    def load_weight_bf(dst_bf, src_dram, rows, cols, stage, stage_cols=2048):
        nt = rows // 128
        total = nt * cols
        step = stage_cols - (stage_cols % cols) if stage_cols >= cols else cols
        o = 0
        while o < total:
            w = min(step, total - o)

            def f_sync(eng, o=o, w=w):
                t0 = o // cols
                ntt = max(1, w // cols)
                eng.dma_start(
                    out=stage[:, 0:w],
                    in_=bass.AP(src_dram, t0 * 128 * cols,
                                [[cols, 128], [128 * cols, ntt], [1, cols]]),
                ).then_inc(qs.sem, 16)
                qs.v += 16
                eng.wait_ge(qs.sem, qs.v)
            sec("sync", f_sync)
            barrier()

            def f_vec(eng, o=o, w=w):
                eng.tensor_copy(out=dst_bf[:, o:o + w], in_=stage[:, 0:w])
                eng.maybe_drain_then_inc((sv.sem, 1))
            sec("vector", f_vec)
            sv.v += 1
            o += w
        barrier()

    def ln_stats(x_tile, xbf_tile, x2bf_tile, w):
        nch = chunks_of(w)
        with PT("pl_s1a", [1, 512], F32) as p_s1a, \
             PT("pl_s2a", [1, 512], F32) as p_s2a:
            def f_scal(eng):
                for a in range(ND):
                    eng.activation(out=xbf_tile[:, a * w:(a + 1) * w],
                                   in_=x_tile[:, a * w:(a + 1) * w], func=AF.Copy)
                eng.maybe_drain_then_inc((ss.sem, 1))
            sec("scalar", f_scal)
            ss.v += 1

            def f_vsq(eng):
                for a in range(ND):
                    eng.tensor_mul(out=x2bf_tile[:, a * w:(a + 1) * w],
                                   in0=x_tile[:, a * w:(a + 1) * w],
                                   in1=x_tile[:, a * w:(a + 1) * w])
                eng.maybe_drain_then_inc((sv.sem, 1))
            sec("vector", f_vsq)
            sv.v += 1

            def f_tens(eng):
                eng.wait_ge(ss.sem, ss.v)
                eng.wait_ge(sv.sem, sv.v)
                for ci, (off, cw) in enumerate(nch):
                    po = 0 if ci == 0 else 448
                    for a in range(ND):
                        eng.matmul(out=p_s1a[0:1, po:po + cw],
                                   lhsT=s_onescol_bf[:, :],
                                   rhs=xbf_tile[:, a * w + off:a * w + off + cw],
                                   start=(a == 0), stop=(a == ND - 1))
                    for a in range(ND):
                        eng.matmul(out=p_s2a[0:1, po:po + cw],
                                   lhsT=s_onescol_bf[:, :],
                                   rhs=x2bf_tile[:, a * w + off:a * w + off + cw],
                                   start=(a == 0), stop=(a == ND - 1))
                eng.maybe_drain_then_inc((st.sem, 1))
            sec("tensor", f_tens)
            st.v += 1

            def f_vec(eng):
                eng.wait_ge(st.sem, st.v)
                for ci, (off, cw) in enumerate(nch):
                    po = 0 if ci == 0 else 448
                    eng.tensor_copy(out=LN["m"][0:1, off:off + cw],
                                    in_=p_s1a[0:1, po:po + cw])
                    eng.tensor_copy(out=LN["v"][0:1, off:off + cw],
                                    in_=p_s2a[0:1, po:po + cw])
                eng.drain()
                eng.tensor_scalar_mul(out=LN["m"][0:1, 0:w], in0=LN["m"][0:1, 0:w],
                                      scalar1=1.0 / D)
                eng.drain()
                eng.tensor_mul(out=s_scr[0:1, 0:w], in0=LN["m"][0:1, 0:w],
                               in1=LN["m"][0:1, 0:w])
                eng.drain()
                eng.scalar_tensor_tensor(out=LN["v"][0:1, 0:w],
                                         in0=LN["v"][0:1, 0:w],
                                         scalar=1.0 / D, in1=s_scr[0:1, 0:w],
                                         op0=ALU.mult, op1=ALU.subtract)
                eng.drain()
                eng.tensor_scalar_add(out=LN["v"][0:1, 0:w], in0=LN["v"][0:1, 0:w],
                                      scalar1=EPS)
                eng.maybe_drain_then_inc((sv.sem, 1))
            sec("vector", f_vec)
            sv.v += 1

            def f_scal2(eng):
                eng.wait_ge(sv.sem, sv.v)
                eng.activation(out=s_rs[0:1, 0:w], in_=LN["v"][0:1, 0:w], func=AF.Ln)
                eng.drain()
                eng.activation(out=s_rs[0:1, 0:w], in_=s_rs[0:1, 0:w],
                               func=AF.Exp, scale=-0.5)
                eng.maybe_drain_then_inc((ss.sem, 1))
            sec("scalar", f_scal2)
            ss.v += 1

            def f_vec2(eng):
                eng.wait_ge(ss.sem, ss.v)
                eng.tensor_scalar_mul(out=s_negm[0:1, 0:w], in0=LN["m"][0:1, 0:w],
                                      scalar1=-1.0)
                eng.drain()
                eng.tensor_copy(out=LN["negm_bf"][0:1, 0:w], in_=s_negm[0:1, 0:w])
                eng.tensor_copy(out=LN["rs_bf"][0:1, 0:w], in_=s_rs[0:1, 0:w])
                eng.maybe_drain_then_inc((sv.sem, 1))
            sec("vector", f_vec2)
            sv.v += 1

        with PT("pl_b1", [128, 512], F32) as p_b1, \
             PT("pl_b2", [128, 512], F32) as p_b2:
            def f_tens2(eng):
                eng.wait_ge(sv.sem, sv.v)
                for ci, (off, cw) in enumerate(nch):
                    po = 0 if ci == 0 else 448
                    eng.matmul(out=p_b1[:, po:po + cw],
                               lhsT=s_onesrow_bf[:, :],
                               rhs=LN["negm_bf"][0:1, off:off + cw],
                               start=True, stop=True)
                    eng.matmul(out=p_b2[:, po:po + cw],
                               lhsT=s_onesrow_bf[:, :],
                               rhs=LN["rs_bf"][0:1, off:off + cw],
                               start=True, stop=True)
                eng.maybe_drain_then_inc((st.sem, 1))
            sec("tensor", f_tens2)
            st.v += 1

            def f_vec3(eng):
                eng.wait_ge(st.sem, st.v)
                for ci, (off, cw) in enumerate(nch):
                    po = 0 if ci == 0 else 448
                    eng.tensor_copy(out=LN["mb"][:, off:off + cw],
                                    in_=p_b1[:, po:po + cw])
                    eng.tensor_copy(out=LN["rsb"][:, off:off + cw],
                                    in_=p_b2[:, po:po + cw])
                eng.maybe_drain_then_inc((sv.sem, 1))
            sec("vector", f_vec3)
            sv.v += 1

    def emit_ln1(x_tile, xbf_tile, x2bf_tile, ln_bf, w):
        ln_stats(x_tile, xbf_tile, x2bf_tile, w)

        def f_gp(eng):
            eng.wait_ge(sv.sem, sv.v)
            for a in range(ND):
                eng.tensor_add(out=x_tile[:, a * w:(a + 1) * w],
                               in0=x_tile[:, a * w:(a + 1) * w],
                               in1=LN["mb"][:, 0:w])
            eng.maybe_drain_then_inc((sg.sem, 1))
        sec("gpsimd", f_gp)
        sg.v += 1

        def f_vec4(eng):
            eng.wait_ge(sg.sem, sg.v)
            for a in range(ND):
                eng.tensor_mul(out=ln_bf[:, a * w:(a + 1) * w],
                               in0=x_tile[:, a * w:(a + 1) * w],
                               in1=LN["rsb"][:, 0:w])
            eng.maybe_drain_then_inc((sv.sem, 1))
        sec("vector", f_vec4)
        sv.v += 1
        barrier()

    def emit_ln2(x_tile, xbf_tile, x2bf_tile, ln_f32, tb1):
        w = TB
        ln_stats(x_tile, xbf_tile, x2bf_tile, w)

        def f_gp(eng):
            eng.wait_ge(sv.sem, sv.v)
            for a in range(ND):
                eng.tensor_add(out=ln_f32[:, a * tb1 + 1:a * tb1 + 1 + w],
                               in0=x_tile[:, a * w:(a + 1) * w],
                               in1=LN["mb"][:, 0:w])
            eng.maybe_drain_then_inc((sg.sem, 1))
        sec("gpsimd", f_gp)
        sg.v += 1

        def f_vec4(eng):
            eng.wait_ge(sg.sem, sg.v)
            for a in range(ND):
                eng.tensor_mul(out=ln_f32[:, a * tb1 + 1:a * tb1 + 1 + w],
                               in0=ln_f32[:, a * tb1 + 1:a * tb1 + 1 + w],
                               in1=LN["rsb"][:, 0:w])
            eng.maybe_drain_then_inc((sv.sem, 1))
        sec("vector", f_vec4)
        sv.v += 1
        barrier()

    def emit_softplus_row(dst_ap, src_ap, w, pre=()):
        def f_scal(eng):
            for c, v in pre:
                eng.wait_ge(c.sem, v)
            eng.activation(out=s_sp1[0:1, 0:w], in_=src_ap, func=AF.Abs)
            eng.drain()
            eng.activation(out=s_sp1[0:1, 0:w], in_=s_sp1[0:1, 0:w],
                           func=AF.Exp, scale=-1.0)
            eng.maybe_drain_then_inc((ss.sem, 1))
        sec("scalar", f_scal)
        ss.v += 1

        def f_vec(eng):
            eng.wait_ge(ss.sem, ss.v)
            for c, v in pre:
                eng.wait_ge(c.sem, v)
            eng.tensor_scalar_add(out=s_sp1[0:1, 0:w], in0=s_sp1[0:1, 0:w],
                                  scalar1=1.0)
            eng.tensor_scalar_max(out=s_sp2[0:1, 0:w], in0=src_ap, scalar1=0.0)
            eng.maybe_drain_then_inc((sv.sem, 1))
        sec("vector", f_vec)
        sv.v += 1

        def f_scal2(eng):
            eng.wait_ge(sv.sem, sv.v)
            eng.activation(out=s_sp1[0:1, 0:w], in_=s_sp1[0:1, 0:w],
                           func=AF.Ln)
            eng.maybe_drain_then_inc((ss.sem, 1))
        sec("scalar", f_scal2)
        ss.v += 1

        def f_vec2(eng):
            eng.wait_ge(ss.sem, ss.v)
            eng.tensor_add(out=dst_ap, in0=s_sp2[0:1, 0:w],
                           in1=s_sp1[0:1, 0:w])
            eng.maybe_drain_then_inc((sv.sem, 1))
        sec("vector", f_vec2)
        sv.v += 1

    def emit_Sbatch(with_ct, accumulate_S):
        with PT("p_dT4", [128, NCH], F32) as p_dT4:
            def f_t1(eng):
                eng.wait_ge(sv.sem, sv.v)
                for c in range(NCH):
                    eng.transpose(out=p_dT4[:, c:c + 1],
                                  in_=s_scr[0:1, c * 128:(c + 1) * 128],
                                  identity=s_id_f32[0:1, 0:1])
                eng.maybe_drain_then_inc((st.sem, 1))
            sec("tensor", f_t1)
            st.v += 1

            def f_v1(eng):
                eng.wait_ge(st.sem, st.v)
                eng.tensor_copy(out=s_dT4[:, :], in_=p_dT4[:, :])
                eng.maybe_drain_then_inc((sv.sem, 1))
            sec("vector", f_v1)
            sv.v += 1

        with PT("p_S4", [128, NCH], F32) as p_S4, \
             PT("p_St4", [128, NCH], F32) as p_St4:
            def f_t2(eng):
                eng.wait_ge(sv.sem, sv.v)
                eng.matmul(out=p_S4[:, :], lhsT=s_tri[:, :], rhs=s_dT4[:, :],
                           start=True, stop=True)
                eng.matmul(out=p_St4[:, :], lhsT=s_allones[:, :],
                           rhs=s_dT4[:, :], start=True, stop=True)
                eng.maybe_drain_then_inc((st.sem, 1))
            sec("tensor", f_t2)
            st.v += 1

            def f_v2(eng):
                eng.wait_ge(st.sem, st.v)
                eng.tensor_copy(out=s_S4[:, :], in_=p_S4[:, :])
                eng.tensor_copy(out=s_Stb4[:, :], in_=p_St4[:, :])
                eng.drain()
                eng.tensor_sub(out=s_gap4[:, :], in0=s_S4[:, :],
                               in1=s_Stb4[:, :])
                eng.tensor_copy(out=s_TAin.rearrange(
                    "p (c two) -> p c two", two=2)[:, :, 1:2],
                    in_=s_S4.rearrange("p (c one) -> p c one", one=1))
                eng.tensor_copy(out=s_TBin.rearrange(
                    "p (c two) -> p c two", two=2)[:, :, 0:1],
                    in_=s_S4.rearrange("p (c one) -> p c one", one=1))
                if accumulate_S:
                    eng.drain()
                    eng.tensor_reduce(out=s_scr[0:1, 0:1],
                                      in_=s_Stb4[0:1, :], axis=AX.X,
                                      op=ALU.add)
                    eng.drain()
                    eng.tensor_add(out=s_accS[:, :], in0=s_accS[:, :],
                                   in1=s_scr[0:1, 0:1])
                eng.maybe_drain_then_inc((sv.sem, 1))
            sec("vector", f_v2)
            sv.v += 1

        with PT("p_LHS", [2, 512], F32) as p_LHS, \
             PT("p_RHS", [2, 512], F32) as p_RHS, \
             PT("p_t16", [NS, NCH], F32) as p_t16:
            def f_t3(eng):
                eng.wait_ge(sv.sem, sv.v)
                for c in range(NCH):
                    eng.transpose(out=p_LHS[:, c * 128:(c + 1) * 128],
                                  in_=s_TAin[:, 2 * c:2 * c + 2],
                                  identity=s_id_f32[:, :])
                    eng.transpose(out=p_RHS[:, c * 128:(c + 1) * 128],
                                  in_=s_TBin[:, 2 * c:2 * c + 2],
                                  identity=s_id_f32[:, :])
                eng.matmul(out=p_t16[:, :], lhsT=s_arow[:, :],
                           rhs=s_Stb4[0:1, :], start=True, stop=True)
                eng.maybe_drain_then_inc((st.sem, 1))
            sec("tensor", f_t3)
            st.v += 1

            def f_v3(eng):
                eng.wait_ge(st.sem, st.v)
                eng.tensor_copy(out=s_lhs4[:, 0:NCH * 128], in_=p_LHS[:, 0:NCH * 128])
                eng.tensor_copy(out=s_rhs4[:, 0:NCH * 128], in_=p_RHS[:, 0:NCH * 128])
                eng.maybe_drain_then_inc((sv.sem, 1))
            sec("vector", f_v3)
            sv.v += 1

            def f_s1(eng):
                eng.wait_ge(sv.sem, sv.v)
                for c in range(NCH):
                    eng.activation(out=s_Etil4[:, c * NS:(c + 1) * NS],
                                   in_=s_nAb[:, :], func=AF.Exp,
                                   scale=s_gap4[:, c:c + 1])
                eng.activation(out=s_dec4[:, :], in_=p_t16[:, :], func=AF.Exp)
                eng.maybe_drain_then_inc((ss.sem, 1))
            sec("scalar", f_s1)
            ss.v += 1

        with PT("p_BdT4", [128, NS * NCH], BF16) as p_BdT4:
            def f_t4(eng):
                eng.wait_ge(sv.sem, sv.v)
                for c in range(NCH):
                    eng.transpose(out=p_BdT4[:, c * NS:(c + 1) * NS],
                                  in_=s_Bd16_bf[:, c * 128:(c + 1) * 128],
                                  identity=s_id_bf[0:NS, 0:NS])
                eng.maybe_drain_then_inc((st.sem, 1))
            sec("tensor", f_t4)
            st.v += 1

            def f_v4(eng):
                eng.wait_ge(st.sem, st.v)
                eng.wait_ge(ss.sem, ss.v)
                eng.tensor_copy(out=s_BdT4[:, :], in_=p_BdT4[:, :])
                eng.drain()
                eng.tensor_mul(out=s_Btil4_bf[:, :], in0=s_Etil4[:, :],
                               in1=s_BdT4[:, :])
                eng.maybe_drain_then_inc((sv.sem, 1))
            sec("vector", f_v4)
            sv.v += 1

        if with_ct:
            with PT("p_sb16", [NS, 512], F32) as p_sb16:
                def f_t5(eng):
                    eng.wait_ge(sv.sem, sv.v)
                    eng.matmul(out=p_sb16[:, 0:NCH * 128],
                               lhsT=s_ones[0:1, 0:NS],
                               rhs=s_rhs4[0:1, 0:NCH * 128],
                               start=True, stop=True)
                    eng.maybe_drain_then_inc((st.sem, 1))
                sec("tensor", f_t5)
                st.v += 1

                def f_s2(eng):
                    eng.wait_ge(st.sem, st.v)
                    eng.activation(out=s_Ctil4_bf[:, 0:NCH * 128],
                                   in_=p_sb16[:, 0:NCH * 128], func=AF.Exp,
                                   scale=s_acol[:, 0:1])
                    eng.maybe_drain_then_inc((ss.sem, 1))
                sec("scalar", f_s2)
                ss.v += 1

                def f_v5(eng):
                    eng.wait_ge(ss.sem, ss.v)
                    eng.tensor_mul(out=s_Ctil4_bf[:, 0:NCH * 128],
                                   in0=s_C16_bf[:, 0:NCH * 128],
                                   in1=s_Ctil4_bf[:, 0:NCH * 128])
                    eng.maybe_drain_then_inc((sv.sem, 1))
                sec("vector", f_v5)
                sv.v += 1

    def emit_Bd_block(b0, w, with_C, pre_q=None, sbuf_pre=None):
        def f_gp(eng):
            if pre_q is not None:
                eng.wait_ge(qs.sem, pre_q)
            if sbuf_pre is not None:
                eng.wait_ge(sv.sem, sbuf_pre)
                eng.dma_start(out=s_B16[:, 0:w],
                              in_=s_xdt[1:NS + 1, 0:w]).then_inc(qg.sem, 16)
                qg.v += 16
            else:
                eng.dma_start(out=s_B16[:, 0:w],
                              in_=bass.AP(xdbl_sp, CT + b0, [[CT, NS], [1, w]])
                              ).then_inc(qg.sem, 16)
                qg.v += 16
            if with_C:
                eng.dma_start(out=s_C16[:, 0:w],
                              in_=bass.AP(xdbl_sp, (NS + 1) * CT + b0,
                                          [[CT, NS], [1, w]])
                              ).then_inc(qg.sem, 16)
                qg.v += 16
            if sbuf_pre is None:
                eng.dma_start(out=s_drow[0:1, 0:w],
                              in_=bass.AP(xdbl_sp, b0, [[1, 1], [1, w]])
                              ).then_inc(qg.sem, 16)
                qg.v += 16
            eng.wait_ge(qg.sem, qg.v)
            eng.maybe_drain_then_inc((sg.sem, 1))
        sec("gpsimd", f_gp)
        sg.v += 1

        sp_src = (s_xdt[0:1, 0:w] if sbuf_pre is not None
                  else s_drow[0:1, 0:w])
        emit_softplus_row(s_scr[0:1, 0:w], sp_src, w,
                          pre=((sg, sg.v),))

        with PT("p_db", [NS, 512], F32) as p_db:
            def f_tens(eng):
                eng.wait_ge(sv.sem, sv.v)
                eng.matmul(out=p_db[:, 0:w], lhsT=s_ones[0:1, 0:NS],
                           rhs=s_scr[0:1, 0:w], start=True, stop=True)
                eng.maybe_drain_then_inc((st.sem, 1))
            sec("tensor", f_tens)
            st.v += 1

            def f_vec(eng):
                eng.wait_ge(st.sem, st.v)
                eng.wait_ge(sg.sem, sg.v)
                eng.tensor_mul(out=s_Bd16_bf[:, 0:w], in0=s_B16[:, 0:w],
                               in1=p_db[:, 0:w])
                if with_C:
                    eng.tensor_copy(out=s_C16_bf[:, 0:w], in_=s_C16[:, 0:w])
                eng.maybe_drain_then_inc((sv.sem, 1))
            sec("vector", f_vec)
            sv.v += 1
        barrier()

    def emit_xcT(kc, src_blk, p_tr, dst_off=0, evac="vector",
                 extra_waits=()):
        ctr = {"vector": sv, "scalar": ss}[evac]
        for half in range(2):
            def f_t(eng, half=half, kc=kc):
                if half == 0:
                    for c, v in extra_waits:
                        eng.wait_ge(c.sem, v)
                else:
                    eng.wait_ge(ctr.sem, ctr.v)
                for j in range(8):
                    et = half * 8 + j
                    eng.transpose(out=p_tr[:, j * 128:(j + 1) * 128],
                                  in_=src_blk[:, et * TB + kc * 128:
                                              et * TB + kc * 128 + 128],
                                  identity=s_id_bf[:, :])
                eng.maybe_drain_then_inc((st.sem, 1))
            sec("tensor", f_t)
            st.v += 1

            def f_v(eng, half=half):
                eng.wait_ge(st.sem, st.v)
                for j in range(8):
                    et = half * 8 + j
                    o = dst_off + et * 128
                    if evac == "vector":
                        eng.tensor_copy(out=s_xcT[:, o:o + 128],
                                        in_=p_tr[:, j * 128:(j + 1) * 128])
                    else:
                        eng.activation(out=s_xcT[:, o:o + 128],
                                       in_=p_tr[:, j * 128:(j + 1) * 128],
                                       func=AF.Copy)
                eng.maybe_drain_then_inc((ctr.sem, 1))
            sec(evac, f_v)
            ctr.v += 1

    def emit_h_update(kc, xcT_tile, h_dst, h_src, copy_bf=False):
        nslice = E // 512
        with PT("p_hd0", [NS, 512], F32) as p0, \
             PT("p_hd1", [NS, 512], F32) as p1, \
             PT("p_hd2", [NS, 512], F32) as p2, \
             PT("p_hd3", [NS, 512], F32) as p3:
            pbanks = [p0, p1, p2, p3]

            def f_tens(eng):
                eng.wait_ge(sv.sem, sv.v)
                eng.wait_ge(ss.sem, ss.v)
                eng.wait_ge(qs.sem, qs.v)
                for j in range(nslice):
                    eng.matmul(out=pbanks[j][:, :],
                               lhsT=s_Btil4_bf[:, kc * NS:(kc + 1) * NS],
                               rhs=xcT_tile[:, j * 512:(j + 1) * 512],
                               start=True, stop=True)
                eng.maybe_drain_then_inc((st.sem, 1))
            sec("tensor", f_tens)
            st.v += 1

            def f_vec(eng):
                eng.wait_ge(st.sem, st.v)
                for j in range(nslice):
                    eng.scalar_tensor_tensor(
                        out=h_dst[:, j * 512:(j + 1) * 512],
                        in0=h_src[:, j * 512:(j + 1) * 512],
                        scalar=s_dec4[:, kc:kc + 1], in1=pbanks[j][:, :],
                        op0=ALU.mult, op1=ALU.add)
                if copy_bf:
                    eng.drain()
                    eng.tensor_copy(out=s_hcur_bf[:, :], in_=h_dst[:, :])
                eng.maybe_drain_then_inc((sv.sem, 1))
            sec("vector", f_vec)
            sv.v += 1

    def emit_phase1a():
        sub = p.get("sub1a", 99)
        ph = contextlib.ExitStack()

        def P(name, shape, dt):
            return ph.enter_context(nc.sbuf_tensor(name, shape, dt))

        alloc_ln_rows(ph)
        w_inx_bf = P("w_inx_bf", [128, ND * E], BF16)
        w_cd_bf = P("w_cd_bf", [128, NE * 512], BF16)
        w_xp_bf = P("w_xp_bf", [128, NE * 33], BF16)
        with nc.sbuf_tensor("w_stage1a", [128, 2048], F32) as w_stage:
            load_weight_bf(w_inx_bf, w_inx, D, E, w_stage)
            load_weight_bf(w_cd_bf, w_cd, E, 512, w_stage)
            load_weight_bf(w_xp_bf, w_xp, E, 33, w_stage)

        x_blk = P("x_blk", [128, ND * SWW], F32)
        p_tr1a = ph.enter_context(PT("p_tr1a", [128, 1024], BF16))
        xbf_blk = P("xbf_blk", [128, ND * SWW], BF16)
        ln_blk = P("ln_blk", [128, ND * SWW], BF16)
        xi_blk = P("xi_blk", [128, NE * SWW], BF16)
        xc_blk = P("xc_blk", [128, NE * TB], BF16)
        x2bf = P("x2bf1a", [128, ND * SWW], BF16)
        s_lnscr = P("s_lnscr", [1, SWW], F32)
        s_lnrs = P("s_lnrs", [1, SWW], F32)
        p_lnA = ph.enter_context(PT("p_lnA", [128, 512], F32))
        p_lnB = ph.enter_context(PT("p_lnB", [128, 512], F32))
        x2bf_h = xbf_blk

        if sub < 1:
            ph.close()
            return
        def f_sync(eng):
            eng.dma_start(out=x_blk[:, 0:ND * 4],
                          in_=bass.AP(x_in, 0, [[CT + 4, 128],
                                                [128 * (CT + 4), ND], [1, 4]])
                          ).then_inc(qs.sem, 16)
            qs.v += 16
            eng.wait_ge(qs.sem, qs.v)
        sec("sync", f_sync)
        barrier()
        emit_ln1(x_blk, xbf_blk, x2bf, s_ln1H_bf, 4)

        if sub < 2:
            ph.close()
            return
        with PT("p_xiH", [128, 64], F32) as p_xiH:
            def mms(eng, et, i):
                for a in range(ND):
                    eng.matmul(out=p_xiH[:, i * 4:i * 4 + 4],
                               lhsT=w_inx_bf[:, a * E + et * 128:a * E + (et + 1) * 128],
                               rhs=s_ln1H_bf[:, a * 4:(a + 1) * 4],
                               start=(a == 0), stop=(a == ND - 1))

            def ev(eng, et, i):
                eng.activation(out=xi_blk[:, et * SWW:et * SWW + 4],
                               in_=p_xiH[:, i * 4:i * 4 + 4], func=AF.Copy)
            mm_rounds(NE, 4, mms, "scalar", ev)
        barrier()
        with PT("p_xcH", [128, 64], F32) as p_xcH:
            def mms(eng, et, i):
                for k in range(4):
                    eng.matmul(out=p_xcH[:, i:i + 1],
                               lhsT=w_cd_bf[:, et * 512 + k * 128:et * 512 + (k + 1) * 128],
                               rhs=xi_blk[:, et * SWW + k:et * SWW + k + 1],
                               start=(k == 0), stop=(k == 3))

            def ev(eng, et, i):
                eng.activation(out=s_xcH_bf[:, et:et + 1], in_=p_xcH[:, i:i + 1],
                               func=AF.Silu, bias=s_cbcol[:, et:et + 1])
            mm_rounds(NE, 4, mms, "scalar", ev)
        barrier()
        with PT("p_xdH", [NSS, 1], F32) as p_xdH:
            def f_tens(eng):
                for et in range(NE):
                    eng.matmul(out=p_xdH[:, :],
                               lhsT=w_xp_bf[:, et * 33:(et + 1) * 33],
                               rhs=s_xcH_bf[:, et:et + 1],
                               start=(et == 0), stop=(et == NE - 1))
                eng.maybe_drain_then_inc((st.sem, 1))
            sec("tensor", f_tens)
            st.v += 1

            def f_vec(eng):
                eng.wait_ge(st.sem, st.v)
                eng.tensor_copy(out=s_xdblH[:, :], in_=p_xdH[:, :])
                eng.maybe_drain_then_inc((sv.sem, 1))
            sec("vector", f_vec)
            sv.v += 1
        barrier()

        if sub < 3:
            ph.close()
            return
        nchm = chunks_of(TB)
        LA = dict(ld_qs={}, vsq_sv={}, stats_st={}, vec2_sv={}, bcast_st={},
                  vec3_sv={}, cadd_sg={}, vec4_sv={}, lnsp_qs={},
                  inproj_st={}, inev_ss={}, silu_ss={}, xcsp_qs={},
                  xdsp_qs={}, bd_sg={}, sp_ss={}, sp_sv={})

        def ld(i):
            svw = LA["vec4_sv"].get(i - 1, 0)

            def f_sync(eng, c0=i * TB, svw=svw):
                eng.wait_ge(sv.sem, svw)
                eng.dma_start(out=x_blk[:, :],
                              in_=bass.AP(x_in, c0, [[CT + 4, 128],
                                                     [128 * (CT + 4), ND],
                                                     [1, SWW]])
                              ).then_inc(qs.sem, 16)
                qs.v += 16
            sec("sync", f_sync)
            LA["ld_qs"][i] = qs.v

        def vsq(i):
            qw = LA["ld_qs"][i]

            def f_vsq(eng, qw=qw):
                eng.wait_ge(qs.sem, qw)
                for a in range(ND):
                    eng.tensor_mul(out=x2bf[:, a * SWW:(a + 1) * SWW],
                                   in0=x_blk[:, a * SWW:(a + 1) * SWW],
                                   in1=x_blk[:, a * SWW:(a + 1) * SWW])
                eng.maybe_drain_then_inc((sv.sem, 1))
            sec("vector", f_vsq)
            sv.v += 1
            LA["vsq_sv"][i] = sv.v

        def ln_stats_p(i):
            qw, svw = LA["ld_qs"][i], LA["vsq_sv"][i]

            def f_tens(eng, qw=qw, svw=svw):
                eng.wait_ge(qs.sem, qw)
                eng.wait_ge(sv.sem, svw)
                for ci, (off, cw) in enumerate(chunks_of(SWW)):
                    po = 0 if ci == 0 else 448
                    for a in range(ND):
                        eng.matmul(out=p_lnA[0:1, po:po + cw],
                                   lhsT=s_onescol_f32[:, :],
                                   rhs=x_blk[:, a * SWW + off:a * SWW + off + cw],
                                   start=(a == 0), stop=(a == ND - 1))
                    for a in range(ND):
                        eng.matmul(out=p_lnB[0:1, po:po + cw],
                                   lhsT=s_onescol_bf[:, :],
                                   rhs=x2bf[:, a * SWW + off:a * SWW + off + cw],
                                   start=(a == 0), stop=(a == ND - 1))
                eng.maybe_drain_then_inc((st.sem, 1))
            sec("tensor", f_tens)
            st.v += 1
            LA["stats_st"][i] = st.v

        def ln_math(i):
            stw = LA["stats_st"][i]

            def f_vec(eng, stw=stw):
                eng.wait_ge(st.sem, stw)
                for ci, (off, cw) in enumerate(chunks_of(SWW)):
                    po = 0 if ci == 0 else 448
                    eng.tensor_copy(out=LN["m"][0:1, off:off + cw],
                                    in_=p_lnA[0:1, po:po + cw])
                    eng.tensor_copy(out=LN["v"][0:1, off:off + cw],
                                    in_=p_lnB[0:1, po:po + cw])
                eng.drain()
                eng.tensor_scalar_mul(out=LN["m"][0:1, 0:SWW],
                                      in0=LN["m"][0:1, 0:SWW], scalar1=1.0 / D)
                eng.drain()
                eng.tensor_mul(out=s_lnscr[0:1, 0:SWW], in0=LN["m"][0:1, 0:SWW],
                               in1=LN["m"][0:1, 0:SWW])
                eng.drain()
                eng.scalar_tensor_tensor(out=LN["v"][0:1, 0:SWW],
                                         in0=LN["v"][0:1, 0:SWW],
                                         scalar=1.0 / D,
                                         in1=s_lnscr[0:1, 0:SWW],
                                         op0=ALU.mult, op1=ALU.subtract)
                eng.drain()
                eng.tensor_scalar_add(out=LN["v"][0:1, 0:SWW],
                                      in0=LN["v"][0:1, 0:SWW], scalar1=EPS)
                eng.maybe_drain_then_inc((sv.sem, 1))
            sec("vector", f_vec)
            sv.v += 1
            svw = sv.v

            def f_scal2(eng, svw=svw):
                eng.wait_ge(sv.sem, svw)
                eng.activation(out=s_lnrs[0:1, 0:SWW], in_=LN["v"][0:1, 0:SWW],
                               func=AF.Ln)
                eng.drain()
                eng.activation(out=s_lnrs[0:1, 0:SWW], in_=s_lnrs[0:1, 0:SWW],
                               func=AF.Exp, scale=-0.5)
                eng.maybe_drain_then_inc((ss.sem, 1))
            sec("scalar", f_scal2)
            ss.v += 1
            ssw = ss.v

            def f_vec2(eng, ssw=ssw):
                eng.wait_ge(ss.sem, ssw)
                eng.tensor_scalar_mul(out=s_lnscr[0:1, 0:SWW],
                                      in0=LN["m"][0:1, 0:SWW], scalar1=-1.0)
                eng.drain()
                eng.tensor_copy(out=LN["negm_bf"][0:1, 0:SWW],
                                in_=s_lnscr[0:1, 0:SWW])
                eng.tensor_copy(out=LN["rs_bf"][0:1, 0:SWW],
                                in_=s_lnrs[0:1, 0:SWW])
                eng.maybe_drain_then_inc((sv.sem, 1))
            sec("vector", f_vec2)
            sv.v += 1
            LA["vec2_sv"][i] = sv.v

        def ln_bcast(i):
            svw = LA["vec2_sv"][i]

            def f_tens2(eng, svw=svw):
                eng.wait_ge(sv.sem, svw)
                for ci, (off, cw) in enumerate(chunks_of(SWW)):
                    po = 0 if ci == 0 else 448
                    eng.matmul(out=p_lnA[:, po:po + cw],
                               lhsT=s_onesrow_bf[:, :],
                               rhs=LN["negm_bf"][0:1, off:off + cw],
                               start=True, stop=True)
                    eng.matmul(out=p_lnB[:, po:po + cw],
                               lhsT=s_onesrow_bf[:, :],
                               rhs=LN["rs_bf"][0:1, off:off + cw],
                               start=True, stop=True)
                eng.maybe_drain_then_inc((st.sem, 1))
            sec("tensor", f_tens2)
            st.v += 1
            LA["bcast_st"][i] = st.v

        def ln_vec3(i):
            stw = LA["bcast_st"][i]
            sgw = LA["cadd_sg"].get(i - 1, 0)

            def f_vec3(eng, stw=stw, sgw=sgw):
                eng.wait_ge(st.sem, stw)
                eng.wait_ge(sg.sem, sgw)
                for ci, (off, cw) in enumerate(chunks_of(SWW)):
                    po = 0 if ci == 0 else 448
                    eng.tensor_copy(out=LN["mb"][:, off:off + cw],
                                    in_=p_lnA[:, po:po + cw])
                    eng.tensor_copy(out=LN["rsb"][:, off:off + cw],
                                    in_=p_lnB[:, po:po + cw])
                eng.maybe_drain_then_inc((sv.sem, 1))
            sec("vector", f_vec3)
            sv.v += 1
            LA["vec3_sv"][i] = sv.v

        def ln_cadd(i):
            svw = LA["vec3_sv"][i]

            def f_gp(eng, svw=svw):
                eng.wait_ge(sv.sem, svw)
                for a in range(ND):
                    eng.tensor_add(out=x_blk[:, a * SWW:(a + 1) * SWW],
                                   in0=x_blk[:, a * SWW:(a + 1) * SWW],
                                   in1=LN["mb"][:, 0:SWW])
                eng.maybe_drain_then_inc((sg.sem, 1))
            sec("gpsimd", f_gp)
            sg.v += 1
            LA["cadd_sg"][i] = sg.v

        def ln_vec4(i):
            sgw = LA["cadd_sg"][i]
            stw = LA["inproj_st"].get(i - 1, 0)
            qw = LA["lnsp_qs"].get(i - 1, 0)

            def f_vec4(eng, sgw=sgw, stw=stw, qw=qw):
                eng.wait_ge(sg.sem, sgw)
                eng.wait_ge(st.sem, stw)
                eng.wait_ge(qs.sem, qw)
                for a in range(ND):
                    eng.tensor_mul(out=ln_blk[:, a * SWW:(a + 1) * SWW],
                                   in0=x_blk[:, a * SWW:(a + 1) * SWW],
                                   in1=LN["rsb"][:, 0:SWW])
                eng.maybe_drain_then_inc((sv.sem, 1))
            sec("vector", f_vec4)
            sv.v += 1
            LA["vec4_sv"][i] = sv.v

        def lnspill(i):
            svw = LA["vec4_sv"][i]

            def f_sync2(eng, c0=i * TB, svw=svw):
                eng.wait_ge(sv.sem, svw)
                eng.dma_start(out=bass.AP(ln1_sp, c0, [[CT + 4, 128],
                                                       [128 * (CT + 4), ND],
                                                       [1, SWW]]),
                              in_=ln_blk[:, :]).then_inc(qs.sem, 16)
                qs.v += 16
            sec("sync", f_sync2)
            LA["lnsp_qs"][i] = qs.v

        def inproj(i):
            with PT("p_xi0", [128, 512], F32) as b0_, \
                 PT("p_xi1", [128, 512], F32) as b1_, \
                 PT("p_xi2", [128, 512], F32) as b2_, \
                 PT("p_xi3", [128, 512], F32) as b3_, \
                 PT("p_xih", [128, 64], F32) as p_xih:
                pb = [b0_, b1_, b2_, b3_]

                def mms(eng, et, i_):
                    for (off, cw) in nchm:
                        for a in range(ND):
                            eng.matmul(out=pb[i_][:, off:off + cw],
                                       lhsT=w_inx_bf[:, a * E + et * 128:a * E + (et + 1) * 128],
                                       rhs=ln_blk[:, a * SWW + 4 + off:a * SWW + 4 + off + cw],
                                       start=(a == 0), stop=(a == ND - 1))
                    for a in range(ND):
                        eng.matmul(out=p_xih[:, i_ * 4:i_ * 4 + 4],
                                   lhsT=w_inx_bf[:, a * E + et * 128:a * E + (et + 1) * 128],
                                   rhs=ln_blk[:, a * SWW:a * SWW + 4],
                                   start=(a == 0), stop=(a == ND - 1))

                def ev(eng, et, i_):
                    eng.activation(out=xi_blk[:, et * SWW:et * SWW + 4],
                                   in_=p_xih[:, i_ * 4:i_ * 4 + 4], func=AF.Copy)
                    eng.activation(out=xi_blk[:, et * SWW + 4:et * SWW + 4 + TB],
                                   in_=pb[i_][:, 0:TB], func=AF.Copy)
                mm_rounds(NE, 4, mms, "scalar", ev,
                          pre_wait=((sv, sv.v),))
            LA["inproj_st"][i] = st.v
            LA["inev_ss"][i] = ss.v

        def conv(i):
            with PT("p_xc0", [128, 512], F32) as q0_, \
                 PT("p_xc1", [128, 512], F32) as q1_, \
                 PT("p_xc2", [128, 512], F32) as q2_, \
                 PT("p_xc3", [128, 512], F32) as q3_:
                qb = [q0_, q1_, q2_, q3_]

                def mms(eng, et, i_):
                    for k in range(4):
                        eng.matmul(out=qb[i_][:, 0:TB],
                                   lhsT=w_cd_bf[:, et * 512 + k * 128:et * 512 + (k + 1) * 128],
                                   rhs=xi_blk[:, et * SWW + 1 + k:et * SWW + 1 + k + TB],
                                   start=(k == 0), stop=(k == 3))

                def ev(eng, et, i_):
                    eng.activation(out=xc_blk[:, et * TB:(et + 1) * TB],
                                   in_=qb[i_][:, 0:TB], func=AF.Silu,
                                   bias=s_cbcol[:, et:et + 1])
                cpw = ((qs, LA["xcsp_qs"].get(i - 1, 0)),)
                mm_rounds(NE, 4, mms, "scalar", ev,
                          pre_wait=((ss, LA["inev_ss"][i]),),
                          cons_pre_wait=cpw)
            LA["silu_ss"][i] = ss.v

        def xconvspill(i):
            ssw = LA["silu_ss"][i]

            def f_sync3(eng, c0=i * TB, ssw=ssw):
                eng.wait_ge(ss.sem, ssw)
                eng.dma_start(out=bass.AP(xconv_sp, c0, [[CT, 128],
                                                         [128 * CT, NE],
                                                         [1, TB]]),
                              in_=xc_blk[:, :]).then_inc(qs.sem, 16)
                qs.v += 16
            sec("sync", f_sync3)
            LA["xcsp_qs"][i] = qs.v

        def xproj(i):
            ssw = LA["silu_ss"][i]
            qw = LA["xdsp_qs"].get(i - 1, 0)
            sgw = LA["bd_sg"].get(i - 1, 0)
            spss = LA["sp_ss"].get(i - 1, 0)
            spsv = LA["sp_sv"].get(i - 1, 0)
            with PT("p_xd", [NSS, 512], F32) as p_xd:
                def f_tens(eng, ssw=ssw):
                    eng.wait_ge(ss.sem, ssw)
                    for (off, cw) in nchm:
                        for et in range(NE):
                            eng.matmul(out=p_xd[:, off:off + cw],
                                       lhsT=w_xp_bf[:, et * 33:(et + 1) * 33],
                                       rhs=xc_blk[:, et * TB + off:et * TB + off + cw],
                                       start=(et == 0), stop=(et == NE - 1))
                    eng.maybe_drain_then_inc((st.sem, 1))
                sec("tensor", f_tens)
                st.v += 1

                def f_vec(eng, qw=qw, sgw=sgw, spss=spss, spsv=spsv):
                    eng.wait_ge(st.sem, st.v)
                    eng.wait_ge(qs.sem, qw)
                    eng.wait_ge(sg.sem, sgw)
                    eng.wait_ge(ss.sem, spss)
                    eng.wait_ge(sv.sem, spsv)
                    eng.tensor_copy(out=s_xdt[:, :], in_=p_xd[:, 0:TB])
                    eng.maybe_drain_then_inc((sv.sem, 1))
                sec("vector", f_vec)
                sv.v += 1

                def f_syncx(eng, c0=i * TB):
                    eng.wait_ge(sv.sem, sv.v)
                    eng.dma_start(out=bass.AP(xdbl_sp, c0, [[CT, NSS],
                                                            [1, TB]]),
                                  in_=s_xdt[:, :]).then_inc(qs.sem, 16)
                    qs.v += 16
                sec("sync", f_syncx)
            LA["xdsp_qs"][i] = qs.v

        ld(0)
        vsq(0)
        ln_stats_p(0)
        ln_math(0)
        ln_bcast(0)
        ln_vec3(0)
        ln_cadd(0)
        ln_vec4(0)
        for i in range(NB):
            if i + 1 < NB:
                ld(i + 1)
                vsq(i + 1)
            inproj(i)
            lnspill(i)
            if i + 1 < NB:
                ln_stats_p(i + 1)
                ln_math(i + 1)
            conv(i)
            if i + 1 < NB:
                ln_bcast(i + 1)
                ln_vec3(i + 1)
                ln_cadd(i + 1)
                ln_vec4(i + 1)
            xconvspill(i)
            xproj(i)
            if sub >= 4:
                emit_Bd_block(i * TB, TB, with_C=False, sbuf_pre=sv.v)
                LA["bd_sg"][i] = sg.v
                LA["sp_ss"][i] = ss.v
                LA["sp_sv"][i] = sv.v
                emit_Sbatch(with_ct=False, accumulate_S=True)
                for kc in range(NCH):
                    ew = (((ss, ss.v), (sv, sv.v)) if kc == 0 else ())
                    emit_xcT(kc, xc_blk, p_tr1a, evac="scalar",
                             extra_waits=ew)
                    emit_h_update(kc, s_xcT, s_hloc, s_hloc)
        barrier()
        ph.close()

    def emit_collective():
        cps = contextlib.ExitStack()
        s_gj = cps.enter_context(nc.sbuf_tensor("s_gj", [NS, E], F32))

        def f_vz(eng):
            eng.memset(s_gj[0:1, :], 0.0)
            eng.maybe_drain_then_inc((sv.sem, 1))
        sec("vector", f_vz)
        sv.v += 1

        def f_gp(eng):
            eng.wait_ge(sv.sem, sv.v)
            eng.dma_start(out=cc_in[0:NS, :], in_=s_hloc[:, :]).then_inc(qg.sem, 16)
            qg.v += 16
            eng.dma_start(out=cc_in[NS:NS + 1, :],
                          in_=s_gj[0:1, :]).then_inc(qg.sem, 16)
            qg.v += 16
            eng.dma_start(out=cc_in[NS:NS + 1, 0:1],
                          in_=s_accS[:, :]).then_inc(qg.sem, 16)
            qg.v += 16
            eng.wait_ge(qg.sem, qg.v)
            eng.collective_compute(
                "AllGather", ALU.bypass,
                replica_groups=[list(range(n_cores))],
                ins=[cc_in.ap().opt()],
                outs=[cc_out.ap().opt()],
            ).then_inc(sc.sem, 1)
            sc.v += 1
            eng.wait_ge(sc.sem, sc.v)
            eng.dma_start(out=s_S8[:, :],
                          in_=bass.AP(cc_out, NS * E,
                                      [[1, 1], [(NS + 1) * E, n_cores], [1, 1]]),
                          allow_slow_non_contiguous=True
                          ).then_inc(qg.sem, 16)
            qg.v += 16
            eng.wait_ge(qg.sem, qg.v)
            eng.maybe_drain_then_inc((sg.sem, 1))
        sec("gpsimd", f_gp)
        sg.v += 1
        barrier()

        with PT("p_s8c", [n_cores, 1], F32) as p_s8c, \
             PT("p_gs", [n_cores, 1], F32) as p_gs, \
             PT("p_gr", [1, n_cores], F32) as p_gr:
            def f_tens(eng):
                eng.transpose(out=p_s8c[:, :], in_=s_S8[:, :],
                              identity=s_id_f32[0:1, 0:1])
                eng.maybe_drain_then_inc((st.sem, 1))
            sec("tensor", f_tens)
            st.v += 1

            def f_vec1(eng):
                eng.wait_ge(st.sem, st.v)
                eng.tensor_copy(out=s_c8[:, :], in_=p_s8c[:, :])
                eng.maybe_drain_then_inc((sv.sem, 1))
            sec("vector", f_vec1)
            sv.v += 1

            def f_tens2(eng):
                eng.wait_ge(sv.sem, sv.v)
                eng.matmul(out=p_gs[:, :], lhsT=s_predT[:, :], rhs=s_c8[:, :],
                           start=True, stop=True)
                eng.maybe_drain_then_inc((st.sem, 1))
            sec("tensor", f_tens2)
            st.v += 1

            def f_vec2(eng):
                eng.wait_ge(st.sem, st.v)
                eng.tensor_copy(out=s_gs[:, :], in_=p_gs[:, :])
                eng.maybe_drain_then_inc((sv.sem, 1))
            sec("vector", f_vec2)
            sv.v += 1

            def f_tens3(eng):
                eng.wait_ge(sv.sem, sv.v)
                eng.transpose(out=p_gr[:, :], in_=s_gs[:, :],
                              identity=s_id_f32[0:n_cores, 0:n_cores])
                eng.maybe_drain_then_inc((st.sem, 1))
            sec("tensor", f_tens3)
            st.v += 1

            def f_vec3(eng):
                eng.wait_ge(st.sem, st.v)
                eng.tensor_copy(out=s_gr[:, :], in_=p_gr[:, :])
                eng.memset(s_hcur[:, :], 0.0)
                eng.maybe_drain_then_inc((sv.sem, 1))
            sec("vector", f_vec3)
            sv.v += 1
        barrier()

        with PT("p_w16", [NS, n_cores], F32) as p_w16, \
             PT("p_mk16", [NS, n_cores], F32) as p_mk16:
            def f_tensw(eng):
                eng.matmul(out=p_w16[:, :], lhsT=s_arow[:, :], rhs=s_gr[:, :],
                           start=True, stop=True)
                eng.matmul(out=p_mk16[:, :], lhsT=s_ones[0:1, 0:NS],
                           rhs=s_maskrow[:, :], start=True, stop=True)
                eng.maybe_drain_then_inc((st.sem, 1))
            sec("tensor", f_tensw)
            st.v += 1

            def f_scalw(eng):
                eng.wait_ge(st.sem, st.v)
                eng.activation(out=s_W16[:, :], in_=p_w16[:, :], func=AF.Exp)
                eng.maybe_drain_then_inc((ss.sem, 1))
            sec("scalar", f_scalw)
            ss.v += 1

            def f_vecw(eng):
                eng.wait_ge(ss.sem, ss.v)
                eng.tensor_mul(out=s_W16[:, :], in0=s_W16[:, :],
                               in1=p_mk16[:, :])
                eng.maybe_drain_then_inc((sv.sem, 1))
            sec("vector", f_vecw)
            sv.v += 1
        barrier()

        for j in range(n_cores):
            def f_gpj(eng, j=j):
                eng.dma_start(out=s_gj[:, :],
                              in_=cc_out[j * (NS + 1):j * (NS + 1) + NS, :]
                              ).then_inc(qg.sem, 16)
                qg.v += 16
                eng.wait_ge(qg.sem, qg.v)
                eng.maybe_drain_then_inc((sg.sem, 1))
            sec("gpsimd", f_gpj)
            sg.v += 1

            def f_vecj2(eng, j=j):
                eng.wait_ge(sg.sem, sg.v)
                eng.scalar_tensor_tensor(out=s_hcur[:, :], in0=s_gj[:, :],
                                         scalar=s_W16[:, j:j + 1],
                                         in1=s_hcur[:, :],
                                         op0=ALU.mult, op1=ALU.add)
                eng.maybe_drain_then_inc((sv.sem, 1))
            sec("vector", f_vecj2)
            sv.v += 1
            barrier()

        def f_vec4(eng):
            eng.tensor_copy(out=s_hcur_bf[:, :], in_=s_hcur[:, :])
            eng.maybe_drain_then_inc((sv.sem, 1))
        sec("vector", f_vec4)
        sv.v += 1
        barrier()
        if DBG:
            def f_dbg(eng):
                eng.dma_start(out=d_hin[:, :], in_=s_hcur[:, :]).then_inc(qs.sem, 16)
                qs.v += 16
                eng.dma_start(out=d_hloc[:, :], in_=s_hloc[:, :]).then_inc(qs.sem, 16)
                qs.v += 16
                eng.dma_start(out=d_xdbl[:, :],
                              in_=bass.AP(xdbl_sp, 0, [[CT, NSS], [1, CT]])
                              ).then_inc(qs.sem, 16)
                qs.v += 16
                eng.dma_start(out=d_rows[0:1, 0:1], in_=s_accS[:, :]).then_inc(qs.sem, 16)
                qs.v += 16
                eng.dma_start(out=d_rows[0:1, 1:1 + n_cores], in_=s_S8[:, :]).then_inc(qs.sem, 16)
                qs.v += 16
                eng.dma_start(out=d_rows[0:1, 9:9 + n_cores], in_=s_gr[:, :]).then_inc(qs.sem, 16)
                qs.v += 16
                eng.wait_ge(qs.sem, qs.v)
            sec("sync", f_dbg)
            barrier()
        cps.close()

    def emit_phase1b():
        ph = contextlib.ExitStack()

        def P(name, shape, dt):
            return ph.enter_context(nc.sbuf_tensor(name, shape, dt))

        w_inz_bf = P("w_inz_bf", [128, ND * E], BF16)
        w_out_bf = P("w_out_bf", [128, NE * D], BF16)
        with nc.sbuf_tensor("w_stage1b", [128, 2048], F32) as w_stage:
            load_weight_bf(w_inz_bf, w_inz, D, E, w_stage)
            load_weight_bf(w_out_bf, w_out, E, D, w_stage)

        x_blk = P("x_blk1b", [128, ND * TB], F32)
        p_tr1b = ph.enter_context(PT("p_tr1b", [128, 1024], BF16))
        p_dst1b = ph.enter_context(PT("p_dst1b", [128, 128], F32))
        p_bcA = ph.enter_context(PT("p_bcA", [128, 512], F32))
        p_bcB = ph.enter_context(PT("p_bcB", [128, 512], F32))
        xc_blk = P("xc_blk1b", [128, NE * TB], BF16)
        z_bf = P("z_bf", [128, NE * TB], BF16)
        y_g = P("y_g", [128, NE * TB], BF16)
        s_E = P("s_E", [128, NS * 128], F32)
        s_bdflat = P("s_bdflat", [1, NS * 128], BF16)
        s_cflat = P("s_cflat", [1, NS * 128], BF16)

        with PT("p_zH", [128, 64], F32) as p_zH:
            def mms(eng, et, i):
                for a in range(ND):
                    eng.matmul(out=p_zH[:, i:i + 1],
                               lhsT=w_inz_bf[:, a * E + et * 128:a * E + (et + 1) * 128],
                               rhs=s_ln1H_bf[:, a * 4 + 3:a * 4 + 4],
                               start=(a == 0), stop=(a == ND - 1))

            def ev(eng, et, i):
                eng.activation(out=s_zH[:, et:et + 1], in_=p_zH[:, i:i + 1],
                               func=AF.Silu)
            mm_rounds(NE, 4, mms, "scalar", ev)
        barrier()

        def f_gp(eng):
            eng.dma_start(out=s_c16h[:, :],
                          in_=s_xdblH[NS + 1:NSS, 0:1]).then_inc(qg.sem, 16)
            qg.v += 16
            eng.wait_ge(qg.sem, qg.v)
            eng.maybe_drain_then_inc((sg.sem, 1))
        sec("gpsimd", f_gp)
        sg.v += 1

        def f_vec(eng):
            eng.wait_ge(sg.sem, sg.v)
            eng.tensor_copy(out=s_c16h_bf[:, :], in_=s_c16h[:, :])
            eng.maybe_drain_then_inc((sv.sem, 1))
        sec("vector", f_vec)
        sv.v += 1
        barrier()

        with PT("p_yH", [128, NE], F32) as p_yH:
            def f_tens(eng):
                for et in range(NE):
                    eng.matmul(out=p_yH[:, et:et + 1],
                               lhsT=s_hcur_bf[:, et * 128:(et + 1) * 128],
                               rhs=s_c16h_bf[:, :], start=True, stop=True)
                eng.maybe_drain_then_inc((st.sem, 1))
            sec("tensor", f_tens)
            st.v += 1

            def f_vec2(eng):
                eng.wait_ge(st.sem, st.v)
                eng.tensor_copy(out=s_ygt[:, :], in_=s_xcH_bf[:, :])
                eng.drain()
                eng.tensor_mul(out=s_ygt[:, :], in0=s_ygt[:, :],
                               in1=s_dcol[:, :])
                eng.drain()
                eng.tensor_add(out=s_ygt[:, :], in0=s_ygt[:, :],
                               in1=p_yH[:, :])
                eng.drain()
                eng.tensor_mul(out=s_ygH_bf[:, 0:NE], in0=s_ygt[:, :],
                               in1=s_zH[:, 0:NE])
                eng.maybe_drain_then_inc((sv.sem, 1))
            sec("vector", f_vec2)
            sv.v += 1
        barrier()

        with PT("p_moH", [128, ND], F32) as p_moH:
            def f_tens(eng):
                for a in range(ND):
                    for et in range(NE):
                        eng.matmul(out=p_moH[:, a:a + 1],
                                   lhsT=w_out_bf[:, et * D + a * 128:et * D + (a + 1) * 128],
                                   rhs=s_ygH_bf[:, et:et + 1],
                                   start=(et == 0), stop=(et == NE - 1))
                eng.maybe_drain_then_inc((st.sem, 1))
            sec("tensor", f_tens)
            st.v += 1

            def f_sync(eng):
                eng.dma_start(out=s_xmidH2[:, :],
                              in_=bass.AP(x_in, 3, [[CT + 4, 128],
                                                    [128 * (CT + 4), ND], [1, 1]]),
                              allow_slow_non_contiguous=True
                              ).then_inc(qs.sem, 16)
                qs.v += 16
                eng.wait_ge(qs.sem, qs.v)
            sec("sync", f_sync)

            def f_vec3(eng):
                eng.wait_ge(st.sem, st.v)
                eng.wait_ge(qs.sem, qs.v)
                eng.tensor_add(out=s_xmidH[:, :], in0=s_xmidH2[:, :],
                               in1=p_moH[:, :])
                eng.maybe_drain_then_inc((sv.sem, 1))
            sec("vector", f_vec3)
            sv.v += 1
        barrier()

        with PT("p_h1", [1, ND], F32) as p_h1, \
             PT("p_h2", [1, ND], F32) as p_h2, \
             PT("p_h3", [128, 3], F32) as p_h3:
            def f_scal(eng):
                eng.activation(out=s_xmidH_bf[:, :], in_=s_xmidH[:, :],
                               func=AF.Copy)
                eng.maybe_drain_then_inc((ss.sem, 1))
            sec("scalar", f_scal)
            ss.v += 1

            def f_vecb(eng):
                eng.tensor_mul(out=s_ygH_bf[:, 0:ND], in0=s_xmidH[:, :],
                               in1=s_xmidH[:, :])
                eng.maybe_drain_then_inc((sv.sem, 1))
            sec("vector", f_vecb)
            sv.v += 1

            def f_tens(eng):
                eng.wait_ge(sv.sem, sv.v)
                eng.wait_ge(ss.sem, ss.v)
                eng.matmul(out=p_h1[:, :], lhsT=s_onescol_bf[:, :],
                           rhs=s_xmidH_bf[:, :], start=True, stop=True)
                eng.matmul(out=p_h2[:, :], lhsT=s_onescol_bf[:, :],
                           rhs=s_ygH_bf[:, 0:ND], start=True, stop=True)
                eng.maybe_drain_then_inc((st.sem, 1))
            sec("tensor", f_tens)
            st.v += 1

            def f_vec2(eng):
                eng.wait_ge(st.sem, st.v)
                eng.tensor_reduce(out=s_h1r[0:1, 0:1], in_=p_h1[:, :],
                                  axis=AX.X, op=ALU.add)
                eng.tensor_reduce(out=s_h1r[0:1, 1:2], in_=p_h2[:, :],
                                  axis=AX.X, op=ALU.add)
                eng.drain()
                eng.tensor_scalar_mul(out=s_h1r[0:1, 0:2], in0=s_h1r[0:1, 0:2],
                                      scalar1=1.0 / D)
                eng.drain()
                eng.tensor_mul(out=s_scr[0:1, 0:1], in0=s_h1r[0:1, 0:1],
                               in1=s_h1r[0:1, 0:1])
                eng.drain()
                eng.tensor_sub(out=s_h1r[0:1, 1:2], in0=s_h1r[0:1, 1:2],
                               in1=s_scr[0:1, 0:1])
                eng.drain()
                eng.tensor_scalar_add(out=s_h1r[0:1, 1:2], in0=s_h1r[0:1, 1:2],
                                      scalar1=EPS)
                eng.maybe_drain_then_inc((sv.sem, 1))
            sec("vector", f_vec2)
            sv.v += 1

            def f_scal2(eng):
                eng.wait_ge(sv.sem, sv.v)
                eng.activation(out=s_h1r[0:1, 1:2], in_=s_h1r[0:1, 1:2],
                               func=AF.Ln)
                eng.drain()
                eng.activation(out=s_h1r[0:1, 1:2], in_=s_h1r[0:1, 1:2],
                               func=AF.Exp, scale=-0.5)
                eng.maybe_drain_then_inc((ss.sem, 1))
            sec("scalar", f_scal2)
            ss.v += 1

            def f_vec3(eng):
                eng.wait_ge(ss.sem, ss.v)
                eng.tensor_copy(out=s_h1r[0:1, 2:3], in_=s_hscale[:, :])
                eng.maybe_drain_then_inc((sv.sem, 1))
            sec("vector", f_vec3)
            sv.v += 1

            def f_tensb(eng):
                eng.wait_ge(sv.sem, sv.v)
                eng.matmul(out=p_h3[:, :], lhsT=s_ones[:, :],
                           rhs=s_h1r[0:1, 0:3], start=True, stop=True)
                eng.maybe_drain_then_inc((st.sem, 1))
            sec("tensor", f_tensb)
            st.v += 1

            def f_vecc3(eng):
                eng.wait_ge(st.sem, st.v)
                eng.tensor_copy(out=s_h1b[:, :], in_=p_h3[:, :])
                eng.drain()
                eng.tensor_scalar(out=s_ln2H[:, :], in0=s_xmidH[:, :],
                                  scalar1=s_h1b[:, 0:1], scalar2=s_h1b[:, 1:2],
                                  op0=ALU.subtract, op1=ALU.mult)
                eng.drain()
                eng.tensor_scalar_mul(out=s_ln2H[:, :], in0=s_ln2H[:, :],
                                      scalar1=s_h1b[:, 2:3])
                eng.maybe_drain_then_inc((sv.sem, 1))
            sec("vector", f_vecc3)
            sv.v += 1
        barrier()

        nchm = chunks_of(TB)
        for b in range(NB):
            c0 = b * TB

            def f_sync(eng, c0=c0):
                eng.dma_start(out=x_blk[:, :],
                              in_=bass.AP(x_in, c0 + 4, [[CT + 4, 128],
                                                         [128 * (CT + 4), ND], [1, TB]])
                              ).then_inc(qs.sem, 16)
                qs.v += 16
                eng.dma_start(out=xc_blk[:, :],
                              in_=bass.AP(xconv_sp, c0, [[CT, 128],
                                                         [128 * CT, NE], [1, TB]])
                              ).then_inc(qs.sem, 16)
                qs.v += 16
                eng.wait_ge(qs.sem, qs.v)
            sec("sync", f_sync)
            barrier()

            if True:
                ln_blk = y_g

                def f_syncl(eng, c0=c0):
                    eng.dma_start(out=ln_blk[:, 0:ND * SWW],
                                  in_=bass.AP(ln1_sp, c0,
                                              [[CT + 4, 128],
                                               [128 * (CT + 4), ND], [1, SWW]])
                                  ).then_inc(qs.sem, 16)
                    qs.v += 16
                    eng.wait_ge(qs.sem, qs.v)
                sec("sync", f_syncl)
                barrier()

                with PT("p_z0", [128, 512], F32) as z0, \
                     PT("p_z1", [128, 512], F32) as z1, \
                     PT("p_z2", [128, 512], F32) as z2, \
                     PT("p_z3", [128, 512], F32) as z3:
                    zb = [z0, z1, z2, z3]

                    def mms(eng, et, i):
                        for (off, cw) in nchm:
                            for a in range(ND):
                                eng.matmul(out=zb[i][:, off:off + cw],
                                           lhsT=w_inz_bf[:, a * E + et * 128:a * E + (et + 1) * 128],
                                           rhs=ln_blk[:, a * SWW + 4 + off:a * SWW + 4 + off + cw],
                                           start=(a == 0), stop=(a == ND - 1))

                    def ev(eng, et, i):
                        eng.activation(out=z_bf[:, et * TB:(et + 1) * TB],
                                       in_=zb[i][:, 0:TB], func=AF.Silu)
                    mm_rounds(NE, 4, mms, "scalar", ev)
                barrier()

            emit_Bd_block(c0, TB, with_C=True)

            emit_Sbatch(with_ct=True, accumulate_S=False)
            KS = dict(flats_sg={}, xcT_ss={}, dst_st={}, scale_sv={},
                      exp_ss={}, bcp0_st={}, mulp0_sv={}, bcp1_st={},
                      mulp1_sv={}, y_st={}, evy_ss={}, gat_sg={},
                      h_st={}, hcomb_sv={})

            def st_flats(j):
                stw = KS["bcp1_st"].get(j - 1, 0)

                def f_gp2(eng, j=j, stw=stw):
                    eng.wait_ge(st.sem, stw)
                    eng.dma_start(out=bass.AP(s_bdflat, 0,
                                              [[1, 1], [128, NS], [1, 128]]),
                                  in_=s_Bd16_bf[:, j * 128:(j + 1) * 128]
                                  ).then_inc(qg.sem, 16)
                    qg.v += 16
                    eng.dma_start(out=bass.AP(s_cflat, 0,
                                              [[1, 1], [128, NS], [1, 128]]),
                                  in_=s_C16_bf[:, j * 128:(j + 1) * 128]
                                  ).then_inc(qg.sem, 16)
                    qg.v += 16
                    eng.wait_ge(qg.sem, qg.v)
                    eng.maybe_drain_then_inc((sg.sem, 1))
                sec("gpsimd", f_gp2)
                sg.v += 1
                KS["flats_sg"][j] = sg.v

            def st_T(j):
                emit_xcT(j, xc_blk, p_tr1b, dst_off=(j % 2) * E,
                         evac="scalar")
                KS["xcT_ss"][j] = ss.v

            def st_dst(j):
                svw = KS["scale_sv"].get(j - 1, 0)

                def f_td(eng, j=j, svw=svw):
                    eng.wait_ge(sv.sem, svw)
                    eng.matmul(out=p_dst1b[:, :],
                               lhsT=s_lhs4[:, j * 128:(j + 1) * 128],
                               rhs=s_rhs4[:, j * 128:(j + 1) * 128],
                               start=True, stop=True)
                    eng.maybe_drain_then_inc((st.sem, 1))
                sec("tensor", f_td)
                st.v += 1
                KS["dst_st"][j] = st.v

            def st_scale(j):
                stw = KS["dst_st"][j]

                def f_vp(eng, stw=stw):
                    eng.wait_ge(st.sem, stw)
                    eng.tensor_add(out=s_dSTm[:, :], in0=p_dst1b[:, :],
                                   in1=s_mask[:, :])
                    eng.drain()
                    for n in range(NS):
                        eng.tensor_scalar_mul(
                            out=s_E[:, n * 128:(n + 1) * 128],
                            in0=s_dSTm[:, :], scalar1=s_Ab[:, n:n + 1])
                    eng.maybe_drain_then_inc((sv.sem, 1))
                sec("vector", f_vp)
                sv.v += 1
                KS["scale_sv"][j] = sv.v

            def st_exp(j):
                svw = KS["scale_sv"][j]

                def f_se(eng, svw=svw):
                    eng.wait_ge(sv.sem, svw)
                    eng.activation(out=s_E[:, :], in_=s_E[:, :], func=AF.Exp)
                    eng.maybe_drain_then_inc((ss.sem, 1))
                sec("scalar", f_se)
                ss.v += 1
                KS["exp_ss"][j] = ss.v

            def st_bc(j, p):
                ssw = KS["exp_ss"][j]
                sgw = KS["flats_sg"][j]
                svw = (KS["mulp1_sv"].get(j - 1, 0) if p == 0
                       else KS["mulp0_sv"][j])

                def f_tbc(eng, p=p, ssw=ssw, sgw=sgw, svw=svw):
                    eng.wait_ge(ss.sem, ssw)
                    eng.wait_ge(sg.sem, sgw)
                    eng.wait_ge(sv.sem, svw)
                    for q in range(8):
                        n = p * 8 + q
                        bank = p_bcA if q < 4 else p_bcB
                        eng.matmul(out=bank[:, (q % 4) * 128:(q % 4) * 128 + 128],
                                   lhsT=s_bdflat[0:1, n * 128:(n + 1) * 128],
                                   rhs=s_cflat[0:1, n * 128:(n + 1) * 128],
                                   start=True, stop=True)
                    eng.maybe_drain_then_inc((st.sem, 1))
                sec("tensor", f_tbc)
                st.v += 1
                KS["bcp%d_st" % p][j] = st.v

            def st_mulp0(j):
                stw = KS["bcp0_st"][j]

                def f_m0(eng, stw=stw):
                    eng.wait_ge(st.sem, stw)
                    eng.tensor_mul(out=s_E[:, 0:512], in0=s_E[:, 0:512],
                                   in1=p_bcA[:, :])
                    eng.tensor_mul(out=s_E[:, 512:1024], in0=s_E[:, 512:1024],
                                   in1=p_bcB[:, :])
                    eng.maybe_drain_then_inc((sv.sem, 1))
                sec("vector", f_m0)
                sv.v += 1
                KS["mulp0_sv"][j] = sv.v

            def st_mulp1K(j):
                stw = KS["bcp1_st"][j]

                def f_m1(eng, j=j, stw=stw):
                    eng.wait_ge(st.sem, stw)
                    eng.tensor_mul(out=s_E[:, 1024:1536], in0=s_E[:, 1024:1536],
                                   in1=p_bcA[:, :])
                    eng.tensor_mul(out=s_E[:, 1536:2048], in0=s_E[:, 1536:2048],
                                   in1=p_bcB[:, :])
                    eng.drain()
                    eng.tensor_reduce(
                        out=s_dSTm[:, :],
                        in_=s_E[:, :].rearrange("p (n t) -> p t n", t=128),
                        axis=AX.X, op=ALU.add)
                    eng.drain()
                    eng.tensor_copy(out=s_K_bf[:, (j % 2) * 128:(j % 2) * 128 + 128],
                                    in_=s_dSTm[:, :])
                    eng.maybe_drain_then_inc((sv.sem, 1))
                sec("vector", f_m1)
                sv.v += 1
                KS["mulp1_sv"][j] = sv.v

            def st_Y(j):
                svw = KS["mulp1_sv"][j]
                ssw = KS["evy_ss"].get(j - 1, KS["xcT_ss"][j])

                def f_tens4(eng, j=j, svw=svw, ssw=ssw):
                    eng.wait_ge(sv.sem, svw)
                    eng.wait_ge(ss.sem, ssw)
                    xo = (j % 2) * E
                    for et in range(NE):
                        pbk = p_Yb[et // 4]
                        sl = (et % 4) * 128
                        eng.matmul(out=pbk[:, sl:sl + 128],
                                   lhsT=s_xcT[:, xo + et * 128:xo + (et + 1) * 128],
                                   rhs=s_K_bf[:, (j % 2) * 128:(j % 2) * 128 + 128],
                                   start=True, stop=False)
                        eng.matmul(out=pbk[:, sl:sl + 128],
                                   lhsT=s_hcur_bf[:, et * 128:(et + 1) * 128],
                                   rhs=s_Ctil4_bf[:, j * 128:(j + 1) * 128],
                                   start=False, stop=True)
                    eng.maybe_drain_then_inc((st.sem, 1))
                sec("tensor", f_tens4)
                st.v += 1
                KS["y_st"][j] = st.v

            def st_evacY(j):
                stw = KS["y_st"][j]

                def f_ey(eng, j=j, stw=stw):
                    eng.wait_ge(st.sem, stw)
                    for et in range(NE):
                        pbk = p_Yb[et // 4]
                        sl = (et % 4) * 128
                        o = et * TB + j * 128
                        eng.activation(out=y_g[:, o:o + 128],
                                       in_=pbk[:, sl:sl + 128], func=AF.Copy)
                    eng.maybe_drain_then_inc((ss.sem, 1))
                sec("scalar", f_ey)
                ss.v += 1
                KS["evy_ss"][j] = ss.v

            def st_gating(j):
                ssw = KS["evy_ss"][j]

                def f_vg(eng, j=j, ssw=ssw):
                    eng.wait_ge(ss.sem, ssw)
                    for et in range(NE):
                        o = et * TB + j * 128
                        eng.scalar_tensor_tensor(
                            out=y_g[:, o:o + 128],
                            in0=xc_blk[:, o:o + 128],
                            scalar=s_dcol[:, et:et + 1],
                            in1=y_g[:, o:o + 128],
                            op0=ALU.mult, op1=ALU.add)
                    eng.drain()
                    for et in range(NE):
                        o = et * TB + j * 128
                        eng.tensor_mul(out=y_g[:, o:o + 128],
                                       in0=y_g[:, o:o + 128],
                                       in1=z_bf[:, o:o + 128])
                    eng.maybe_drain_then_inc((sv.sem, 1))
                sec("vector", f_vg)
                sv.v += 1
                KS["gat_sg"][j] = sv.v

            def st_h(j):
                ssw = KS["evy_ss"][j]

                def f_th(eng, j=j, ssw=ssw):
                    eng.wait_ge(ss.sem, ssw)
                    xo = (j % 2) * E
                    for i4 in range(4):
                        eng.matmul(out=p_Yb[i4][0:NS, :],
                                   lhsT=s_Btil4_bf[:, j * NS:(j + 1) * NS],
                                   rhs=s_xcT[:, xo + i4 * 512:xo + (i4 + 1) * 512],
                                   start=True, stop=True)
                    eng.maybe_drain_then_inc((st.sem, 1))
                sec("tensor", f_th)
                st.v += 1
                KS["h_st"][j] = st.v

            def st_hcomb(j):
                stw = KS["h_st"][j]

                def f_hc(eng, j=j, stw=stw):
                    eng.wait_ge(st.sem, stw)
                    for i4 in range(4):
                        eng.scalar_tensor_tensor(
                            out=s_hcur[:, i4 * 512:(i4 + 1) * 512],
                            in0=s_hcur[:, i4 * 512:(i4 + 1) * 512],
                            scalar=s_dec4[:, j:j + 1], in1=p_Yb[i4][0:NS, :],
                            op0=ALU.mult, op1=ALU.add)
                    eng.drain()
                    eng.tensor_copy(out=s_hcur_bf[:, :], in_=s_hcur[:, :])
                    eng.maybe_drain_then_inc((sv.sem, 1))
                sec("vector", f_hc)
                sv.v += 1
                KS["hcomb_sv"][j] = sv.v

            with PT("p_Y0", [128, 512], F32) as y0, \
                 PT("p_Y1", [128, 512], F32) as y1, \
                 PT("p_Y2", [128, 512], F32) as y2, \
                 PT("p_Y3", [128, 512], F32) as y3:
                p_Yb = [y0, y1, y2, y3]
                st_flats(0)
                st_T(0)
                st_dst(0)
                st_scale(0)
                st_exp(0)
                st_bc(0, 0)
                st_mulp0(0)
                st_bc(0, 1)
                st_mulp1K(0)
                for j in range(NCH):
                    if j + 1 < NCH:
                        st_flats(j + 1)
                        st_T(j + 1)
                        st_dst(j + 1)
                        st_scale(j + 1)
                        st_exp(j + 1)
                    st_Y(j)
                    st_evacY(j)
                    if j + 1 < NCH:
                        st_bc(j + 1, 0)
                    st_gating(j)
                    st_h(j)
                    if j + 1 < NCH:
                        st_mulp0(j + 1)
                    st_hcomb(j)
                    if j + 1 < NCH:
                        st_bc(j + 1, 1)
                        st_mulp1K(j + 1)

            with PT("p_mo0", [128, 512], F32) as m0, \
                 PT("p_mo1", [128, 512], F32) as m1, \
                 PT("p_mo2", [128, 512], F32) as m2, \
                 PT("p_mo3", [128, 512], F32) as m3:
                mbk = [m0, m1, m2, m3]

                def mms(eng, a, i):
                    for (off, cw) in nchm:
                        for et in range(NE):
                            eng.matmul(out=mbk[i][:, off:off + cw],
                                       lhsT=w_out_bf[:, et * D + a * 128:et * D + (a + 1) * 128],
                                       rhs=y_g[:, et * TB + off:et * TB + off + cw],
                                       start=(et == 0), stop=(et == NE - 1))

                def ev(eng, a, i):
                    eng.tensor_add(out=x_blk[:, a * TB:(a + 1) * TB],
                                   in0=x_blk[:, a * TB:(a + 1) * TB],
                                   in1=mbk[i][:, 0:TB])
                mm_rounds(ND, 4, mms, "vector", ev,
                          pre_wait=((sv, sv.v), (sg, sg.v)))

            def f_sync4(eng, c0=c0):
                eng.wait_ge(sv.sem, sv.v)
                eng.dma_start(out=bass.AP(xmid_sp, c0, [[CT, 128],
                                                        [128 * CT, ND], [1, TB]]),
                              in_=x_blk[:, :]).then_inc(qs.sem, 16)
                qs.v += 16
            sec("sync", f_sync4)
            barrier()
        ph.close()

    def emit_phase2():
        ph = contextlib.ExitStack()

        def P(name, shape, dt):
            return ph.enter_context(nc.sbuf_tensor(name, shape, dt))

        alloc_ln_rows(ph)
        w_r_bf = P("w_r_bf", [128, ND * D], BF16)
        w_k_bf = P("w_k_bf", [128, ND * D], BF16)
        w_v_bf = P("w_v_bf", [128, ND * D], BF16)
        with nc.sbuf_tensor("w_stage2", [128, 2048], F32) as w_stage:
            load_weight_bf(w_r_bf, w_r, D, D, w_stage)
            load_weight_bf(w_k_bf, w_k, D, D, w_stage)
            load_weight_bf(w_v_bf, w_v, D, D, w_stage)

        TB1 = TB + 1
        xm3 = [P(f"xm{j}", [128, ND * TB], F32) for j in range(3)]
        ln2_b = P("ln2_b", [128, ND * TB1], BF16)
        rsb_bf = P("rsb_bf", [128, TB], BF16)
        xr_s = P("xr_s", [128, ND * TB], BF16)
        xk_s = P("xk_s", [128, ND * TB], BF16)
        t1a = P("t1a", [128, ND * TB], BF16)
        t1v = P("t1v", [128, ND * TB], F32)
        r_f32 = P("r_f32", [128, ND * TB], F32)
        kr_bf = P("kr_bf", [128, ND * TB], BF16)
        x2bf = P("x2bf2", [128, ND * TB], BF16)
        fb = [ph.enter_context(PT(f"p2_f{i}", [128, 512], F32))
              for i in range(3)]
        gb = [ph.enter_context(PT(f"p2_g{i}", [128, 512], F32))
              for i in range(3)]
        pl1 = ph.enter_context(PT("p2_l1", [128, 512], F32))
        pl2 = ph.enter_context(PT("p2_l2", [128, 512], F32))

        nchm = chunks_of(TB)
        LS = dict(qs_load={}, vsq_sv={}, stats_st={}, vec2_sv={},
                  bcast_st={}, vec3c_ss={}, ln2_sg={}, vecmr_sv={},
                  vecmk_sv={}, sig_ss={}, sq_sv={}, evv_sv={},
                  gadd_sg={}, qs_out={}, rmm_st={}, kmm_st={}, vech_sv={})

        def f_vec0(eng):
            for a in range(ND):
                eng.tensor_copy(out=ln2_b[:, a * TB1:a * TB1 + 1],
                                in_=s_ln2H[:, a:a + 1])
            eng.maybe_drain_then_inc((sv.sem, 1))
        sec("vector", f_vec0)
        sv.v += 1

        def load_blk(i):
            A = i % 3
            sgw = LS["gadd_sg"].get(i - 3)

            def f_sync(eng, c0=i * TB, A=A, sgw=sgw):
                if sgw is not None:
                    eng.wait_ge(sg.sem, sgw)
                eng.dma_start(out=xm3[A][:, :],
                              in_=bass.AP(xmid_sp, c0, [[CT, 128],
                                                        [128 * CT, ND],
                                                        [1, TB]])
                              ).then_inc(qs.sem, 16)
                qs.v += 16
            sec("sync", f_sync)
            LS["qs_load"][i] = qs.v

        def vsq(i):
            A = i % 3
            qw = LS["qs_load"][i]

            def f_vsq(eng, A=A, qw=qw):
                eng.wait_ge(qs.sem, qw)
                for a in range(ND):
                    eng.tensor_mul(out=x2bf[:, a * TB:(a + 1) * TB],
                                   in0=xm3[A][:, a * TB:(a + 1) * TB],
                                   in1=xm3[A][:, a * TB:(a + 1) * TB])
                eng.maybe_drain_then_inc((sv.sem, 1))
            sec("vector", f_vsq)
            sv.v += 1
            LS["vsq_sv"][i] = sv.v

        def ln2_s2(i):
            A = i % 3
            qw, svw = LS["qs_load"][i], LS["vsq_sv"][i]
            ssw = LS["vec3c_ss"].get(i - 1, 0)

            def f_tens(eng, A=A, qw=qw, svw=svw, ssw=ssw):
                eng.wait_ge(qs.sem, qw)
                eng.wait_ge(sv.sem, svw)
                eng.wait_ge(ss.sem, ssw)
                for a in range(ND):
                    eng.matmul(out=pl1[0:1, 0:TB], lhsT=s_onescol_f32[:, :],
                               rhs=xm3[A][:, a * TB:(a + 1) * TB],
                               start=(a == 0), stop=(a == ND - 1))
                for a in range(ND):
                    eng.matmul(out=pl2[0:1, 0:TB], lhsT=s_onescol_bf[:, :],
                               rhs=x2bf[:, a * TB:(a + 1) * TB],
                               start=(a == 0), stop=(a == ND - 1))
                eng.maybe_drain_then_inc((st.sem, 1))
            sec("tensor", f_tens)
            st.v += 1
            LS["stats_st"][i] = st.v

        def ln2_s3(i):
            stw = LS["stats_st"][i]

            def f_vec(eng, stw=stw):
                eng.wait_ge(st.sem, stw)
                eng.tensor_copy(out=LN["m"][0:1, 0:TB], in_=pl1[0:1, 0:TB])
                eng.tensor_copy(out=LN["v"][0:1, 0:TB], in_=pl2[0:1, 0:TB])
                eng.drain()
                eng.tensor_scalar_mul(out=LN["m"][0:1, 0:TB],
                                      in0=LN["m"][0:1, 0:TB], scalar1=1.0 / D)
                eng.drain()
                eng.tensor_mul(out=s_scr[0:1, 0:TB], in0=LN["m"][0:1, 0:TB],
                               in1=LN["m"][0:1, 0:TB])
                eng.drain()
                eng.scalar_tensor_tensor(out=LN["v"][0:1, 0:TB],
                                         in0=LN["v"][0:1, 0:TB],
                                         scalar=1.0 / D,
                                         in1=s_scr[0:1, 0:TB],
                                         op0=ALU.mult, op1=ALU.subtract)
                eng.drain()
                eng.tensor_scalar_add(out=LN["v"][0:1, 0:TB],
                                      in0=LN["v"][0:1, 0:TB], scalar1=EPS)
                eng.maybe_drain_then_inc((sv.sem, 1))
            sec("vector", f_vec)
            sv.v += 1
            svw = sv.v

            def f_scal2(eng, svw=svw):
                eng.wait_ge(sv.sem, svw)
                eng.activation(out=s_rs[0:1, 0:TB], in_=LN["v"][0:1, 0:TB],
                               func=AF.Ln)
                eng.drain()
                eng.activation(out=s_rs[0:1, 0:TB], in_=s_rs[0:1, 0:TB],
                               func=AF.Exp, scale=-0.5)
                eng.maybe_drain_then_inc((ss.sem, 1))
            sec("scalar", f_scal2)
            ss.v += 1
            ssw = ss.v

            def f_vec2(eng, ssw=ssw):
                eng.wait_ge(ss.sem, ssw)
                eng.tensor_scalar_mul(out=s_negm[0:1, 0:TB],
                                      in0=LN["m"][0:1, 0:TB], scalar1=-1.0)
                eng.drain()
                eng.tensor_copy(out=LN["negm_bf"][0:1, 0:TB],
                                in_=s_negm[0:1, 0:TB])
                eng.tensor_copy(out=LN["rs_bf"][0:1, 0:TB],
                                in_=s_rs[0:1, 0:TB])
                eng.maybe_drain_then_inc((sv.sem, 1))
            sec("vector", f_vec2)
            sv.v += 1
            LS["vec2_sv"][i] = sv.v

        def ln2_s4(i):
            svw = LS["vec2_sv"][i]

            def f_tens2(eng, svw=svw):
                eng.wait_ge(sv.sem, svw)
                eng.matmul(out=pl1[:, 0:TB], lhsT=s_onesrow_bf[:, :],
                           rhs=LN["negm_bf"][0:1, 0:TB], start=True, stop=True)
                eng.matmul(out=pl2[:, 0:TB], lhsT=s_onesrow_bf[:, :],
                           rhs=LN["rs_bf"][0:1, 0:TB], start=True, stop=True)
                eng.maybe_drain_then_inc((st.sem, 1))
            sec("tensor", f_tens2)
            st.v += 1
            LS["bcast_st"][i] = st.v

        def vec3copy(i):
            stw = LS["bcast_st"][i]
            sgw = LS["ln2_sg"].get(i - 1, 0)

            def f_v3c(eng, stw=stw, sgw=sgw):
                eng.wait_ge(st.sem, stw)
                eng.wait_ge(sg.sem, sgw)
                eng.activation(out=LN["mb"][:, 0:TB], in_=pl1[:, 0:TB],
                               func=AF.Copy)
                eng.activation(out=rsb_bf[:, 0:TB], in_=pl2[:, 0:TB],
                               func=AF.Copy)
                eng.maybe_drain_then_inc((ss.sem, 1))
            sec("scalar", f_v3c)
            ss.v += 1
            LS["vec3c_ss"][i] = ss.v

        def ln2_s5b(i):
            A = i % 3
            ssw = LS["vec3c_ss"][i]
            svw = LS["vech_sv"].get(i - 1, 0)
            qw = LS["qs_load"][i]

            def f_gp(eng, A=A, ssw=ssw, svw=svw, qw=qw):
                eng.wait_ge(ss.sem, ssw)
                eng.wait_ge(sv.sem, svw)
                eng.wait_ge(qs.sem, qw)
                for a in range(ND):
                    eng.tensor_add(out=ln2_b[:, a * TB1 + 1:a * TB1 + 1 + TB],
                                   in0=xm3[A][:, a * TB:(a + 1) * TB],
                                   in1=LN["mb"][:, 0:TB])
                eng.drain()
                for a in range(ND):
                    eng.tensor_mul(out=ln2_b[:, a * TB1 + 1:a * TB1 + 1 + TB],
                                   in0=ln2_b[:, a * TB1 + 1:a * TB1 + 1 + TB],
                                   in1=rsb_bf[:, 0:TB])
                eng.maybe_drain_then_inc((sg.sem, 1))
            sec("gpsimd", f_gp)
            sg.v += 1
            LS["ln2_sg"][i] = sg.v

        def vech(i):
            def f_vech(eng):
                for a in range(ND):
                    eng.tensor_copy(out=ln2_b[:, a * TB1:a * TB1 + 1],
                                    in_=ln2_b[:, a * TB1 + TB:a * TB1 + TB + 1])
                eng.maybe_drain_then_inc((sv.sem, 1))
            sec("vector", f_vech)
            sv.v += 1
            LS["vech_sv"][i] = sv.v

        def stt_half(i, pi):
            mu, imu, dst = ((s_mur, s_imur, xr_s),
                            (s_muk, s_imuk, xk_s))[pi]
            sgw = LS["ln2_sg"][i]
            svw = (LS["vecmk_sv"].get(i - 1, 0) if pi == 0
                   else LS["vecmr_sv"][i])
            stw = (LS["rmm_st"].get(i - 1, 0) if pi == 0
                   else LS["kmm_st"].get(i - 1, 0))

            def f_scal(eng, mu=mu, sgw=sgw, svw=svw):
                eng.wait_ge(sg.sem, sgw)
                eng.wait_ge(sv.sem, svw)
                for a in range(ND):
                    eng.activation(out=t1a[:, a * TB:(a + 1) * TB],
                                   in_=ln2_b[:, a * TB1 + 1:a * TB1 + 1 + TB],
                                   func=AF.Copy, scale=mu[:, a:a + 1])
                eng.maybe_drain_then_inc((ss.sem, 1))
            sec("scalar", f_scal)
            ss.v += 1
            ssw = ss.v

            def f_vecm(eng, imu=imu, dst=dst, ssw=ssw, stw=stw, sgw=sgw):
                eng.wait_ge(ss.sem, ssw)
                eng.wait_ge(st.sem, stw)
                eng.wait_ge(sg.sem, sgw)
                for a in range(ND):
                    eng.scalar_tensor_tensor(
                        out=dst[:, a * TB:(a + 1) * TB],
                        in0=ln2_b[:, a * TB1:a * TB1 + TB],
                        scalar=imu[:, a:a + 1],
                        in1=t1a[:, a * TB:(a + 1) * TB],
                        op0=ALU.mult, op1=ALU.add)
                eng.maybe_drain_then_inc((sv.sem, 1))
            sec("vector", f_vecm)
            sv.v += 1
            if pi == 0:
                LS["vecmr_sv"][i] = sv.v
            else:
                LS["vecmk_sv"][i] = sv.v

        def rmm(i):
            def mmsr(eng, o, bi):
                for (off, cw) in nchm:
                    for a in range(ND):
                        eng.matmul(out=fb[bi][:, off:off + cw],
                                   lhsT=w_r_bf[:, a * D + o * 128:a * D + (o + 1) * 128],
                                   rhs=xr_s[:, a * TB + off:a * TB + off + cw],
                                   start=(a == 0), stop=(a == ND - 1))

            def evr(eng, o, bi):
                eng.activation(out=r_f32[:, o * TB:(o + 1) * TB],
                               in_=fb[bi][:, 0:TB], func=AF.Sigmoid)
            cpw = ((sv, LS["evv_sv"][i - 1]),) if i > 0 else ()
            mm_rounds(ND, 3, mmsr, "scalar", evr,
                      pre_wait=((sv, LS["vecmr_sv"][i]),), cons_pre_wait=cpw)
            LS["sig_ss"][i] = ss.v
            LS["rmm_st"][i] = st.v

        def kmm(i):
            def mmsk(eng, o, bi):
                for (off, cw) in nchm:
                    for a in range(ND):
                        eng.matmul(out=gb[bi][:, off:off + cw],
                                   lhsT=w_k_bf[:, a * D + o * 128:a * D + (o + 1) * 128],
                                   rhs=xk_s[:, a * TB + off:a * TB + off + cw],
                                   start=(a == 0), stop=(a == ND - 1))

            def evk(eng, o, bi):
                eng.tensor_scalar_max(out=kr_bf[:, o * TB:(o + 1) * TB],
                                      in0=gb[bi][:, 0:TB], scalar1=0.0)
            mm_rounds(ND, 3, mmsk, "vector", evk,
                      pre_wait=((sv, LS["vecmk_sv"][i]),))
            LS["kmm_st"][i] = st.v

        def sq(i):
            def f_veck(eng):
                for o in range(ND):
                    eng.tensor_mul(out=kr_bf[:, o * TB:(o + 1) * TB],
                                   in0=kr_bf[:, o * TB:(o + 1) * TB],
                                   in1=kr_bf[:, o * TB:(o + 1) * TB])
                eng.maybe_drain_then_inc((sv.sem, 1))
            sec("vector", f_veck)
            sv.v += 1
            LS["sq_sv"][i] = sv.v

        def vmm(i):
            def mmsv(eng, o, bi):
                for (off, cw) in nchm:
                    for a in range(ND):
                        eng.matmul(out=fb[bi][:, off:off + cw],
                                   lhsT=w_v_bf[:, a * D + o * 128:a * D + (o + 1) * 128],
                                   rhs=kr_bf[:, a * TB + off:a * TB + off + cw],
                                   start=(a == 0), stop=(a == ND - 1))

            def evv(eng, o, bi):
                eng.tensor_mul(out=t1v[:, o * TB:(o + 1) * TB],
                               in0=r_f32[:, o * TB:(o + 1) * TB],
                               in1=fb[bi][:, 0:TB])
            cpw = [(ss, LS["sig_ss"][i])]
            if i > 0:
                cpw.append((qg, LS["qs_out"][i - 1]))
            mm_rounds(ND, 3, mmsv, "vector", evv,
                      pre_wait=((sv, LS["sq_sv"][i]),),
                      cons_pre_wait=tuple(cpw))
            LS["evv_sv"][i] = sv.v

        def fin(i):
            A = i % 3
            svw = LS["evv_sv"][i]

            def f_gp(eng, A=A, svw=svw, c0=i * TB):
                eng.wait_ge(sv.sem, svw)
                for a in range(ND):
                    eng.tensor_add(out=t1v[:, a * TB:(a + 1) * TB],
                                   in0=t1v[:, a * TB:(a + 1) * TB],
                                   in1=xm3[A][:, a * TB:(a + 1) * TB])
                eng.drain()
                eng.dma_start(out=bass.AP(out_d, c0, [[CT, 128],
                                                      [128 * CT, ND], [1, TB]]),
                              in_=t1v[:, :]).then_inc(qg.sem, 16)
                qg.v += 16
                eng.sem_inc(sg.sem, 1)
            sec("gpsimd", f_gp)
            sg.v += 1
            LS["gadd_sg"][i] = sg.v
            LS["qs_out"][i] = qg.v

        load_blk(0)
        vsq(0)
        ln2_s2(0)
        ln2_s3(0)
        ln2_s4(0)
        vec3copy(0)
        ln2_s5b(0)
        load_blk(1)
        vsq(1)
        ln2_s2(1)
        ln2_s3(1)
        ln2_s4(1)
        vec3copy(1)
        stt_half(0, 0)
        stt_half(0, 1)
        vech(0)
        ln2_s5b(1)
        for i in range(NB):
            if i + 2 < NB:
                load_blk(i + 2)
            rmm(i)
            kmm(i)
            if i + 2 < NB:
                vsq(i + 2)
                ln2_s2(i + 2)
                ln2_s3(i + 2)
                ln2_s4(i + 2)
                vec3copy(i + 2)
            sq(i)
            vmm(i)
            if i + 1 < NB:
                stt_half(i + 1, 0)
                stt_half(i + 1, 1)
                vech(i + 1)
            if i + 2 < NB:
                ln2_s5b(i + 2)
            fin(i)
        barrier()
        ph.close()

    upto = p.get("upto", 99)
    emit_init()
    if upto >= 1:
        emit_phase1a()
    if upto >= 2:
        emit_collective()
        coll_es.close()
    if upto >= 3:
        emit_phase1b()
    if upto >= 4:
        scan_es.close()
        emit_phase2()

    if DBG:
        def f_dbg2(eng):
            eng.dma_start(out=d_ln[:, :], in_=ln1_sp[:, :]).then_inc(qs.sem, 16)
            qs.v += 16
            eng.dma_start(out=d_xc[:, :], in_=xconv_sp[:, :]).then_inc(qs.sem, 16)
            qs.v += 16
            eng.dma_start(out=d_xmid[:, :], in_=xmid_sp[:, :]).then_inc(qs.sem, 16)
            qs.v += 16
            eng.wait_ge(qs.sem, qs.v)
        sec("sync", f_dbg2)

    def f_final(eng):
        eng.wait_ge(qs.sem, qs.v)
    sec("sync", f_final)

    def f_finalg(eng):
        eng.wait_ge(qg.sem, qg.v)
    sec("gpsimd", f_finalg)

    es.close()
    return nc



def make_consts():
    tri = np.zeros((128, 128), np.float32)
    for s in range(128):
        tri[s, s:] = 1.0
    mask = np.zeros((128, 128), np.float32)
    for s in range(128):
        mask[s, :s] = BIG
    return dict(
        c_id_bf=np.eye(128, dtype=np.float32).astype(ml_dtypes.bfloat16),
        c_id_f32=np.eye(128, dtype=np.float32),
        c_tri=tri,
        c_mask=mask,
        c_ones=np.ones((1, 128), np.float32),
        c_allones=np.ones((128, 128), np.float32),
    )


def prep_inputs(p, inp):
    n_cores = p["n_cores"]
    CPB = p["cores_per_batch"]
    D = p["d_model"]
    E = p["d_inner"]
    CT = p["ct"]
    ND = D // 128
    NE = E // 128

    x = np.asarray(inp["L0_attn_x"], np.float32)
    conv_w = np.asarray(inp["L0_attn_conv_w"], np.float32)[:, 0, :]
    w_cd = np.zeros((E, 512), np.float32)
    for dt in range(NE):
        for k in range(4):
            blk = np.zeros((128, 128), np.float32)
            np.fill_diagonal(blk, conv_w[dt * 128:(dt + 1) * 128, k])
            w_cd[dt * 128:(dt + 1) * 128, k * 128:(k + 1) * 128] = blk

    def colt(v, nt):
        return np.ascontiguousarray(np.asarray(v, np.float32).reshape(nt, 128).T)

    shared = dict(
        w_inx=np.ascontiguousarray(
            np.asarray(inp["L0_attn_in_proj_w"], np.float32)[:E].T),
        w_inz=np.ascontiguousarray(
            np.asarray(inp["L0_attn_in_proj_w"], np.float32)[E:].T),
        w_xp=np.ascontiguousarray(
            np.asarray(inp["L0_attn_x_proj_w"], np.float32).T),
        w_out=np.ascontiguousarray(
            np.asarray(inp["L0_attn_out_proj_w"], np.float32).T),
        w_r=np.ascontiguousarray(np.asarray(inp["L0_ffn_W_r"], np.float32).T),
        w_k=np.ascontiguousarray(np.asarray(inp["L0_ffn_W_k"], np.float32).T),
        w_v=np.ascontiguousarray(np.asarray(inp["L0_ffn_W_v"], np.float32).T),
        w_cd=w_cd,
        cb_col=colt(inp["L0_attn_conv_b"], NE),
        d_col=colt(inp["L0_attn_D"], NE),
        mur_col=colt(inp["L0_ffn_mu_r"], ND),
        muk_col=colt(inp["L0_ffn_mu_k"], ND),
        alog_row=np.ascontiguousarray(
            np.asarray(inp["L0_attn_A_log"], np.float32)[None, :]),
    )
    shared.update(make_consts())

    in_maps = []
    for c in range(n_cores):
        b = c // CPB
        q0 = (c % CPB) * CT
        xs = np.zeros((D, CT + 4), np.float32)
        lo = q0 - 4
        src_lo = max(lo, 0)
        xs[:, src_lo - lo:] = np.ascontiguousarray(x[b, src_lo:q0 + CT, :].T)
        maskrow = np.zeros((1, n_cores), np.float32)
        predT = np.zeros((n_cores, n_cores), np.float32)
        for j in range(n_cores):
            if j // CPB == b and j < c:
                maskrow[0, j] = 1.0
                for u in range(n_cores):
                    if u // CPB == b and j < u < c:
                        predT[u, j] = 1.0
        m = dict(shared)
        m["x"] = np.ascontiguousarray(xs)
        m["c_maskrow"] = maskrow
        m["c_predT"] = predT
        m["c_hscale"] = np.array([[0.0 if c % CPB == 0 else 1.0]], np.float32)
        in_maps.append(m)
    return in_maps


_CACHE = {}


def kernel(**inputs):
    p = dict(REAL)
    if "real" not in _CACHE:
        _CACHE["real"] = build_program(p)
    nc = _CACHE["real"]
    in_maps = prep_inputs(p, inputs)
    res = run_bass_kernel_spmd(nc, in_maps, core_ids=list(range(p["n_cores"])))
    B, T, D, CT, CPB = (p["batch"], p["seq"], p["d_model"], p["ct"],
                        p["cores_per_batch"])
    out = np.empty((B, T, D), np.float32)
    for c in range(p["n_cores"]):
        b = c // CPB
        q0 = (c % CPB) * CT
        out[b, q0:q0 + CT, :] = res.results[c]["out"].T
    return out


# version 20
# speedup vs baseline: 1.0169x; 1.0169x over previous
import contextlib
import numpy as np
import ml_dtypes

import concourse.bass as bass
import concourse.mybir as mybir
from concourse.bass_utils import run_bass_kernel_spmd

F32 = mybir.dt.float32
BF16 = mybir.dt.bfloat16
AF = mybir.ActivationFunctionType
ALU = mybir.AluOpType
AX = mybir.AxisListType

BIG = 1.0e30
EPS = 1.0e-5

REAL = dict(n_cores=8, batch=2, seq=16384, d_model=1024, d_inner=2048,
            d_state=16, ct=4096, tb=512, cores_per_batch=4)


class Ctr:
    def __init__(self, sem):
        self.sem = sem
        self.v = 0


def build_program(p):
    n_cores = p["n_cores"]
    D = p["d_model"]
    E = p["d_inner"]
    NS = p["d_state"]
    CT = p["ct"]
    TB = p["tb"]
    L = 128
    ND = D // 128
    NE = E // 128
    NB = CT // TB
    NCH = TB // L
    NSS = 2 * NS + 1
    SWW = TB + 4

    nc = bass.Bass("TRN2", target_bir_lowering=False, debug=False,
                   detect_race_conditions=False)
    dp = nc.declare_dram_parameter

    x_in = dp("x", [D, CT + 4], F32, isOutput=False)
    w_inx = dp("w_inx", [D, E], F32, isOutput=False)
    w_inz = dp("w_inz", [D, E], F32, isOutput=False)
    w_xp = dp("w_xp", [E, NSS], F32, isOutput=False)
    w_out = dp("w_out", [E, D], F32, isOutput=False)
    w_r = dp("w_r", [D, D], F32, isOutput=False)
    w_k = dp("w_k", [D, D], F32, isOutput=False)
    w_v = dp("w_v", [D, D], F32, isOutput=False)
    w_cd = dp("w_cd", [E, 512], F32, isOutput=False)
    cb_col = dp("cb_col", [128, NE], F32, isOutput=False)
    d_colp = dp("d_col", [128, NE], F32, isOutput=False)
    mur_col = dp("mur_col", [128, ND], F32, isOutput=False)
    muk_col = dp("muk_col", [128, ND], F32, isOutput=False)
    alog_row = dp("alog_row", [1, NS], F32, isOutput=False)
    c_id_bf = dp("c_id_bf", [128, 128], BF16, isOutput=False)
    c_id_f32 = dp("c_id_f32", [128, 128], F32, isOutput=False)
    c_tri = dp("c_tri", [128, 128], F32, isOutput=False)
    c_mask = dp("c_mask", [128, 128], F32, isOutput=False)
    c_ones = dp("c_ones", [1, 128], F32, isOutput=False)
    c_allones = dp("c_allones", [128, 128], F32, isOutput=False)
    c_maskrow = dp("c_maskrow", [1, n_cores], F32, isOutput=False)
    c_predT = dp("c_predT", [n_cores, n_cores], F32, isOutput=False)
    c_hscale = dp("c_hscale", [1, 1], F32, isOutput=False)
    out_d = dp("out", [D, CT], F32, isOutput=True)
    DBG = bool(p.get("dbg"))
    if DBG:
        d_xdbl = dp("d_xdbl", [NSS, CT], F32, isOutput=True)
        d_hloc = dp("d_hloc", [NS, E], F32, isOutput=True)
        d_hin = dp("d_hin", [NS, E], F32, isOutput=True)
        d_rows = dp("d_rows", [1, 20], F32, isOutput=True)
        d_ln = dp("d_ln", [D, CT + 4], BF16, isOutput=True)
        d_xc = dp("d_xc", [E, CT], BF16, isOutput=True)
        d_xmid = dp("d_xmid", [D, CT], F32, isOutput=True)
        d_xcT = dp("d_xcT", [128, E], BF16, isOutput=True)
        d_etil = dp("d_etil", [128, NS], F32, isOutput=True)
        d_btil = dp("d_btil", [128, NS], BF16, isOutput=True)
        d_dec16 = dp("d_dec16", [NS, 2], F32, isOutput=True)
        d_h0 = dp("d_h0", [NS, E], F32, isOutput=True)
        d_b16 = dp("d_b16", [NS, 512], F32, isOutput=True)
        d_bdt = dp("d_bdt", [128, NS], F32, isOutput=True)

    ln1_sp = nc.dram_tensor("ln1_sp", [D, CT + 4], BF16)
    xconv_sp = nc.dram_tensor("xconv_sp", [E, CT], BF16)
    xmid_sp = nc.dram_tensor("xmid_sp", [D, CT], F32)
    xdbl_sp = nc.dram_tensor("xdbl_sp", [NSS, CT], F32)
    cc_in = nc.dram_tensor("cc_in", [NS + 1, E], F32)
    cc_out = nc.dram_tensor("cc_out", [n_cores * (NS + 1), E], F32)

    es = contextlib.ExitStack()

    def SB(name, shape, dt):
        return es.enter_context(nc.sbuf_tensor(name, shape, dt))

    def PT(name, shape, dt):
        return nc.psum_tensor(name, shape, dt, allow_name_mangling=True)

    def SBM(name, shape, dt):
        return nc.sbuf_tensor(name, shape, dt, allow_name_mangling=True)

    s_onescol_bf = SB("s_onescol_bf", [128, 1], BF16)
    s_onescol_f32 = SB("s_onescol_f32", [128, 1], F32)
    s_onesrow_bf = SB("s_onesrow_bf", [1, 128], BF16)
    s_mur = SB("s_mur", [128, ND], F32)
    s_muk = SB("s_muk", [128, ND], F32)
    s_imur = SB("s_imur", [128, ND], F32)
    s_imuk = SB("s_imuk", [128, ND], F32)
    s_ln2H = SB("s_ln2H", [128, ND], F32)
    s_rs = SB("s_rs", [1, SWW], F32)
    s_negm = SB("s_negm", [1, SWW], F32)
    s_scr = SB("s_scr", [1, SWW], F32)
    scan_es = es.enter_context(contextlib.ExitStack())

    def SC(name, shape, dt):
        return scan_es.enter_context(nc.sbuf_tensor(name, shape, dt))

    s_id_bf = SC("s_id_bf", [128, 128], BF16)
    s_id_f32 = SC("s_id_f32", [128, 128], F32)
    s_tri = SC("s_tri", [128, 128], F32)
    s_mask = SC("s_mask", [128, 128], F32)
    s_ones = SC("s_ones", [1, 128], F32)
    s_allones = SC("s_allones", [128, 128], F32)
    s_h1b = SC("s_h1b", [128, 3], F32)
    s_arow = SC("s_arow", [1, NS], F32)
    s_nAb = SC("s_nAb", [128, NS], F32)
    s_Ab = SC("s_Ab", [128, NS], F32)
    s_acol = SC("s_acol", [NS, 1], F32)
    s_dcol = SC("s_dcol", [128, NE], F32)
    s_cbcol = SC("s_cbcol", [128, NE], F32)
    s_hcur = SC("s_hcur", [NS, E], F32)
    s_hcur_bf = SC("s_hcur_bf", [NS, E], BF16)
    s_xdt = SC("s_xdt", [NSS, TB], F32)
    s_drow = SC("s_drow", [1, TB], F32)
    s_sp1 = SC("s_sp1", [1, TB], F32)
    s_sp2 = SC("s_sp2", [1, TB], F32)
    s_xdblH = SC("s_xdblH", [NSS, 1], F32)
    s_xcH_bf = SC("s_xcH_bf", [128, NE], BF16)
    s_ln1H_bf = SC("s_ln1H_bf", [128, ND * 4], BF16)
    s_zH = SC("s_zH", [128, NE], F32)
    s_ygt = SC("s_ygt", [128, NE], F32)
    s_c16h = SC("s_c16h", [NS, 1], F32)
    s_c16h_bf = SC("s_c16h_bf", [NS, 1], BF16)
    s_ygH_bf = SC("s_ygH_bf", [128, NE], BF16)
    s_xmidH = SC("s_xmidH", [128, ND], F32)
    s_xmidH_bf = SC("s_xmidH_bf", [128, ND], BF16)
    s_xmidH2 = SC("s_xmidH2", [128, ND], F32)
    s_h1r = SC("s_h1r", [1, 3], F32)
    s_hscale = SC("s_hscale", [1, 1], F32)
    s_dT4 = SC("s_dT4", [128, NCH], F32)
    s_S4 = SC("s_S4", [128, NCH], F32)
    s_Stb4 = SC("s_Stb4", [128, NCH], F32)
    s_gap4 = SC("s_gap4", [128, NCH], F32)
    s_TAin = SC("s_TAin", [128, 2 * NCH], F32)
    s_TBin = SC("s_TBin", [128, 2 * NCH], F32)
    s_lhs4 = SC("s_lhs4", [2, 512], F32)
    s_rhs4 = SC("s_rhs4", [2, 512], F32)
    s_Etil4 = SC("s_Etil4", [128, NS * NCH], F32)
    s_dec4 = SC("s_dec4", [NS, NCH], F32)
    s_BdT4 = SC("s_BdT4", [128, NS * NCH], F32)
    s_Btil4_bf = SC("s_Btil4_bf", [128, NS * NCH], BF16)
    s_Ctil4_bf = SC("s_Ctil4_bf", [NS, 512], BF16)
    s_B16 = SC("s_B16", [NS, TB], F32)
    s_C16 = SC("s_C16", [NS, TB], F32)
    s_C16_bf = SC("s_C16_bf", [NS, TB], BF16)
    s_Bd16_bf = SC("s_Bd16_bf", [NS, TB], BF16)
    s_dSTm = SC("s_dSTm", [128, 128], F32)
    s_K_bf = SC("s_K_bf", [128, 256], BF16)
    s_xcT = SC("s_xcT", [128, 2 * E], BF16)
    coll_es = scan_es.enter_context(contextlib.ExitStack())

    def CC(name, shape, dt):
        return coll_es.enter_context(nc.sbuf_tensor(name, shape, dt))

    s_hloc = CC("s_hloc", [NS, E], F32)
    s_W16 = CC("s_W16", [NS, n_cores], F32)
    s_maskrow = CC("s_maskrow", [1, n_cores], F32)
    s_predT = CC("s_predT", [n_cores, n_cores], F32)
    s_accS = CC("s_accS", [1, 1], F32)
    s_S8 = CC("s_S8", [1, n_cores], F32)
    s_c8 = CC("s_c8", [n_cores, 1], F32)
    s_gs = CC("s_gs", [n_cores, 1], F32)
    s_gr = CC("s_gr", [1, n_cores], F32)

    LN = {}

    def alloc_ln_rows(scope_es):
        LN["m"] = scope_es.enter_context(SBM("s_m", [1, SWW], F32))
        LN["v"] = scope_es.enter_context(SBM("s_v", [1, SWW], F32))
        LN["negm_bf"] = scope_es.enter_context(SBM("s_negm_bf", [1, SWW], BF16))
        LN["rs_bf"] = scope_es.enter_context(SBM("s_rs_bf", [1, SWW], BF16))
        LN["mb"] = scope_es.enter_context(SBM("s_mb", [128, SWW], F32))
        LN["rsb"] = scope_es.enter_context(SBM("s_rsb", [128, SWW], F32))

    qs = Ctr(es.enter_context(nc.semaphore("qs")))
    qg = Ctr(es.enter_context(nc.semaphore("qg")))
    st = Ctr(es.enter_context(nc.semaphore("st")))
    sv = Ctr(es.enter_context(nc.semaphore("sv")))
    ss = Ctr(es.enter_context(nc.semaphore("ss")))
    sg = Ctr(es.enter_context(nc.semaphore("sg")))
    sc = Ctr(es.enter_context(nc.semaphore("sc")))
    ENG = ("tensor", "vector", "scalar", "gpsimd", "sync")
    bars = {e: Ctr(es.enter_context(nc.semaphore("bar_" + e))) for e in ENG}
    block = es.enter_context(nc.Block())
    bar_round = [0]

    _OPS = {"matmul", "transpose", "tensor_copy", "tensor_tensor",
            "tensor_add", "tensor_sub", "tensor_mul", "tensor_max",
            "tensor_scalar", "tensor_scalar_mul", "tensor_scalar_add",
            "tensor_scalar_max", "tensor_scalar_min", "tensor_single_scalar",
            "scalar_tensor_tensor", "tensor_reduce", "reduce_sum",
            "reduce_max", "activation", "copy", "add", "mul", "memset",
            "memzero", "reciprocal", "tensor_relu", "tensor_tensor_scan"}

    class EProxy:
        def __init__(self, e):
            self._e = e
            self._last = None

        def __getattr__(self, name):
            a = getattr(self._e, name)
            if name in _OPS and callable(a):
                def wrap(*args, **kw):
                    r = a(*args, **kw)
                    self._last = r
                    return r
                return wrap
            return a

        def drain(self, *a, **kw):
            self._last = None
            return self._e.drain(*a, **kw)

        def dma_start(self, *a, **kw):
            return self._e.dma_start(*a, **kw)

        def maybe_drain_then_inc(self, update, **kw):
            if self._last is not None:
                r = self._last.then_inc(*update)
                self._last = None
                return r
            return self._e.maybe_drain_then_inc(update, **kw)

    def sec(en, f):
        getattr(block, en)(lambda eng: f(EProxy(eng)))

    def barrier():
        bar_round[0] += 1
        r = bar_round[0]

        def mk(en):
            def f(eng):
                if en == "sync":
                    eng.wait_ge(qs.sem, qs.v)
                if en == "gpsimd":
                    eng.wait_ge(qg.sem, qg.v)
                    eng.wait_ge(sc.sem, sc.v)
                eng.maybe_drain_then_inc((bars[en].sem, 1))
                for o in ENG:
                    if o != en:
                        eng.wait_ge(bars[o].sem, r)
            return f
        for en in ENG:
            sec(en, mk(en))

    def chunks_of(width, step=512):
        out = []
        o = 0
        while o < width:
            out.append((o, min(step, width - o)))
            o += step
        return out

    def mm_rounds(n_out, n_banks, emit_mms, cons, emit_evac, pre_wait=(),
                  cons_pre_wait=()):
        rounds = [list(range(r, min(r + n_banks, n_out)))
                  for r in range(0, n_out, n_banks)]
        st_base = st.v
        cctr = {"vector": sv, "scalar": ss}[cons]
        c_base = cctr.v

        def f_tens(eng):
            for c, v in pre_wait:
                eng.wait_ge(c.sem, v)
            for ri, ots in enumerate(rounds):
                if ri > 0:
                    eng.wait_ge(cctr.sem, c_base + ri)
                for i, ot in enumerate(ots):
                    emit_mms(eng, ot, i)
                eng.maybe_drain_then_inc((st.sem, 1))
        sec("tensor", f_tens)
        st.v = st_base + len(rounds)

        def f_cons(eng):
            for c, v in cons_pre_wait:
                eng.wait_ge(c.sem, v)
            for ri, ots in enumerate(rounds):
                eng.wait_ge(st.sem, st_base + ri + 1)
                for i, ot in enumerate(ots):
                    emit_evac(eng, ot, i)
                eng.maybe_drain_then_inc((cctr.sem, 1))
        sec(cons, f_cons)
        cctr.v = c_base + len(rounds)

    def emit_init():
        def f_sync(eng):
            for dst, src in ((s_id_bf, c_id_bf), (s_id_f32, c_id_f32),
                             (s_tri, c_tri), (s_mask, c_mask), (s_ones, c_ones),
                             (s_allones, c_allones),
                             (s_arow, alog_row), (s_dcol, d_colp),
                             (s_cbcol, cb_col), (s_mur, mur_col),
                             (s_muk, muk_col), (s_maskrow, c_maskrow),
                             (s_predT, c_predT), (s_hscale, c_hscale)):
                eng.dma_start(out=dst[:, :], in_=src[:, :]).then_inc(qs.sem, 16)
                qs.v += 16
        sec("sync", f_sync)
        barrier()
        with PT("pi_a", [128, NS], F32) as p_ab, \
             PT("pi_ac", [NS, 1], F32) as p_ac:
            def f_scal(eng):
                eng.activation(out=s_arow[:, :], in_=s_arow[:, :], func=AF.Exp)
                eng.maybe_drain_then_inc((ss.sem, 1))
            sec("scalar", f_scal)
            ss.v += 1

            def f_vec(eng):
                eng.memset(s_onescol_bf[:, :], 1.0)
                eng.memset(s_onescol_f32[:, :], 1.0)
                eng.memset(s_onesrow_bf[:, :], 1.0)
                eng.memset(s_TAin.rearrange("p (c two) -> p c two",
                                             two=2)[:, :, 0:1], 1.0)
                eng.memset(s_TBin.rearrange("p (c two) -> p c two",
                                            two=2)[:, :, 1:2], -1.0)
                eng.memset(s_accS[:, :], 0.0)
                eng.memset(s_hloc[:, :], 0.0)
                eng.memset(s_hcur[:, :], 0.0)
                eng.tensor_scalar(out=s_imur[:, :], in0=s_mur[:, :], scalar1=-1.0,
                                  scalar2=1.0, op0=ALU.mult, op1=ALU.add)
                eng.tensor_scalar(out=s_imuk[:, :], in0=s_muk[:, :], scalar1=-1.0,
                                  scalar2=1.0, op0=ALU.mult, op1=ALU.add)
                eng.maybe_drain_then_inc((sv.sem, 1))
            sec("vector", f_vec)
            sv.v += 1

            def f_tens(eng):
                eng.wait_ge(ss.sem, ss.v)
                eng.matmul(out=p_ab[:, :], lhsT=s_ones[0:1, :], rhs=s_arow[:, :],
                           start=True, stop=True)
                eng.transpose(out=p_ac[:, :], in_=s_arow[:, :],
                              identity=s_id_f32[0:1, 0:1])
                eng.maybe_drain_then_inc((st.sem, 1))
            sec("tensor", f_tens)
            st.v += 1

            def f_vec2(eng):
                eng.wait_ge(st.sem, st.v)
                eng.tensor_copy(out=s_nAb[:, :], in_=p_ab[:, :])
                eng.tensor_scalar_mul(out=s_Ab[:, :], in0=p_ab[:, :], scalar1=-1.0)
                eng.tensor_scalar_mul(out=s_acol[:, :], in0=p_ac[:, :], scalar1=-1.0)
                eng.tensor_scalar_mul(out=s_arow[:, :], in0=s_arow[:, :],
                                      scalar1=-1.0)
                eng.maybe_drain_then_inc((sv.sem, 1))
            sec("vector", f_vec2)
            sv.v += 1
            barrier()

    def load_weight_bf(dst_bf, src_dram, rows, cols, stage):
        nt = rows // 128
        total = nt * cols
        H = 1024
        step = (H - (H % cols)) if H >= cols else H
        slices = []
        o = 0
        while o < total:
            w = min(step, total - o)
            if cols > step:
                w = min(w, cols - o % cols)
            slices.append((o, w))
            o += w
        guard0 = sv.v
        cast_sv = {}
        for k, (o, w) in enumerate(slices):
            half = (k % 2) * H
            svw = cast_sv.get(k - 2, guard0)

            def f_sync(eng, o=o, w=w, half=half, svw=svw):
                eng.wait_ge(sv.sem, svw)
                t0 = o // cols
                co = o % cols
                ntt = max(1, w // cols)
                eng.dma_start(
                    out=stage[:, half:half + w],
                    in_=bass.AP(src_dram, t0 * 128 * cols + co,
                                [[cols, 128], [128 * cols, ntt], [1, min(w, cols)]]),
                ).then_inc(qs.sem, 16)
                qs.v += 16
            sec("sync", f_sync)
            qw = qs.v

            def f_vec(eng, o=o, w=w, half=half, qw=qw):
                eng.wait_ge(qs.sem, qw)
                eng.tensor_copy(out=dst_bf[:, o:o + w],
                                in_=stage[:, half:half + w])
                eng.maybe_drain_then_inc((sv.sem, 1))
            sec("vector", f_vec)
            sv.v += 1
            cast_sv[k] = sv.v

    def ln_stats(x_tile, xbf_tile, x2bf_tile, w):
        nch = chunks_of(w)
        with PT("pl_s1a", [1, 512], F32) as p_s1a, \
             PT("pl_s2a", [1, 512], F32) as p_s2a:
            def f_scal(eng):
                for a in range(ND):
                    eng.activation(out=xbf_tile[:, a * w:(a + 1) * w],
                                   in_=x_tile[:, a * w:(a + 1) * w], func=AF.Copy)
                eng.maybe_drain_then_inc((ss.sem, 1))
            sec("scalar", f_scal)
            ss.v += 1

            def f_vsq(eng):
                for a in range(ND):
                    eng.tensor_mul(out=x2bf_tile[:, a * w:(a + 1) * w],
                                   in0=x_tile[:, a * w:(a + 1) * w],
                                   in1=x_tile[:, a * w:(a + 1) * w])
                eng.maybe_drain_then_inc((sv.sem, 1))
            sec("vector", f_vsq)
            sv.v += 1

            def f_tens(eng):
                eng.wait_ge(ss.sem, ss.v)
                eng.wait_ge(sv.sem, sv.v)
                for ci, (off, cw) in enumerate(nch):
                    po = 0 if ci == 0 else 448
                    for a in range(ND):
                        eng.matmul(out=p_s1a[0:1, po:po + cw],
                                   lhsT=s_onescol_bf[:, :],
                                   rhs=xbf_tile[:, a * w + off:a * w + off + cw],
                                   start=(a == 0), stop=(a == ND - 1))
                    for a in range(ND):
                        eng.matmul(out=p_s2a[0:1, po:po + cw],
                                   lhsT=s_onescol_bf[:, :],
                                   rhs=x2bf_tile[:, a * w + off:a * w + off + cw],
                                   start=(a == 0), stop=(a == ND - 1))
                eng.maybe_drain_then_inc((st.sem, 1))
            sec("tensor", f_tens)
            st.v += 1

            def f_vec(eng):
                eng.wait_ge(st.sem, st.v)
                for ci, (off, cw) in enumerate(nch):
                    po = 0 if ci == 0 else 448
                    eng.tensor_copy(out=LN["m"][0:1, off:off + cw],
                                    in_=p_s1a[0:1, po:po + cw])
                    eng.tensor_copy(out=LN["v"][0:1, off:off + cw],
                                    in_=p_s2a[0:1, po:po + cw])
                eng.drain()
                eng.tensor_scalar_mul(out=LN["m"][0:1, 0:w], in0=LN["m"][0:1, 0:w],
                                      scalar1=1.0 / D)
                eng.drain()
                eng.tensor_mul(out=s_scr[0:1, 0:w], in0=LN["m"][0:1, 0:w],
                               in1=LN["m"][0:1, 0:w])
                eng.drain()
                eng.scalar_tensor_tensor(out=LN["v"][0:1, 0:w],
                                         in0=LN["v"][0:1, 0:w],
                                         scalar=1.0 / D, in1=s_scr[0:1, 0:w],
                                         op0=ALU.mult, op1=ALU.subtract)
                eng.drain()
                eng.tensor_scalar_add(out=LN["v"][0:1, 0:w], in0=LN["v"][0:1, 0:w],
                                      scalar1=EPS)
                eng.maybe_drain_then_inc((sv.sem, 1))
            sec("vector", f_vec)
            sv.v += 1

            def f_scal2(eng):
                eng.wait_ge(sv.sem, sv.v)
                eng.activation(out=s_rs[0:1, 0:w], in_=LN["v"][0:1, 0:w], func=AF.Ln)
                eng.drain()
                eng.activation(out=s_rs[0:1, 0:w], in_=s_rs[0:1, 0:w],
                               func=AF.Exp, scale=-0.5)
                eng.maybe_drain_then_inc((ss.sem, 1))
            sec("scalar", f_scal2)
            ss.v += 1

            def f_vec2(eng):
                eng.wait_ge(ss.sem, ss.v)
                eng.tensor_scalar_mul(out=s_negm[0:1, 0:w], in0=LN["m"][0:1, 0:w],
                                      scalar1=-1.0)
                eng.drain()
                eng.tensor_copy(out=LN["negm_bf"][0:1, 0:w], in_=s_negm[0:1, 0:w])
                eng.tensor_copy(out=LN["rs_bf"][0:1, 0:w], in_=s_rs[0:1, 0:w])
                eng.maybe_drain_then_inc((sv.sem, 1))
            sec("vector", f_vec2)
            sv.v += 1

        with PT("pl_b1", [128, 512], F32) as p_b1, \
             PT("pl_b2", [128, 512], F32) as p_b2:
            def f_tens2(eng):
                eng.wait_ge(sv.sem, sv.v)
                for ci, (off, cw) in enumerate(nch):
                    po = 0 if ci == 0 else 448
                    eng.matmul(out=p_b1[:, po:po + cw],
                               lhsT=s_onesrow_bf[:, :],
                               rhs=LN["negm_bf"][0:1, off:off + cw],
                               start=True, stop=True)
                    eng.matmul(out=p_b2[:, po:po + cw],
                               lhsT=s_onesrow_bf[:, :],
                               rhs=LN["rs_bf"][0:1, off:off + cw],
                               start=True, stop=True)
                eng.maybe_drain_then_inc((st.sem, 1))
            sec("tensor", f_tens2)
            st.v += 1

            def f_vec3(eng):
                eng.wait_ge(st.sem, st.v)
                for ci, (off, cw) in enumerate(nch):
                    po = 0 if ci == 0 else 448
                    eng.tensor_copy(out=LN["mb"][:, off:off + cw],
                                    in_=p_b1[:, po:po + cw])
                    eng.tensor_copy(out=LN["rsb"][:, off:off + cw],
                                    in_=p_b2[:, po:po + cw])
                eng.maybe_drain_then_inc((sv.sem, 1))
            sec("vector", f_vec3)
            sv.v += 1

    def emit_ln1(x_tile, xbf_tile, x2bf_tile, ln_bf, w):
        ln_stats(x_tile, xbf_tile, x2bf_tile, w)

        def f_gp(eng):
            eng.wait_ge(sv.sem, sv.v)
            for a in range(ND):
                eng.tensor_add(out=x_tile[:, a * w:(a + 1) * w],
                               in0=x_tile[:, a * w:(a + 1) * w],
                               in1=LN["mb"][:, 0:w])
            eng.maybe_drain_then_inc((sg.sem, 1))
        sec("gpsimd", f_gp)
        sg.v += 1

        def f_vec4(eng):
            eng.wait_ge(sg.sem, sg.v)
            for a in range(ND):
                eng.tensor_mul(out=ln_bf[:, a * w:(a + 1) * w],
                               in0=x_tile[:, a * w:(a + 1) * w],
                               in1=LN["rsb"][:, 0:w])
            eng.maybe_drain_then_inc((sv.sem, 1))
        sec("vector", f_vec4)
        sv.v += 1
        barrier()

    def emit_ln2(x_tile, xbf_tile, x2bf_tile, ln_f32, tb1):
        w = TB
        ln_stats(x_tile, xbf_tile, x2bf_tile, w)

        def f_gp(eng):
            eng.wait_ge(sv.sem, sv.v)
            for a in range(ND):
                eng.tensor_add(out=ln_f32[:, a * tb1 + 1:a * tb1 + 1 + w],
                               in0=x_tile[:, a * w:(a + 1) * w],
                               in1=LN["mb"][:, 0:w])
            eng.maybe_drain_then_inc((sg.sem, 1))
        sec("gpsimd", f_gp)
        sg.v += 1

        def f_vec4(eng):
            eng.wait_ge(sg.sem, sg.v)
            for a in range(ND):
                eng.tensor_mul(out=ln_f32[:, a * tb1 + 1:a * tb1 + 1 + w],
                               in0=ln_f32[:, a * tb1 + 1:a * tb1 + 1 + w],
                               in1=LN["rsb"][:, 0:w])
            eng.maybe_drain_then_inc((sv.sem, 1))
        sec("vector", f_vec4)
        sv.v += 1
        barrier()

    def emit_softplus_row(dst_ap, src_ap, w, pre=()):
        def f_scal(eng):
            for c, v in pre:
                eng.wait_ge(c.sem, v)
            eng.activation(out=s_sp1[0:1, 0:w], in_=src_ap, func=AF.Abs)
            eng.drain()
            eng.activation(out=s_sp1[0:1, 0:w], in_=s_sp1[0:1, 0:w],
                           func=AF.Exp, scale=-1.0)
            eng.maybe_drain_then_inc((ss.sem, 1))
        sec("scalar", f_scal)
        ss.v += 1

        def f_vec(eng):
            eng.wait_ge(ss.sem, ss.v)
            for c, v in pre:
                eng.wait_ge(c.sem, v)
            eng.tensor_scalar_add(out=s_sp1[0:1, 0:w], in0=s_sp1[0:1, 0:w],
                                  scalar1=1.0)
            eng.tensor_scalar_max(out=s_sp2[0:1, 0:w], in0=src_ap, scalar1=0.0)
            eng.maybe_drain_then_inc((sv.sem, 1))
        sec("vector", f_vec)
        sv.v += 1

        def f_scal2(eng):
            eng.wait_ge(sv.sem, sv.v)
            eng.activation(out=s_sp1[0:1, 0:w], in_=s_sp1[0:1, 0:w],
                           func=AF.Ln)
            eng.maybe_drain_then_inc((ss.sem, 1))
        sec("scalar", f_scal2)
        ss.v += 1

        def f_vec2(eng):
            eng.wait_ge(ss.sem, ss.v)
            eng.tensor_add(out=dst_ap, in0=s_sp2[0:1, 0:w],
                           in1=s_sp1[0:1, 0:w])
            eng.maybe_drain_then_inc((sv.sem, 1))
        sec("vector", f_vec2)
        sv.v += 1

    def emit_Sbatch(with_ct, accumulate_S):
        with PT("p_dT4", [128, NCH], F32) as p_dT4:
            def f_t1(eng):
                eng.wait_ge(sv.sem, sv.v)
                for c in range(NCH):
                    eng.transpose(out=p_dT4[:, c:c + 1],
                                  in_=s_scr[0:1, c * 128:(c + 1) * 128],
                                  identity=s_id_f32[0:1, 0:1])
                eng.maybe_drain_then_inc((st.sem, 1))
            sec("tensor", f_t1)
            st.v += 1

            def f_v1(eng):
                eng.wait_ge(st.sem, st.v)
                eng.tensor_copy(out=s_dT4[:, :], in_=p_dT4[:, :])
                eng.maybe_drain_then_inc((sv.sem, 1))
            sec("vector", f_v1)
            sv.v += 1

        with PT("p_S4", [128, NCH], F32) as p_S4, \
             PT("p_St4", [128, NCH], F32) as p_St4:
            def f_t2(eng):
                eng.wait_ge(sv.sem, sv.v)
                eng.matmul(out=p_S4[:, :], lhsT=s_tri[:, :], rhs=s_dT4[:, :],
                           start=True, stop=True)
                eng.matmul(out=p_St4[:, :], lhsT=s_allones[:, :],
                           rhs=s_dT4[:, :], start=True, stop=True)
                eng.maybe_drain_then_inc((st.sem, 1))
            sec("tensor", f_t2)
            st.v += 1

            def f_v2(eng):
                eng.wait_ge(st.sem, st.v)
                eng.tensor_copy(out=s_S4[:, :], in_=p_S4[:, :])
                eng.tensor_copy(out=s_Stb4[:, :], in_=p_St4[:, :])
                eng.drain()
                eng.tensor_sub(out=s_gap4[:, :], in0=s_S4[:, :],
                               in1=s_Stb4[:, :])
                eng.tensor_copy(out=s_TAin.rearrange(
                    "p (c two) -> p c two", two=2)[:, :, 1:2],
                    in_=s_S4.rearrange("p (c one) -> p c one", one=1))
                eng.tensor_copy(out=s_TBin.rearrange(
                    "p (c two) -> p c two", two=2)[:, :, 0:1],
                    in_=s_S4.rearrange("p (c one) -> p c one", one=1))
                if accumulate_S:
                    eng.drain()
                    eng.tensor_reduce(out=s_scr[0:1, 0:1],
                                      in_=s_Stb4[0:1, :], axis=AX.X,
                                      op=ALU.add)
                    eng.drain()
                    eng.tensor_add(out=s_accS[:, :], in0=s_accS[:, :],
                                   in1=s_scr[0:1, 0:1])
                eng.maybe_drain_then_inc((sv.sem, 1))
            sec("vector", f_v2)
            sv.v += 1

        with PT("p_LHS", [2, 512], F32) as p_LHS, \
             PT("p_RHS", [2, 512], F32) as p_RHS, \
             PT("p_t16", [NS, NCH], F32) as p_t16:
            def f_t3(eng):
                eng.wait_ge(sv.sem, sv.v)
                for c in range(NCH):
                    eng.transpose(out=p_LHS[:, c * 128:(c + 1) * 128],
                                  in_=s_TAin[:, 2 * c:2 * c + 2],
                                  identity=s_id_f32[:, :])
                    eng.transpose(out=p_RHS[:, c * 128:(c + 1) * 128],
                                  in_=s_TBin[:, 2 * c:2 * c + 2],
                                  identity=s_id_f32[:, :])
                eng.matmul(out=p_t16[:, :], lhsT=s_arow[:, :],
                           rhs=s_Stb4[0:1, :], start=True, stop=True)
                eng.maybe_drain_then_inc((st.sem, 1))
            sec("tensor", f_t3)
            st.v += 1

            def f_v3(eng):
                eng.wait_ge(st.sem, st.v)
                eng.tensor_copy(out=s_lhs4[:, 0:NCH * 128], in_=p_LHS[:, 0:NCH * 128])
                eng.tensor_copy(out=s_rhs4[:, 0:NCH * 128], in_=p_RHS[:, 0:NCH * 128])
                eng.maybe_drain_then_inc((sv.sem, 1))
            sec("vector", f_v3)
            sv.v += 1

            def f_s1(eng):
                eng.wait_ge(sv.sem, sv.v)
                for c in range(NCH):
                    eng.activation(out=s_Etil4[:, c * NS:(c + 1) * NS],
                                   in_=s_nAb[:, :], func=AF.Exp,
                                   scale=s_gap4[:, c:c + 1])
                eng.activation(out=s_dec4[:, :], in_=p_t16[:, :], func=AF.Exp)
                eng.maybe_drain_then_inc((ss.sem, 1))
            sec("scalar", f_s1)
            ss.v += 1

        with PT("p_BdT4", [128, NS * NCH], BF16) as p_BdT4:
            def f_t4(eng):
                eng.wait_ge(sv.sem, sv.v)
                for c in range(NCH):
                    eng.transpose(out=p_BdT4[:, c * NS:(c + 1) * NS],
                                  in_=s_Bd16_bf[:, c * 128:(c + 1) * 128],
                                  identity=s_id_bf[0:NS, 0:NS])
                eng.maybe_drain_then_inc((st.sem, 1))
            sec("tensor", f_t4)
            st.v += 1

            def f_v4(eng):
                eng.wait_ge(st.sem, st.v)
                eng.wait_ge(ss.sem, ss.v)
                eng.tensor_copy(out=s_BdT4[:, :], in_=p_BdT4[:, :])
                eng.drain()
                eng.tensor_mul(out=s_Btil4_bf[:, :], in0=s_Etil4[:, :],
                               in1=s_BdT4[:, :])
                eng.maybe_drain_then_inc((sv.sem, 1))
            sec("vector", f_v4)
            sv.v += 1

        if with_ct:
            with PT("p_sb16", [NS, 512], F32) as p_sb16:
                def f_t5(eng):
                    eng.wait_ge(sv.sem, sv.v)
                    eng.matmul(out=p_sb16[:, 0:NCH * 128],
                               lhsT=s_ones[0:1, 0:NS],
                               rhs=s_rhs4[0:1, 0:NCH * 128],
                               start=True, stop=True)
                    eng.maybe_drain_then_inc((st.sem, 1))
                sec("tensor", f_t5)
                st.v += 1

                def f_s2(eng):
                    eng.wait_ge(st.sem, st.v)
                    eng.activation(out=s_Ctil4_bf[:, 0:NCH * 128],
                                   in_=p_sb16[:, 0:NCH * 128], func=AF.Exp,
                                   scale=s_acol[:, 0:1])
                    eng.maybe_drain_then_inc((ss.sem, 1))
                sec("scalar", f_s2)
                ss.v += 1

                def f_v5(eng):
                    eng.wait_ge(ss.sem, ss.v)
                    eng.tensor_mul(out=s_Ctil4_bf[:, 0:NCH * 128],
                                   in0=s_C16_bf[:, 0:NCH * 128],
                                   in1=s_Ctil4_bf[:, 0:NCH * 128])
                    eng.maybe_drain_then_inc((sv.sem, 1))
                sec("vector", f_v5)
                sv.v += 1

    def emit_Bd_block(b0, w, with_C, pre_q=None, sbuf_pre=None):
        def f_gp(eng):
            if pre_q is not None:
                eng.wait_ge(qs.sem, pre_q)
            if sbuf_pre is not None:
                eng.wait_ge(sv.sem, sbuf_pre)
                eng.dma_start(out=s_B16[:, 0:w],
                              in_=s_xdt[1:NS + 1, 0:w]).then_inc(qg.sem, 16)
                qg.v += 16
            else:
                eng.dma_start(out=s_B16[:, 0:w],
                              in_=bass.AP(xdbl_sp, CT + b0, [[CT, NS], [1, w]])
                              ).then_inc(qg.sem, 16)
                qg.v += 16
            if with_C:
                eng.dma_start(out=s_C16[:, 0:w],
                              in_=bass.AP(xdbl_sp, (NS + 1) * CT + b0,
                                          [[CT, NS], [1, w]])
                              ).then_inc(qg.sem, 16)
                qg.v += 16
            if sbuf_pre is None:
                eng.dma_start(out=s_drow[0:1, 0:w],
                              in_=bass.AP(xdbl_sp, b0, [[1, 1], [1, w]])
                              ).then_inc(qg.sem, 16)
                qg.v += 16
            eng.wait_ge(qg.sem, qg.v)
            eng.maybe_drain_then_inc((sg.sem, 1))
        sec("gpsimd", f_gp)
        sg.v += 1

        sp_src = (s_xdt[0:1, 0:w] if sbuf_pre is not None
                  else s_drow[0:1, 0:w])
        emit_softplus_row(s_scr[0:1, 0:w], sp_src, w,
                          pre=((sg, sg.v),))

        with PT("p_db", [NS, 512], F32) as p_db:
            def f_tens(eng):
                eng.wait_ge(sv.sem, sv.v)
                eng.matmul(out=p_db[:, 0:w], lhsT=s_ones[0:1, 0:NS],
                           rhs=s_scr[0:1, 0:w], start=True, stop=True)
                eng.maybe_drain_then_inc((st.sem, 1))
            sec("tensor", f_tens)
            st.v += 1

            def f_vec(eng):
                eng.wait_ge(st.sem, st.v)
                eng.wait_ge(sg.sem, sg.v)
                eng.tensor_mul(out=s_Bd16_bf[:, 0:w], in0=s_B16[:, 0:w],
                               in1=p_db[:, 0:w])
                if with_C:
                    eng.tensor_copy(out=s_C16_bf[:, 0:w], in_=s_C16[:, 0:w])
                eng.maybe_drain_then_inc((sv.sem, 1))
            sec("vector", f_vec)
            sv.v += 1

    def emit_xcT(kc, src_blk, p_tr, dst_off=0, evac="vector",
                 extra_waits=()):
        ctr = {"vector": sv, "scalar": ss}[evac]
        for half in range(2):
            def f_t(eng, half=half, kc=kc):
                if half == 0:
                    for c, v in extra_waits:
                        eng.wait_ge(c.sem, v)
                else:
                    eng.wait_ge(ctr.sem, ctr.v)
                for j in range(8):
                    et = half * 8 + j
                    eng.transpose(out=p_tr[:, j * 128:(j + 1) * 128],
                                  in_=src_blk[:, et * TB + kc * 128:
                                              et * TB + kc * 128 + 128],
                                  identity=s_id_bf[:, :])
                eng.maybe_drain_then_inc((st.sem, 1))
            sec("tensor", f_t)
            st.v += 1

            def f_v(eng, half=half):
                eng.wait_ge(st.sem, st.v)
                for j in range(8):
                    et = half * 8 + j
                    o = dst_off + et * 128
                    if evac == "vector":
                        eng.tensor_copy(out=s_xcT[:, o:o + 128],
                                        in_=p_tr[:, j * 128:(j + 1) * 128])
                    else:
                        eng.activation(out=s_xcT[:, o:o + 128],
                                       in_=p_tr[:, j * 128:(j + 1) * 128],
                                       func=AF.Copy)
                eng.maybe_drain_then_inc((ctr.sem, 1))
            sec(evac, f_v)
            ctr.v += 1

    def emit_h_update(kc, xcT_tile, h_dst, h_src, copy_bf=False):
        nslice = E // 512
        with PT("p_hd0", [NS, 512], F32) as p0, \
             PT("p_hd1", [NS, 512], F32) as p1, \
             PT("p_hd2", [NS, 512], F32) as p2, \
             PT("p_hd3", [NS, 512], F32) as p3:
            pbanks = [p0, p1, p2, p3]

            def f_tens(eng):
                eng.wait_ge(sv.sem, sv.v)
                eng.wait_ge(ss.sem, ss.v)
                eng.wait_ge(qs.sem, qs.v)
                for j in range(nslice):
                    eng.matmul(out=pbanks[j][:, :],
                               lhsT=s_Btil4_bf[:, kc * NS:(kc + 1) * NS],
                               rhs=xcT_tile[:, j * 512:(j + 1) * 512],
                               start=True, stop=True)
                eng.maybe_drain_then_inc((st.sem, 1))
            sec("tensor", f_tens)
            st.v += 1

            def f_vec(eng):
                eng.wait_ge(st.sem, st.v)
                for j in range(nslice):
                    eng.scalar_tensor_tensor(
                        out=h_dst[:, j * 512:(j + 1) * 512],
                        in0=h_src[:, j * 512:(j + 1) * 512],
                        scalar=s_dec4[:, kc:kc + 1], in1=pbanks[j][:, :],
                        op0=ALU.mult, op1=ALU.add)
                if copy_bf:
                    eng.drain()
                    eng.tensor_copy(out=s_hcur_bf[:, :], in_=h_dst[:, :])
                eng.maybe_drain_then_inc((sv.sem, 1))
            sec("vector", f_vec)
            sv.v += 1

    def emit_phase1a():
        sub = p.get("sub1a", 99)
        ph = contextlib.ExitStack()

        def P(name, shape, dt):
            return ph.enter_context(nc.sbuf_tensor(name, shape, dt))

        alloc_ln_rows(ph)
        w_inx_bf = P("w_inx_bf", [128, ND * E], BF16)
        w_cd_bf = P("w_cd_bf", [128, NE * 512], BF16)
        w_xp_bf = P("w_xp_bf", [128, NE * 33], BF16)
        with nc.sbuf_tensor("w_stage1a", [128, 2048], F32) as w_stage:
            load_weight_bf(w_inx_bf, w_inx, D, E, w_stage)
            load_weight_bf(w_cd_bf, w_cd, E, 512, w_stage)
            load_weight_bf(w_xp_bf, w_xp, E, 33, w_stage)
        barrier()

        x_blk = P("x_blk", [128, ND * SWW], F32)
        p_tr1a = ph.enter_context(PT("p_tr1a", [128, 1024], BF16))
        xbf_blk = P("xbf_blk", [128, ND * SWW], BF16)
        ln_blk = P("ln_blk", [128, ND * SWW], BF16)
        xi_blk = P("xi_blk", [128, NE * SWW], BF16)
        xc_blk = P("xc_blk", [128, NE * TB], BF16)
        x2bf = P("x2bf1a", [128, ND * SWW], BF16)
        s_lnscr = P("s_lnscr", [1, SWW], F32)
        s_lnrs = P("s_lnrs", [1, SWW], F32)
        p_lnA = ph.enter_context(PT("p_lnA", [128, 512], F32))
        p_lnB = ph.enter_context(PT("p_lnB", [128, 512], F32))
        x2bf_h = xbf_blk

        if sub < 1:
            ph.close()
            return
        def f_sync(eng):
            eng.dma_start(out=x_blk[:, 0:ND * 4],
                          in_=bass.AP(x_in, 0, [[CT + 4, 128],
                                                [128 * (CT + 4), ND], [1, 4]])
                          ).then_inc(qs.sem, 16)
            qs.v += 16
            eng.wait_ge(qs.sem, qs.v)
        sec("sync", f_sync)
        barrier()
        emit_ln1(x_blk, xbf_blk, x2bf, s_ln1H_bf, 4)

        if sub < 2:
            ph.close()
            return
        with PT("p_xiH", [128, 64], F32) as p_xiH:
            def mms(eng, et, i):
                for a in range(ND):
                    eng.matmul(out=p_xiH[:, i * 4:i * 4 + 4],
                               lhsT=w_inx_bf[:, a * E + et * 128:a * E + (et + 1) * 128],
                               rhs=s_ln1H_bf[:, a * 4:(a + 1) * 4],
                               start=(a == 0), stop=(a == ND - 1))

            def ev(eng, et, i):
                eng.activation(out=xi_blk[:, et * SWW:et * SWW + 4],
                               in_=p_xiH[:, i * 4:i * 4 + 4], func=AF.Copy)
            mm_rounds(NE, 4, mms, "scalar", ev)
        barrier()
        with PT("p_xcH", [128, 64], F32) as p_xcH:
            def mms(eng, et, i):
                for k in range(4):
                    eng.matmul(out=p_xcH[:, i:i + 1],
                               lhsT=w_cd_bf[:, et * 512 + k * 128:et * 512 + (k + 1) * 128],
                               rhs=xi_blk[:, et * SWW + k:et * SWW + k + 1],
                               start=(k == 0), stop=(k == 3))

            def ev(eng, et, i):
                eng.activation(out=s_xcH_bf[:, et:et + 1], in_=p_xcH[:, i:i + 1],
                               func=AF.Silu, bias=s_cbcol[:, et:et + 1])
            mm_rounds(NE, 4, mms, "scalar", ev)
        barrier()
        with PT("p_xdH", [NSS, 1], F32) as p_xdH:
            def f_tens(eng):
                for et in range(NE):
                    eng.matmul(out=p_xdH[:, :],
                               lhsT=w_xp_bf[:, et * 33:(et + 1) * 33],
                               rhs=s_xcH_bf[:, et:et + 1],
                               start=(et == 0), stop=(et == NE - 1))
                eng.maybe_drain_then_inc((st.sem, 1))
            sec("tensor", f_tens)
            st.v += 1

            def f_vec(eng):
                eng.wait_ge(st.sem, st.v)
                eng.tensor_copy(out=s_xdblH[:, :], in_=p_xdH[:, :])
                eng.maybe_drain_then_inc((sv.sem, 1))
            sec("vector", f_vec)
            sv.v += 1
        barrier()

        if sub < 3:
            ph.close()
            return
        nchm = chunks_of(TB)
        LA = dict(ld_qs={}, vsq_sv={}, stats_st={}, vec2_sv={}, bcast_st={},
                  vec3_sv={}, cadd_sg={}, vec4_sv={}, lnsp_qs={},
                  inproj_st={}, inev_ss={}, silu_ss={}, xcsp_qs={},
                  xdsp_qs={}, bd_sg={}, sp_ss={}, sp_sv={})

        def ld(i):
            svw = LA["vec4_sv"].get(i - 1, 0)

            def f_sync(eng, c0=i * TB, svw=svw):
                eng.wait_ge(sv.sem, svw)
                eng.dma_start(out=x_blk[:, :],
                              in_=bass.AP(x_in, c0, [[CT + 4, 128],
                                                     [128 * (CT + 4), ND],
                                                     [1, SWW]])
                              ).then_inc(qs.sem, 16)
                qs.v += 16
            sec("sync", f_sync)
            LA["ld_qs"][i] = qs.v

        def vsq(i):
            qw = LA["ld_qs"][i]

            def f_vsq(eng, qw=qw):
                eng.wait_ge(qs.sem, qw)
                for a in range(ND):
                    eng.tensor_mul(out=x2bf[:, a * SWW:(a + 1) * SWW],
                                   in0=x_blk[:, a * SWW:(a + 1) * SWW],
                                   in1=x_blk[:, a * SWW:(a + 1) * SWW])
                eng.maybe_drain_then_inc((sv.sem, 1))
            sec("vector", f_vsq)
            sv.v += 1
            LA["vsq_sv"][i] = sv.v

        def ln_stats_p(i):
            qw, svw = LA["ld_qs"][i], LA["vsq_sv"][i]

            def f_tens(eng, qw=qw, svw=svw):
                eng.wait_ge(qs.sem, qw)
                eng.wait_ge(sv.sem, svw)
                for ci, (off, cw) in enumerate(chunks_of(SWW)):
                    po = 0 if ci == 0 else 448
                    for a in range(ND):
                        eng.matmul(out=p_lnA[0:1, po:po + cw],
                                   lhsT=s_onescol_f32[:, :],
                                   rhs=x_blk[:, a * SWW + off:a * SWW + off + cw],
                                   start=(a == 0), stop=(a == ND - 1))
                    for a in range(ND):
                        eng.matmul(out=p_lnB[0:1, po:po + cw],
                                   lhsT=s_onescol_bf[:, :],
                                   rhs=x2bf[:, a * SWW + off:a * SWW + off + cw],
                                   start=(a == 0), stop=(a == ND - 1))
                eng.maybe_drain_then_inc((st.sem, 1))
            sec("tensor", f_tens)
            st.v += 1
            LA["stats_st"][i] = st.v

        def ln_math(i):
            stw = LA["stats_st"][i]

            def f_vec(eng, stw=stw):
                eng.wait_ge(st.sem, stw)
                for ci, (off, cw) in enumerate(chunks_of(SWW)):
                    po = 0 if ci == 0 else 448
                    eng.tensor_copy(out=LN["m"][0:1, off:off + cw],
                                    in_=p_lnA[0:1, po:po + cw])
                    eng.tensor_copy(out=LN["v"][0:1, off:off + cw],
                                    in_=p_lnB[0:1, po:po + cw])
                eng.drain()
                eng.tensor_scalar_mul(out=LN["m"][0:1, 0:SWW],
                                      in0=LN["m"][0:1, 0:SWW], scalar1=1.0 / D)
                eng.drain()
                eng.tensor_mul(out=s_lnscr[0:1, 0:SWW], in0=LN["m"][0:1, 0:SWW],
                               in1=LN["m"][0:1, 0:SWW])
                eng.drain()
                eng.scalar_tensor_tensor(out=LN["v"][0:1, 0:SWW],
                                         in0=LN["v"][0:1, 0:SWW],
                                         scalar=1.0 / D,
                                         in1=s_lnscr[0:1, 0:SWW],
                                         op0=ALU.mult, op1=ALU.subtract)
                eng.drain()
                eng.tensor_scalar_add(out=LN["v"][0:1, 0:SWW],
                                      in0=LN["v"][0:1, 0:SWW], scalar1=EPS)
                eng.maybe_drain_then_inc((sv.sem, 1))
            sec("vector", f_vec)
            sv.v += 1
            svw = sv.v

            def f_scal2(eng, svw=svw):
                eng.wait_ge(sv.sem, svw)
                eng.activation(out=s_lnrs[0:1, 0:SWW], in_=LN["v"][0:1, 0:SWW],
                               func=AF.Ln)
                eng.drain()
                eng.activation(out=s_lnrs[0:1, 0:SWW], in_=s_lnrs[0:1, 0:SWW],
                               func=AF.Exp, scale=-0.5)
                eng.maybe_drain_then_inc((ss.sem, 1))
            sec("scalar", f_scal2)
            ss.v += 1
            ssw = ss.v

            def f_vec2(eng, ssw=ssw):
                eng.wait_ge(ss.sem, ssw)
                eng.tensor_scalar_mul(out=s_lnscr[0:1, 0:SWW],
                                      in0=LN["m"][0:1, 0:SWW], scalar1=-1.0)
                eng.drain()
                eng.tensor_copy(out=LN["negm_bf"][0:1, 0:SWW],
                                in_=s_lnscr[0:1, 0:SWW])
                eng.tensor_copy(out=LN["rs_bf"][0:1, 0:SWW],
                                in_=s_lnrs[0:1, 0:SWW])
                eng.maybe_drain_then_inc((sv.sem, 1))
            sec("vector", f_vec2)
            sv.v += 1
            LA["vec2_sv"][i] = sv.v

        def ln_bcast(i):
            svw = LA["vec2_sv"][i]

            def f_tens2(eng, svw=svw):
                eng.wait_ge(sv.sem, svw)
                for ci, (off, cw) in enumerate(chunks_of(SWW)):
                    po = 0 if ci == 0 else 448
                    eng.matmul(out=p_lnA[:, po:po + cw],
                               lhsT=s_onesrow_bf[:, :],
                               rhs=LN["negm_bf"][0:1, off:off + cw],
                               start=True, stop=True)
                    eng.matmul(out=p_lnB[:, po:po + cw],
                               lhsT=s_onesrow_bf[:, :],
                               rhs=LN["rs_bf"][0:1, off:off + cw],
                               start=True, stop=True)
                eng.maybe_drain_then_inc((st.sem, 1))
            sec("tensor", f_tens2)
            st.v += 1
            LA["bcast_st"][i] = st.v

        def ln_vec3(i):
            stw = LA["bcast_st"][i]
            sgw = LA["cadd_sg"].get(i - 1, 0)

            def f_vec3(eng, stw=stw, sgw=sgw):
                eng.wait_ge(st.sem, stw)
                eng.wait_ge(sg.sem, sgw)
                for ci, (off, cw) in enumerate(chunks_of(SWW)):
                    po = 0 if ci == 0 else 448
                    eng.tensor_copy(out=LN["mb"][:, off:off + cw],
                                    in_=p_lnA[:, po:po + cw])
                    eng.tensor_copy(out=LN["rsb"][:, off:off + cw],
                                    in_=p_lnB[:, po:po + cw])
                eng.maybe_drain_then_inc((sv.sem, 1))
            sec("vector", f_vec3)
            sv.v += 1
            LA["vec3_sv"][i] = sv.v

        def ln_cadd(i):
            svw = LA["vec3_sv"][i]

            def f_gp(eng, svw=svw):
                eng.wait_ge(sv.sem, svw)
                for a in range(ND):
                    eng.tensor_add(out=x_blk[:, a * SWW:(a + 1) * SWW],
                                   in0=x_blk[:, a * SWW:(a + 1) * SWW],
                                   in1=LN["mb"][:, 0:SWW])
                eng.maybe_drain_then_inc((sg.sem, 1))
            sec("gpsimd", f_gp)
            sg.v += 1
            LA["cadd_sg"][i] = sg.v

        def ln_vec4(i):
            sgw = LA["cadd_sg"][i]
            stw = LA["inproj_st"].get(i - 1, 0)
            qw = LA["lnsp_qs"].get(i - 1, 0)

            def f_vec4(eng, sgw=sgw, stw=stw, qw=qw):
                eng.wait_ge(sg.sem, sgw)
                eng.wait_ge(st.sem, stw)
                eng.wait_ge(qs.sem, qw)
                for a in range(ND):
                    eng.tensor_mul(out=ln_blk[:, a * SWW:(a + 1) * SWW],
                                   in0=x_blk[:, a * SWW:(a + 1) * SWW],
                                   in1=LN["rsb"][:, 0:SWW])
                eng.maybe_drain_then_inc((sv.sem, 1))
            sec("vector", f_vec4)
            sv.v += 1
            LA["vec4_sv"][i] = sv.v

        def lnspill(i):
            svw = LA["vec4_sv"][i]

            def f_sync2(eng, c0=i * TB, svw=svw):
                eng.wait_ge(sv.sem, svw)
                eng.dma_start(out=bass.AP(ln1_sp, c0, [[CT + 4, 128],
                                                       [128 * (CT + 4), ND],
                                                       [1, SWW]]),
                              in_=ln_blk[:, :]).then_inc(qs.sem, 16)
                qs.v += 16
            sec("sync", f_sync2)
            LA["lnsp_qs"][i] = qs.v

        def inproj(i):
            with PT("p_xi0", [128, 512], F32) as b0_, \
                 PT("p_xi1", [128, 512], F32) as b1_, \
                 PT("p_xi2", [128, 512], F32) as b2_, \
                 PT("p_xi3", [128, 512], F32) as b3_, \
                 PT("p_xih", [128, 64], F32) as p_xih:
                pb = [b0_, b1_, b2_, b3_]

                def mms(eng, et, i_):
                    for (off, cw) in nchm:
                        for a in range(ND):
                            eng.matmul(out=pb[i_][:, off:off + cw],
                                       lhsT=w_inx_bf[:, a * E + et * 128:a * E + (et + 1) * 128],
                                       rhs=ln_blk[:, a * SWW + 4 + off:a * SWW + 4 + off + cw],
                                       start=(a == 0), stop=(a == ND - 1))
                    for a in range(ND):
                        eng.matmul(out=p_xih[:, i_ * 4:i_ * 4 + 4],
                                   lhsT=w_inx_bf[:, a * E + et * 128:a * E + (et + 1) * 128],
                                   rhs=ln_blk[:, a * SWW:a * SWW + 4],
                                   start=(a == 0), stop=(a == ND - 1))

                def ev(eng, et, i_):
                    eng.activation(out=xi_blk[:, et * SWW:et * SWW + 4],
                                   in_=p_xih[:, i_ * 4:i_ * 4 + 4], func=AF.Copy)
                    eng.activation(out=xi_blk[:, et * SWW + 4:et * SWW + 4 + TB],
                                   in_=pb[i_][:, 0:TB], func=AF.Copy)
                mm_rounds(NE, 4, mms, "scalar", ev,
                          pre_wait=((sv, sv.v),))
            LA["inproj_st"][i] = st.v
            LA["inev_ss"][i] = ss.v

        def conv(i):
            with PT("p_xc0", [128, 512], F32) as q0_, \
                 PT("p_xc1", [128, 512], F32) as q1_, \
                 PT("p_xc2", [128, 512], F32) as q2_, \
                 PT("p_xc3", [128, 512], F32) as q3_:
                qb = [q0_, q1_, q2_, q3_]

                def mms(eng, et, i_):
                    for k in range(4):
                        eng.matmul(out=qb[i_][:, 0:TB],
                                   lhsT=w_cd_bf[:, et * 512 + k * 128:et * 512 + (k + 1) * 128],
                                   rhs=xi_blk[:, et * SWW + 1 + k:et * SWW + 1 + k + TB],
                                   start=(k == 0), stop=(k == 3))

                def ev(eng, et, i_):
                    eng.activation(out=xc_blk[:, et * TB:(et + 1) * TB],
                                   in_=qb[i_][:, 0:TB], func=AF.Silu,
                                   bias=s_cbcol[:, et:et + 1])
                cpw = ((qs, LA["xcsp_qs"].get(i - 1, 0)),)
                mm_rounds(NE, 4, mms, "scalar", ev,
                          pre_wait=((ss, LA["inev_ss"][i]),),
                          cons_pre_wait=cpw)
            LA["silu_ss"][i] = ss.v

        def xconvspill(i):
            ssw = LA["silu_ss"][i]

            def f_sync3(eng, c0=i * TB, ssw=ssw):
                eng.wait_ge(ss.sem, ssw)
                eng.dma_start(out=bass.AP(xconv_sp, c0, [[CT, 128],
                                                         [128 * CT, NE],
                                                         [1, TB]]),
                              in_=xc_blk[:, :]).then_inc(qs.sem, 16)
                qs.v += 16
            sec("sync", f_sync3)
            LA["xcsp_qs"][i] = qs.v

        def xproj(i):
            ssw = LA["silu_ss"][i]
            qw = LA["xdsp_qs"].get(i - 1, 0)
            sgw = LA["bd_sg"].get(i - 1, 0)
            spss = LA["sp_ss"].get(i - 1, 0)
            spsv = LA["sp_sv"].get(i - 1, 0)
            with PT("p_xd", [NSS, 512], F32) as p_xd:
                def f_tens(eng, ssw=ssw):
                    eng.wait_ge(ss.sem, ssw)
                    for (off, cw) in nchm:
                        for et in range(NE):
                            eng.matmul(out=p_xd[:, off:off + cw],
                                       lhsT=w_xp_bf[:, et * 33:(et + 1) * 33],
                                       rhs=xc_blk[:, et * TB + off:et * TB + off + cw],
                                       start=(et == 0), stop=(et == NE - 1))
                    eng.maybe_drain_then_inc((st.sem, 1))
                sec("tensor", f_tens)
                st.v += 1

                def f_vec(eng, qw=qw, sgw=sgw, spss=spss, spsv=spsv):
                    eng.wait_ge(st.sem, st.v)
                    eng.wait_ge(qs.sem, qw)
                    eng.wait_ge(sg.sem, sgw)
                    eng.wait_ge(ss.sem, spss)
                    eng.wait_ge(sv.sem, spsv)
                    eng.tensor_copy(out=s_xdt[:, :], in_=p_xd[:, 0:TB])
                    eng.maybe_drain_then_inc((sv.sem, 1))
                sec("vector", f_vec)
                sv.v += 1

                def f_syncx(eng, c0=i * TB):
                    eng.wait_ge(sv.sem, sv.v)
                    eng.dma_start(out=bass.AP(xdbl_sp, c0, [[CT, NSS],
                                                            [1, TB]]),
                                  in_=s_xdt[:, :]).then_inc(qs.sem, 16)
                    qs.v += 16
                sec("sync", f_syncx)
            LA["xdsp_qs"][i] = qs.v

        ld(0)
        vsq(0)
        ln_stats_p(0)
        ln_math(0)
        ln_bcast(0)
        ln_vec3(0)
        ln_cadd(0)
        ln_vec4(0)
        for i in range(NB):
            if i + 1 < NB:
                ld(i + 1)
                vsq(i + 1)
            inproj(i)
            lnspill(i)
            if i + 1 < NB:
                ln_stats_p(i + 1)
                ln_math(i + 1)
            conv(i)
            if i + 1 < NB:
                ln_bcast(i + 1)
                ln_vec3(i + 1)
                ln_cadd(i + 1)
                ln_vec4(i + 1)
            xconvspill(i)
            xproj(i)
            if sub >= 4:
                emit_Bd_block(i * TB, TB, with_C=False, sbuf_pre=sv.v)
                LA["bd_sg"][i] = sg.v
                LA["sp_ss"][i] = ss.v
                LA["sp_sv"][i] = sv.v
                emit_Sbatch(with_ct=False, accumulate_S=True)
                for kc in range(NCH):
                    ew = (((ss, ss.v), (sv, sv.v)) if kc == 0 else ())
                    emit_xcT(kc, xc_blk, p_tr1a, evac="scalar",
                             extra_waits=ew)
                    emit_h_update(kc, s_xcT, s_hloc, s_hloc)
        barrier()
        ph.close()

    def emit_collective():
        cps = contextlib.ExitStack()
        s_gj = cps.enter_context(nc.sbuf_tensor("s_gj", [NS, 2 * E], F32))

        def f_vz(eng):
            eng.memset(s_gj[0:1, :], 0.0)
            eng.maybe_drain_then_inc((sv.sem, 1))
        sec("vector", f_vz)
        sv.v += 1

        def f_gp(eng):
            eng.wait_ge(sv.sem, sv.v)
            eng.dma_start(out=cc_in[0:NS, :], in_=s_hloc[:, :]).then_inc(qg.sem, 16)
            qg.v += 16
            eng.dma_start(out=cc_in[NS:NS + 1, :],
                          in_=s_gj[0:1, 0:E]).then_inc(qg.sem, 16)
            qg.v += 16
            eng.dma_start(out=cc_in[NS:NS + 1, 0:1],
                          in_=s_accS[:, :]).then_inc(qg.sem, 16)
            qg.v += 16
            eng.wait_ge(qg.sem, qg.v)
            eng.collective_compute(
                "AllGather", ALU.bypass,
                replica_groups=[list(range(n_cores))],
                ins=[cc_in.ap().opt()],
                outs=[cc_out.ap().opt()],
            ).then_inc(sc.sem, 1)
            sc.v += 1
            eng.wait_ge(sc.sem, sc.v)
            eng.dma_start(out=s_S8[:, :],
                          in_=bass.AP(cc_out, NS * E,
                                      [[1, 1], [(NS + 1) * E, n_cores], [1, 1]]),
                          allow_slow_non_contiguous=True
                          ).then_inc(qg.sem, 16)
            qg.v += 16
            eng.wait_ge(qg.sem, qg.v)
            eng.maybe_drain_then_inc((sg.sem, 1))
        sec("gpsimd", f_gp)
        sg.v += 1
        barrier()

        with PT("p_s8c", [n_cores, 1], F32) as p_s8c, \
             PT("p_gs", [n_cores, 1], F32) as p_gs, \
             PT("p_gr", [1, n_cores], F32) as p_gr:
            def f_tens(eng):
                eng.transpose(out=p_s8c[:, :], in_=s_S8[:, :],
                              identity=s_id_f32[0:1, 0:1])
                eng.maybe_drain_then_inc((st.sem, 1))
            sec("tensor", f_tens)
            st.v += 1

            def f_vec1(eng):
                eng.wait_ge(st.sem, st.v)
                eng.tensor_copy(out=s_c8[:, :], in_=p_s8c[:, :])
                eng.maybe_drain_then_inc((sv.sem, 1))
            sec("vector", f_vec1)
            sv.v += 1

            def f_tens2(eng):
                eng.wait_ge(sv.sem, sv.v)
                eng.matmul(out=p_gs[:, :], lhsT=s_predT[:, :], rhs=s_c8[:, :],
                           start=True, stop=True)
                eng.maybe_drain_then_inc((st.sem, 1))
            sec("tensor", f_tens2)
            st.v += 1

            def f_vec2(eng):
                eng.wait_ge(st.sem, st.v)
                eng.tensor_copy(out=s_gs[:, :], in_=p_gs[:, :])
                eng.maybe_drain_then_inc((sv.sem, 1))
            sec("vector", f_vec2)
            sv.v += 1

            def f_tens3(eng):
                eng.wait_ge(sv.sem, sv.v)
                eng.transpose(out=p_gr[:, :], in_=s_gs[:, :],
                              identity=s_id_f32[0:n_cores, 0:n_cores])
                eng.maybe_drain_then_inc((st.sem, 1))
            sec("tensor", f_tens3)
            st.v += 1

            def f_vec3(eng):
                eng.wait_ge(st.sem, st.v)
                eng.tensor_copy(out=s_gr[:, :], in_=p_gr[:, :])
                eng.memset(s_hcur[:, :], 0.0)
                eng.maybe_drain_then_inc((sv.sem, 1))
            sec("vector", f_vec3)
            sv.v += 1
        barrier()

        with PT("p_w16", [NS, n_cores], F32) as p_w16, \
             PT("p_mk16", [NS, n_cores], F32) as p_mk16:
            def f_tensw(eng):
                eng.matmul(out=p_w16[:, :], lhsT=s_arow[:, :], rhs=s_gr[:, :],
                           start=True, stop=True)
                eng.matmul(out=p_mk16[:, :], lhsT=s_ones[0:1, 0:NS],
                           rhs=s_maskrow[:, :], start=True, stop=True)
                eng.maybe_drain_then_inc((st.sem, 1))
            sec("tensor", f_tensw)
            st.v += 1

            def f_scalw(eng):
                eng.wait_ge(st.sem, st.v)
                eng.activation(out=s_W16[:, :], in_=p_w16[:, :], func=AF.Exp)
                eng.maybe_drain_then_inc((ss.sem, 1))
            sec("scalar", f_scalw)
            ss.v += 1

            def f_vecw(eng):
                eng.wait_ge(ss.sem, ss.v)
                eng.tensor_mul(out=s_W16[:, :], in0=s_W16[:, :],
                               in1=p_mk16[:, :])
                eng.maybe_drain_then_inc((sv.sem, 1))
            sec("vector", f_vecw)
            sv.v += 1
        barrier()

        stt_sv = {}
        g0 = sv.v
        for j in range(n_cores):
            svw = stt_sv.get(j - 2, g0)

            def f_gpj(eng, j=j, svw=svw):
                eng.wait_ge(sv.sem, svw)
                eng.dma_start(out=s_gj[:, (j % 2) * E:(j % 2) * E + E],
                              in_=cc_out[j * (NS + 1):j * (NS + 1) + NS, :]
                              ).then_inc(qg.sem, 16)
                qg.v += 16
                eng.wait_ge(qg.sem, qg.v)
                eng.maybe_drain_then_inc((sg.sem, 1))
            sec("gpsimd", f_gpj)
            sg.v += 1
            sgw = sg.v

            def f_vecj2(eng, j=j, sgw=sgw):
                eng.wait_ge(sg.sem, sgw)
                eng.scalar_tensor_tensor(
                    out=s_hcur[:, :],
                    in0=s_gj[:, (j % 2) * E:(j % 2) * E + E],
                    scalar=s_W16[:, j:j + 1], in1=s_hcur[:, :],
                    op0=ALU.mult, op1=ALU.add)
                eng.drain()
                eng.maybe_drain_then_inc((sv.sem, 1))
            sec("vector", f_vecj2)
            sv.v += 1
            stt_sv[j] = sv.v

        def f_vec4(eng):
            eng.tensor_copy(out=s_hcur_bf[:, :], in_=s_hcur[:, :])
            eng.maybe_drain_then_inc((sv.sem, 1))
        sec("vector", f_vec4)
        sv.v += 1
        barrier()
        if DBG:
            def f_dbg(eng):
                eng.dma_start(out=d_hin[:, :], in_=s_hcur[:, :]).then_inc(qs.sem, 16)
                qs.v += 16
                eng.dma_start(out=d_hloc[:, :], in_=s_hloc[:, :]).then_inc(qs.sem, 16)
                qs.v += 16
                eng.dma_start(out=d_xdbl[:, :],
                              in_=bass.AP(xdbl_sp, 0, [[CT, NSS], [1, CT]])
                              ).then_inc(qs.sem, 16)
                qs.v += 16
                eng.dma_start(out=d_rows[0:1, 0:1], in_=s_accS[:, :]).then_inc(qs.sem, 16)
                qs.v += 16
                eng.dma_start(out=d_rows[0:1, 1:1 + n_cores], in_=s_S8[:, :]).then_inc(qs.sem, 16)
                qs.v += 16
                eng.dma_start(out=d_rows[0:1, 9:9 + n_cores], in_=s_gr[:, :]).then_inc(qs.sem, 16)
                qs.v += 16
                eng.wait_ge(qs.sem, qs.v)
            sec("sync", f_dbg)
            barrier()
        cps.close()

    def emit_phase1b():
        ph = contextlib.ExitStack()

        def P(name, shape, dt):
            return ph.enter_context(nc.sbuf_tensor(name, shape, dt))

        w_inz_bf = P("w_inz_bf", [128, ND * E], BF16)
        w_out_bf = P("w_out_bf", [128, NE * D], BF16)
        with nc.sbuf_tensor("w_stage1b", [128, 2048], F32) as w_stage:
            load_weight_bf(w_inz_bf, w_inz, D, E, w_stage)
            load_weight_bf(w_out_bf, w_out, E, D, w_stage)
        barrier()

        x_blk = P("x_blk1b", [128, ND * TB], F32)
        p_tr1b = ph.enter_context(PT("p_tr1b", [128, 1024], BF16))
        p_dst1b = ph.enter_context(PT("p_dst1b", [128, 128], F32))
        p_bcA = ph.enter_context(PT("p_bcA", [128, 512], F32))
        p_bcB = ph.enter_context(PT("p_bcB", [128, 512], F32))
        xc_blk = P("xc_blk1b", [128, NE * TB], BF16)
        z_bf = P("z_bf", [128, NE * TB], BF16)
        y_g = P("y_g", [128, NE * TB], BF16)
        s_E = P("s_E", [128, NS * 128], F32)
        s_bdflat = P("s_bdflat", [1, NS * 128], BF16)
        s_cflat = P("s_cflat", [1, NS * 128], BF16)

        with PT("p_zH", [128, 64], F32) as p_zH:
            def mms(eng, et, i):
                for a in range(ND):
                    eng.matmul(out=p_zH[:, i:i + 1],
                               lhsT=w_inz_bf[:, a * E + et * 128:a * E + (et + 1) * 128],
                               rhs=s_ln1H_bf[:, a * 4 + 3:a * 4 + 4],
                               start=(a == 0), stop=(a == ND - 1))

            def ev(eng, et, i):
                eng.activation(out=s_zH[:, et:et + 1], in_=p_zH[:, i:i + 1],
                               func=AF.Silu)
            mm_rounds(NE, 4, mms, "scalar", ev)
        barrier()

        def f_gp(eng):
            eng.dma_start(out=s_c16h[:, :],
                          in_=s_xdblH[NS + 1:NSS, 0:1]).then_inc(qg.sem, 16)
            qg.v += 16
            eng.wait_ge(qg.sem, qg.v)
            eng.maybe_drain_then_inc((sg.sem, 1))
        sec("gpsimd", f_gp)
        sg.v += 1

        def f_vec(eng):
            eng.wait_ge(sg.sem, sg.v)
            eng.tensor_copy(out=s_c16h_bf[:, :], in_=s_c16h[:, :])
            eng.maybe_drain_then_inc((sv.sem, 1))
        sec("vector", f_vec)
        sv.v += 1
        barrier()

        with PT("p_yH", [128, NE], F32) as p_yH:
            def f_tens(eng):
                for et in range(NE):
                    eng.matmul(out=p_yH[:, et:et + 1],
                               lhsT=s_hcur_bf[:, et * 128:(et + 1) * 128],
                               rhs=s_c16h_bf[:, :], start=True, stop=True)
                eng.maybe_drain_then_inc((st.sem, 1))
            sec("tensor", f_tens)
            st.v += 1

            def f_vec2(eng):
                eng.wait_ge(st.sem, st.v)
                eng.tensor_copy(out=s_ygt[:, :], in_=s_xcH_bf[:, :])
                eng.drain()
                eng.tensor_mul(out=s_ygt[:, :], in0=s_ygt[:, :],
                               in1=s_dcol[:, :])
                eng.drain()
                eng.tensor_add(out=s_ygt[:, :], in0=s_ygt[:, :],
                               in1=p_yH[:, :])
                eng.drain()
                eng.tensor_mul(out=s_ygH_bf[:, 0:NE], in0=s_ygt[:, :],
                               in1=s_zH[:, 0:NE])
                eng.maybe_drain_then_inc((sv.sem, 1))
            sec("vector", f_vec2)
            sv.v += 1
        barrier()

        with PT("p_moH", [128, ND], F32) as p_moH:
            def f_tens(eng):
                for a in range(ND):
                    for et in range(NE):
                        eng.matmul(out=p_moH[:, a:a + 1],
                                   lhsT=w_out_bf[:, et * D + a * 128:et * D + (a + 1) * 128],
                                   rhs=s_ygH_bf[:, et:et + 1],
                                   start=(et == 0), stop=(et == NE - 1))
                eng.maybe_drain_then_inc((st.sem, 1))
            sec("tensor", f_tens)
            st.v += 1

            def f_sync(eng):
                eng.dma_start(out=s_xmidH2[:, :],
                              in_=bass.AP(x_in, 3, [[CT + 4, 128],
                                                    [128 * (CT + 4), ND], [1, 1]]),
                              allow_slow_non_contiguous=True
                              ).then_inc(qs.sem, 16)
                qs.v += 16
                eng.wait_ge(qs.sem, qs.v)
            sec("sync", f_sync)

            def f_vec3(eng):
                eng.wait_ge(st.sem, st.v)
                eng.wait_ge(qs.sem, qs.v)
                eng.tensor_add(out=s_xmidH[:, :], in0=s_xmidH2[:, :],
                               in1=p_moH[:, :])
                eng.maybe_drain_then_inc((sv.sem, 1))
            sec("vector", f_vec3)
            sv.v += 1
        barrier()

        with PT("p_h1", [1, ND], F32) as p_h1, \
             PT("p_h2", [1, ND], F32) as p_h2, \
             PT("p_h3", [128, 3], F32) as p_h3:
            def f_scal(eng):
                eng.activation(out=s_xmidH_bf[:, :], in_=s_xmidH[:, :],
                               func=AF.Copy)
                eng.maybe_drain_then_inc((ss.sem, 1))
            sec("scalar", f_scal)
            ss.v += 1

            def f_vecb(eng):
                eng.tensor_mul(out=s_ygH_bf[:, 0:ND], in0=s_xmidH[:, :],
                               in1=s_xmidH[:, :])
                eng.maybe_drain_then_inc((sv.sem, 1))
            sec("vector", f_vecb)
            sv.v += 1

            def f_tens(eng):
                eng.wait_ge(sv.sem, sv.v)
                eng.wait_ge(ss.sem, ss.v)
                eng.matmul(out=p_h1[:, :], lhsT=s_onescol_bf[:, :],
                           rhs=s_xmidH_bf[:, :], start=True, stop=True)
                eng.matmul(out=p_h2[:, :], lhsT=s_onescol_bf[:, :],
                           rhs=s_ygH_bf[:, 0:ND], start=True, stop=True)
                eng.maybe_drain_then_inc((st.sem, 1))
            sec("tensor", f_tens)
            st.v += 1

            def f_vec2(eng):
                eng.wait_ge(st.sem, st.v)
                eng.tensor_reduce(out=s_h1r[0:1, 0:1], in_=p_h1[:, :],
                                  axis=AX.X, op=ALU.add)
                eng.tensor_reduce(out=s_h1r[0:1, 1:2], in_=p_h2[:, :],
                                  axis=AX.X, op=ALU.add)
                eng.drain()
                eng.tensor_scalar_mul(out=s_h1r[0:1, 0:2], in0=s_h1r[0:1, 0:2],
                                      scalar1=1.0 / D)
                eng.drain()
                eng.tensor_mul(out=s_scr[0:1, 0:1], in0=s_h1r[0:1, 0:1],
                               in1=s_h1r[0:1, 0:1])
                eng.drain()
                eng.tensor_sub(out=s_h1r[0:1, 1:2], in0=s_h1r[0:1, 1:2],
                               in1=s_scr[0:1, 0:1])
                eng.drain()
                eng.tensor_scalar_add(out=s_h1r[0:1, 1:2], in0=s_h1r[0:1, 1:2],
                                      scalar1=EPS)
                eng.maybe_drain_then_inc((sv.sem, 1))
            sec("vector", f_vec2)
            sv.v += 1

            def f_scal2(eng):
                eng.wait_ge(sv.sem, sv.v)
                eng.activation(out=s_h1r[0:1, 1:2], in_=s_h1r[0:1, 1:2],
                               func=AF.Ln)
                eng.drain()
                eng.activation(out=s_h1r[0:1, 1:2], in_=s_h1r[0:1, 1:2],
                               func=AF.Exp, scale=-0.5)
                eng.maybe_drain_then_inc((ss.sem, 1))
            sec("scalar", f_scal2)
            ss.v += 1

            def f_vec3(eng):
                eng.wait_ge(ss.sem, ss.v)
                eng.tensor_copy(out=s_h1r[0:1, 2:3], in_=s_hscale[:, :])
                eng.maybe_drain_then_inc((sv.sem, 1))
            sec("vector", f_vec3)
            sv.v += 1

            def f_tensb(eng):
                eng.wait_ge(sv.sem, sv.v)
                eng.matmul(out=p_h3[:, :], lhsT=s_ones[:, :],
                           rhs=s_h1r[0:1, 0:3], start=True, stop=True)
                eng.maybe_drain_then_inc((st.sem, 1))
            sec("tensor", f_tensb)
            st.v += 1

            def f_vecc3(eng):
                eng.wait_ge(st.sem, st.v)
                eng.tensor_copy(out=s_h1b[:, :], in_=p_h3[:, :])
                eng.drain()
                eng.tensor_scalar(out=s_ln2H[:, :], in0=s_xmidH[:, :],
                                  scalar1=s_h1b[:, 0:1], scalar2=s_h1b[:, 1:2],
                                  op0=ALU.subtract, op1=ALU.mult)
                eng.drain()
                eng.tensor_scalar_mul(out=s_ln2H[:, :], in0=s_ln2H[:, :],
                                      scalar1=s_h1b[:, 2:3])
                eng.maybe_drain_then_inc((sv.sem, 1))
            sec("vector", f_vecc3)
            sv.v += 1
        barrier()

        nchm = chunks_of(TB)
        for b in range(NB):
            c0 = b * TB
            pg_sv, pg_ss, pg_st, pg_qs = sv.v, ss.v, st.v, qs.v

            def f_sync(eng, c0=c0, pg_sv=pg_sv, pg_st=pg_st, pg_qs=pg_qs):
                eng.wait_ge(sv.sem, pg_sv)
                eng.wait_ge(st.sem, pg_st)
                eng.wait_ge(qs.sem, pg_qs)
                eng.dma_start(out=x_blk[:, :],
                              in_=bass.AP(x_in, c0 + 4, [[CT + 4, 128],
                                                         [128 * (CT + 4), ND], [1, TB]])
                              ).then_inc(qs.sem, 16)
                qs.v += 16
                eng.dma_start(out=xc_blk[:, :],
                              in_=bass.AP(xconv_sp, c0, [[CT, 128],
                                                         [128 * CT, NE], [1, TB]])
                              ).then_inc(qs.sem, 16)
                qs.v += 16
                eng.dma_start(out=y_g[:, 0:ND * SWW],
                              in_=bass.AP(ln1_sp, c0,
                                          [[CT + 4, 128],
                                           [128 * (CT + 4), ND], [1, SWW]])
                              ).then_inc(qs.sem, 16)
                qs.v += 16
            sec("sync", f_sync)
            ldq = qs.v

            if True:
                ln_blk = y_g

                with PT("p_z0", [128, 512], F32) as z0, \
                     PT("p_z1", [128, 512], F32) as z1, \
                     PT("p_z2", [128, 512], F32) as z2, \
                     PT("p_z3", [128, 512], F32) as z3:
                    zb = [z0, z1, z2, z3]

                    def mms(eng, et, i):
                        for (off, cw) in nchm:
                            for a in range(ND):
                                eng.matmul(out=zb[i][:, off:off + cw],
                                           lhsT=w_inz_bf[:, a * E + et * 128:a * E + (et + 1) * 128],
                                           rhs=ln_blk[:, a * SWW + 4 + off:a * SWW + 4 + off + cw],
                                           start=(a == 0), stop=(a == ND - 1))

                    def ev(eng, et, i):
                        eng.activation(out=z_bf[:, et * TB:(et + 1) * TB],
                                       in_=zb[i][:, 0:TB], func=AF.Silu)
                    mm_rounds(NE, 4, mms, "scalar", ev,
                              pre_wait=((qs, ldq), (sv, pg_sv), (ss, pg_ss)),
                              cons_pre_wait=((sv, pg_sv),))

            emit_Bd_block(c0, TB, with_C=True)
            bd_sv = sv.v

            emit_Sbatch(with_ct=True, accumulate_S=False)
            KS = dict(flats_sg={}, xcT_ss={}, dst_st={}, scale_sv={},
                      exp_ss={}, bcp0_st={}, mulp0_sv={}, bcp1_st={},
                      mulp1_sv={}, y_st={}, evy_ss={}, gat_sg={},
                      h_st={}, hcomb_sv={})

            def st_flats(j):
                stw = KS["bcp1_st"].get(j - 1, 0)
                svw = bd_sv if j == 0 else 0

                def f_gp2(eng, j=j, stw=stw, svw=svw):
                    eng.wait_ge(st.sem, stw)
                    eng.wait_ge(sv.sem, svw)
                    eng.dma_start(out=bass.AP(s_bdflat, 0,
                                              [[1, 1], [128, NS], [1, 128]]),
                                  in_=s_Bd16_bf[:, j * 128:(j + 1) * 128]
                                  ).then_inc(qg.sem, 16)
                    qg.v += 16
                    eng.dma_start(out=bass.AP(s_cflat, 0,
                                              [[1, 1], [128, NS], [1, 128]]),
                                  in_=s_C16_bf[:, j * 128:(j + 1) * 128]
                                  ).then_inc(qg.sem, 16)
                    qg.v += 16
                    eng.wait_ge(qg.sem, qg.v)
                    eng.maybe_drain_then_inc((sg.sem, 1))
                sec("gpsimd", f_gp2)
                sg.v += 1
                KS["flats_sg"][j] = sg.v

            def st_T(j):
                ew = ((qs, ldq),) if j == 0 else ()
                emit_xcT(j, xc_blk, p_tr1b, dst_off=(j % 2) * E,
                         evac="scalar", extra_waits=ew)
                KS["xcT_ss"][j] = ss.v

            def st_dst(j):
                svw = KS["scale_sv"].get(j - 1, 0)

                def f_td(eng, j=j, svw=svw):
                    eng.wait_ge(sv.sem, svw)
                    eng.matmul(out=p_dst1b[:, :],
                               lhsT=s_lhs4[:, j * 128:(j + 1) * 128],
                               rhs=s_rhs4[:, j * 128:(j + 1) * 128],
                               start=True, stop=True)
                    eng.maybe_drain_then_inc((st.sem, 1))
                sec("tensor", f_td)
                st.v += 1
                KS["dst_st"][j] = st.v

            def st_scale(j):
                stw = KS["dst_st"][j]

                def f_vp(eng, stw=stw):
                    eng.wait_ge(st.sem, stw)
                    eng.tensor_add(out=s_dSTm[:, :], in0=p_dst1b[:, :],
                                   in1=s_mask[:, :])
                    eng.drain()
                    for n in range(NS):
                        eng.tensor_scalar_mul(
                            out=s_E[:, n * 128:(n + 1) * 128],
                            in0=s_dSTm[:, :], scalar1=s_Ab[:, n:n + 1])
                    eng.maybe_drain_then_inc((sv.sem, 1))
                sec("vector", f_vp)
                sv.v += 1
                KS["scale_sv"][j] = sv.v

            def st_exp(j):
                svw = KS["scale_sv"][j]

                def f_se(eng, svw=svw):
                    eng.wait_ge(sv.sem, svw)
                    eng.activation(out=s_E[:, :], in_=s_E[:, :], func=AF.Exp)
                    eng.maybe_drain_then_inc((ss.sem, 1))
                sec("scalar", f_se)
                ss.v += 1
                KS["exp_ss"][j] = ss.v

            def st_bc(j, p):
                ssw = KS["exp_ss"][j]
                sgw = KS["flats_sg"][j]
                svw = (KS["mulp1_sv"].get(j - 1, 0) if p == 0
                       else KS["mulp0_sv"][j])

                def f_tbc(eng, p=p, ssw=ssw, sgw=sgw, svw=svw):
                    eng.wait_ge(ss.sem, ssw)
                    eng.wait_ge(sg.sem, sgw)
                    eng.wait_ge(sv.sem, svw)
                    for q in range(8):
                        n = p * 8 + q
                        bank = p_bcA if q < 4 else p_bcB
                        eng.matmul(out=bank[:, (q % 4) * 128:(q % 4) * 128 + 128],
                                   lhsT=s_bdflat[0:1, n * 128:(n + 1) * 128],
                                   rhs=s_cflat[0:1, n * 128:(n + 1) * 128],
                                   start=True, stop=True)
                    eng.maybe_drain_then_inc((st.sem, 1))
                sec("tensor", f_tbc)
                st.v += 1
                KS["bcp%d_st" % p][j] = st.v

            def st_mulp0(j):
                stw = KS["bcp0_st"][j]

                def f_m0(eng, stw=stw):
                    eng.wait_ge(st.sem, stw)
                    eng.tensor_mul(out=s_E[:, 0:512], in0=s_E[:, 0:512],
                                   in1=p_bcA[:, :])
                    eng.tensor_mul(out=s_E[:, 512:1024], in0=s_E[:, 512:1024],
                                   in1=p_bcB[:, :])
                    eng.maybe_drain_then_inc((sv.sem, 1))
                sec("vector", f_m0)
                sv.v += 1
                KS["mulp0_sv"][j] = sv.v

            def st_mulp1K(j):
                stw = KS["bcp1_st"][j]

                def f_m1(eng, j=j, stw=stw):
                    eng.wait_ge(st.sem, stw)
                    eng.tensor_mul(out=s_E[:, 1024:1536], in0=s_E[:, 1024:1536],
                                   in1=p_bcA[:, :])
                    eng.tensor_mul(out=s_E[:, 1536:2048], in0=s_E[:, 1536:2048],
                                   in1=p_bcB[:, :])
                    eng.drain()
                    eng.tensor_reduce(
                        out=s_dSTm[:, :],
                        in_=s_E[:, :].rearrange("p (n t) -> p t n", t=128),
                        axis=AX.X, op=ALU.add)
                    eng.drain()
                    eng.tensor_copy(out=s_K_bf[:, (j % 2) * 128:(j % 2) * 128 + 128],
                                    in_=s_dSTm[:, :])
                    eng.maybe_drain_then_inc((sv.sem, 1))
                sec("vector", f_m1)
                sv.v += 1
                KS["mulp1_sv"][j] = sv.v

            def st_Y(j):
                svw = KS["mulp1_sv"][j]
                ssw = KS["evy_ss"].get(j - 1, KS["xcT_ss"][j])

                def f_tens4(eng, j=j, svw=svw, ssw=ssw):
                    eng.wait_ge(sv.sem, svw)
                    eng.wait_ge(ss.sem, ssw)
                    xo = (j % 2) * E
                    for et in range(NE):
                        pbk = p_Yb[et // 4]
                        sl = (et % 4) * 128
                        eng.matmul(out=pbk[:, sl:sl + 128],
                                   lhsT=s_xcT[:, xo + et * 128:xo + (et + 1) * 128],
                                   rhs=s_K_bf[:, (j % 2) * 128:(j % 2) * 128 + 128],
                                   start=True, stop=False)
                        eng.matmul(out=pbk[:, sl:sl + 128],
                                   lhsT=s_hcur_bf[:, et * 128:(et + 1) * 128],
                                   rhs=s_Ctil4_bf[:, j * 128:(j + 1) * 128],
                                   start=False, stop=True)
                    eng.maybe_drain_then_inc((st.sem, 1))
                sec("tensor", f_tens4)
                st.v += 1
                KS["y_st"][j] = st.v

            def st_evacY(j):
                stw = KS["y_st"][j]

                def f_ey(eng, j=j, stw=stw):
                    eng.wait_ge(st.sem, stw)
                    for et in range(NE):
                        pbk = p_Yb[et // 4]
                        sl = (et % 4) * 128
                        o = et * TB + j * 128
                        eng.activation(out=y_g[:, o:o + 128],
                                       in_=pbk[:, sl:sl + 128], func=AF.Copy)
                    eng.maybe_drain_then_inc((ss.sem, 1))
                sec("scalar", f_ey)
                ss.v += 1
                KS["evy_ss"][j] = ss.v

            def st_gating(j):
                ssw = KS["evy_ss"][j]

                def f_vg(eng, j=j, ssw=ssw):
                    eng.wait_ge(ss.sem, ssw)
                    for et in range(NE):
                        o = et * TB + j * 128
                        eng.scalar_tensor_tensor(
                            out=y_g[:, o:o + 128],
                            in0=xc_blk[:, o:o + 128],
                            scalar=s_dcol[:, et:et + 1],
                            in1=y_g[:, o:o + 128],
                            op0=ALU.mult, op1=ALU.add)
                    eng.drain()
                    for et in range(NE):
                        o = et * TB + j * 128
                        eng.tensor_mul(out=y_g[:, o:o + 128],
                                       in0=y_g[:, o:o + 128],
                                       in1=z_bf[:, o:o + 128])
                    eng.maybe_drain_then_inc((sv.sem, 1))
                sec("vector", f_vg)
                sv.v += 1
                KS["gat_sg"][j] = sv.v

            def st_h(j):
                ssw = KS["evy_ss"][j]

                def f_th(eng, j=j, ssw=ssw):
                    eng.wait_ge(ss.sem, ssw)
                    xo = (j % 2) * E
                    for i4 in range(4):
                        eng.matmul(out=p_Yb[i4][0:NS, :],
                                   lhsT=s_Btil4_bf[:, j * NS:(j + 1) * NS],
                                   rhs=s_xcT[:, xo + i4 * 512:xo + (i4 + 1) * 512],
                                   start=True, stop=True)
                    eng.maybe_drain_then_inc((st.sem, 1))
                sec("tensor", f_th)
                st.v += 1
                KS["h_st"][j] = st.v

            def st_hcomb(j):
                stw = KS["h_st"][j]

                def f_hc(eng, j=j, stw=stw):
                    eng.wait_ge(st.sem, stw)
                    for i4 in range(4):
                        eng.scalar_tensor_tensor(
                            out=s_hcur[:, i4 * 512:(i4 + 1) * 512],
                            in0=s_hcur[:, i4 * 512:(i4 + 1) * 512],
                            scalar=s_dec4[:, j:j + 1], in1=p_Yb[i4][0:NS, :],
                            op0=ALU.mult, op1=ALU.add)
                    eng.drain()
                    eng.tensor_copy(out=s_hcur_bf[:, :], in_=s_hcur[:, :])
                    eng.maybe_drain_then_inc((sv.sem, 1))
                sec("vector", f_hc)
                sv.v += 1
                KS["hcomb_sv"][j] = sv.v

            with PT("p_Y0", [128, 512], F32) as y0, \
                 PT("p_Y1", [128, 512], F32) as y1, \
                 PT("p_Y2", [128, 512], F32) as y2, \
                 PT("p_Y3", [128, 512], F32) as y3:
                p_Yb = [y0, y1, y2, y3]
                st_flats(0)
                st_T(0)
                st_dst(0)
                st_scale(0)
                st_exp(0)
                st_bc(0, 0)
                st_mulp0(0)
                st_bc(0, 1)
                st_mulp1K(0)
                for j in range(NCH):
                    if j + 1 < NCH:
                        st_flats(j + 1)
                        st_T(j + 1)
                        st_dst(j + 1)
                        st_scale(j + 1)
                        st_exp(j + 1)
                    st_Y(j)
                    st_evacY(j)
                    if j + 1 < NCH:
                        st_bc(j + 1, 0)
                    st_h(j)
                    if j + 1 < NCH:
                        st_mulp0(j + 1)
                    st_gating(j)
                    st_hcomb(j)
                    if j + 1 < NCH:
                        st_bc(j + 1, 1)
                        st_mulp1K(j + 1)

            with PT("p_mo0", [128, 512], F32) as m0, \
                 PT("p_mo1", [128, 512], F32) as m1, \
                 PT("p_mo2", [128, 512], F32) as m2, \
                 PT("p_mo3", [128, 512], F32) as m3:
                mbk = [m0, m1, m2, m3]

                def mms(eng, a, i):
                    for (off, cw) in nchm:
                        for et in range(NE):
                            eng.matmul(out=mbk[i][:, off:off + cw],
                                       lhsT=w_out_bf[:, et * D + a * 128:et * D + (a + 1) * 128],
                                       rhs=y_g[:, et * TB + off:et * TB + off + cw],
                                       start=(et == 0), stop=(et == NE - 1))

                def ev(eng, a, i):
                    eng.tensor_add(out=x_blk[:, a * TB:(a + 1) * TB],
                                   in0=x_blk[:, a * TB:(a + 1) * TB],
                                   in1=mbk[i][:, 0:TB])
                mm_rounds(ND, 4, mms, "vector", ev,
                          pre_wait=((sv, sv.v), (sg, sg.v)),
                          cons_pre_wait=((qs, ldq),))

            def f_sync4(eng, c0=c0):
                eng.wait_ge(sv.sem, sv.v)
                eng.dma_start(out=bass.AP(xmid_sp, c0, [[CT, 128],
                                                        [128 * CT, ND], [1, TB]]),
                              in_=x_blk[:, :]).then_inc(qs.sem, 16)
                qs.v += 16
            sec("sync", f_sync4)
            
        ph.close()

    def emit_phase2():
        ph = contextlib.ExitStack()

        def P(name, shape, dt):
            return ph.enter_context(nc.sbuf_tensor(name, shape, dt))

        alloc_ln_rows(ph)
        w_r_bf = P("w_r_bf", [128, ND * D], BF16)
        w_k_bf = P("w_k_bf", [128, ND * D], BF16)
        w_v_bf = P("w_v_bf", [128, ND * D], BF16)
        with nc.sbuf_tensor("w_stage2", [128, 2048], F32) as w_stage:
            load_weight_bf(w_r_bf, w_r, D, D, w_stage)
            load_weight_bf(w_k_bf, w_k, D, D, w_stage)
            load_weight_bf(w_v_bf, w_v, D, D, w_stage)
        barrier()

        TB1 = TB + 1
        xm3 = [P(f"xm{j}", [128, ND * TB], F32) for j in range(3)]
        ln2_b = P("ln2_b", [128, ND * TB1], BF16)
        rsb_bf = P("rsb_bf", [128, TB], BF16)
        xr_s = P("xr_s", [128, ND * TB], BF16)
        xk_s = P("xk_s", [128, ND * TB], BF16)
        t1a = P("t1a", [128, ND * TB], BF16)
        t1v = P("t1v", [128, ND * TB], F32)
        r_f32 = P("r_f32", [128, ND * TB], F32)
        kr_bf = P("kr_bf", [128, ND * TB], BF16)
        x2bf = P("x2bf2", [128, ND * TB], BF16)
        fb = [ph.enter_context(PT(f"p2_f{i}", [128, 512], F32))
              for i in range(3)]
        gb = [ph.enter_context(PT(f"p2_g{i}", [128, 512], F32))
              for i in range(3)]
        pl1 = ph.enter_context(PT("p2_l1", [128, 512], F32))
        pl2 = ph.enter_context(PT("p2_l2", [128, 512], F32))

        nchm = chunks_of(TB)
        LS = dict(qs_load={}, vsq_sv={}, stats_st={}, vec2_sv={},
                  bcast_st={}, vec3c_ss={}, ln2_sg={}, vecmr_sv={},
                  vecmk_sv={}, sig_ss={}, sq_sv={}, evv_sv={},
                  gadd_sg={}, qs_out={}, rmm_st={}, kmm_st={}, vech_sv={})

        def f_vec0(eng):
            for a in range(ND):
                eng.tensor_copy(out=ln2_b[:, a * TB1:a * TB1 + 1],
                                in_=s_ln2H[:, a:a + 1])
            eng.maybe_drain_then_inc((sv.sem, 1))
        sec("vector", f_vec0)
        sv.v += 1

        def load_blk(i):
            A = i % 3
            sgw = LS["gadd_sg"].get(i - 3)

            def f_sync(eng, c0=i * TB, A=A, sgw=sgw):
                if sgw is not None:
                    eng.wait_ge(sg.sem, sgw)
                eng.dma_start(out=xm3[A][:, :],
                              in_=bass.AP(xmid_sp, c0, [[CT, 128],
                                                        [128 * CT, ND],
                                                        [1, TB]])
                              ).then_inc(qs.sem, 16)
                qs.v += 16
            sec("sync", f_sync)
            LS["qs_load"][i] = qs.v

        def vsq(i):
            A = i % 3
            qw = LS["qs_load"][i]

            def f_vsq(eng, A=A, qw=qw):
                eng.wait_ge(qs.sem, qw)
                for a in range(ND):
                    eng.tensor_mul(out=x2bf[:, a * TB:(a + 1) * TB],
                                   in0=xm3[A][:, a * TB:(a + 1) * TB],
                                   in1=xm3[A][:, a * TB:(a + 1) * TB])
                eng.maybe_drain_then_inc((sv.sem, 1))
            sec("vector", f_vsq)
            sv.v += 1
            LS["vsq_sv"][i] = sv.v

        def ln2_s2(i):
            A = i % 3
            qw, svw = LS["qs_load"][i], LS["vsq_sv"][i]
            ssw = LS["vec3c_ss"].get(i - 1, 0)

            def f_tens(eng, A=A, qw=qw, svw=svw, ssw=ssw):
                eng.wait_ge(qs.sem, qw)
                eng.wait_ge(sv.sem, svw)
                eng.wait_ge(ss.sem, ssw)
                for a in range(ND):
                    eng.matmul(out=pl1[0:1, 0:TB], lhsT=s_onescol_f32[:, :],
                               rhs=xm3[A][:, a * TB:(a + 1) * TB],
                               start=(a == 0), stop=(a == ND - 1))
                for a in range(ND):
                    eng.matmul(out=pl2[0:1, 0:TB], lhsT=s_onescol_bf[:, :],
                               rhs=x2bf[:, a * TB:(a + 1) * TB],
                               start=(a == 0), stop=(a == ND - 1))
                eng.maybe_drain_then_inc((st.sem, 1))
            sec("tensor", f_tens)
            st.v += 1
            LS["stats_st"][i] = st.v

        def ln2_s3(i):
            stw = LS["stats_st"][i]

            def f_vec(eng, stw=stw):
                eng.wait_ge(st.sem, stw)
                eng.tensor_copy(out=LN["m"][0:1, 0:TB], in_=pl1[0:1, 0:TB])
                eng.tensor_copy(out=LN["v"][0:1, 0:TB], in_=pl2[0:1, 0:TB])
                eng.drain()
                eng.tensor_scalar_mul(out=LN["m"][0:1, 0:TB],
                                      in0=LN["m"][0:1, 0:TB], scalar1=1.0 / D)
                eng.drain()
                eng.tensor_mul(out=s_scr[0:1, 0:TB], in0=LN["m"][0:1, 0:TB],
                               in1=LN["m"][0:1, 0:TB])
                eng.drain()
                eng.scalar_tensor_tensor(out=LN["v"][0:1, 0:TB],
                                         in0=LN["v"][0:1, 0:TB],
                                         scalar=1.0 / D,
                                         in1=s_scr[0:1, 0:TB],
                                         op0=ALU.mult, op1=ALU.subtract)
                eng.drain()
                eng.tensor_scalar_add(out=LN["v"][0:1, 0:TB],
                                      in0=LN["v"][0:1, 0:TB], scalar1=EPS)
                eng.maybe_drain_then_inc((sv.sem, 1))
            sec("vector", f_vec)
            sv.v += 1
            svw = sv.v

            def f_scal2(eng, svw=svw):
                eng.wait_ge(sv.sem, svw)
                eng.activation(out=s_rs[0:1, 0:TB], in_=LN["v"][0:1, 0:TB],
                               func=AF.Ln)
                eng.drain()
                eng.activation(out=s_rs[0:1, 0:TB], in_=s_rs[0:1, 0:TB],
                               func=AF.Exp, scale=-0.5)
                eng.maybe_drain_then_inc((ss.sem, 1))
            sec("scalar", f_scal2)
            ss.v += 1
            ssw = ss.v

            def f_vec2(eng, ssw=ssw):
                eng.wait_ge(ss.sem, ssw)
                eng.tensor_scalar_mul(out=s_negm[0:1, 0:TB],
                                      in0=LN["m"][0:1, 0:TB], scalar1=-1.0)
                eng.drain()
                eng.tensor_copy(out=LN["negm_bf"][0:1, 0:TB],
                                in_=s_negm[0:1, 0:TB])
                eng.tensor_copy(out=LN["rs_bf"][0:1, 0:TB],
                                in_=s_rs[0:1, 0:TB])
                eng.maybe_drain_then_inc((sv.sem, 1))
            sec("vector", f_vec2)
            sv.v += 1
            LS["vec2_sv"][i] = sv.v

        def ln2_s4(i):
            svw = LS["vec2_sv"][i]

            def f_tens2(eng, svw=svw):
                eng.wait_ge(sv.sem, svw)
                eng.matmul(out=pl1[:, 0:TB], lhsT=s_onesrow_bf[:, :],
                           rhs=LN["negm_bf"][0:1, 0:TB], start=True, stop=True)
                eng.matmul(out=pl2[:, 0:TB], lhsT=s_onesrow_bf[:, :],
                           rhs=LN["rs_bf"][0:1, 0:TB], start=True, stop=True)
                eng.maybe_drain_then_inc((st.sem, 1))
            sec("tensor", f_tens2)
            st.v += 1
            LS["bcast_st"][i] = st.v

        def vec3copy(i):
            stw = LS["bcast_st"][i]
            sgw = LS["ln2_sg"].get(i - 1, 0)

            def f_v3c(eng, stw=stw, sgw=sgw):
                eng.wait_ge(st.sem, stw)
                eng.wait_ge(sg.sem, sgw)
                eng.activation(out=LN["mb"][:, 0:TB], in_=pl1[:, 0:TB],
                               func=AF.Copy)
                eng.activation(out=rsb_bf[:, 0:TB], in_=pl2[:, 0:TB],
                               func=AF.Copy)
                eng.maybe_drain_then_inc((ss.sem, 1))
            sec("scalar", f_v3c)
            ss.v += 1
            LS["vec3c_ss"][i] = ss.v

        def ln2_s5b(i):
            A = i % 3
            ssw = LS["vec3c_ss"][i]
            svw = LS["vech_sv"].get(i - 1, 0)
            qw = LS["qs_load"][i]

            def f_gp(eng, A=A, ssw=ssw, svw=svw, qw=qw):
                eng.wait_ge(ss.sem, ssw)
                eng.wait_ge(sv.sem, svw)
                eng.wait_ge(qs.sem, qw)
                for a in range(ND):
                    eng.tensor_add(out=ln2_b[:, a * TB1 + 1:a * TB1 + 1 + TB],
                                   in0=xm3[A][:, a * TB:(a + 1) * TB],
                                   in1=LN["mb"][:, 0:TB])
                eng.drain()
                for a in range(ND):
                    eng.tensor_mul(out=ln2_b[:, a * TB1 + 1:a * TB1 + 1 + TB],
                                   in0=ln2_b[:, a * TB1 + 1:a * TB1 + 1 + TB],
                                   in1=rsb_bf[:, 0:TB])
                eng.maybe_drain_then_inc((sg.sem, 1))
            sec("gpsimd", f_gp)
            sg.v += 1
            LS["ln2_sg"][i] = sg.v

        def vech(i):
            def f_vech(eng):
                for a in range(ND):
                    eng.tensor_copy(out=ln2_b[:, a * TB1:a * TB1 + 1],
                                    in_=ln2_b[:, a * TB1 + TB:a * TB1 + TB + 1])
                eng.maybe_drain_then_inc((sv.sem, 1))
            sec("vector", f_vech)
            sv.v += 1
            LS["vech_sv"][i] = sv.v

        def stt_half(i, pi):
            mu, imu, dst = ((s_mur, s_imur, xr_s),
                            (s_muk, s_imuk, xk_s))[pi]
            sgw = LS["ln2_sg"][i]
            svw = (LS["vecmk_sv"].get(i - 1, 0) if pi == 0
                   else LS["vecmr_sv"][i])
            stw = (LS["rmm_st"].get(i - 1, 0) if pi == 0
                   else LS["kmm_st"].get(i - 1, 0))

            def f_scal(eng, mu=mu, sgw=sgw, svw=svw):
                eng.wait_ge(sg.sem, sgw)
                eng.wait_ge(sv.sem, svw)
                for a in range(ND):
                    eng.activation(out=t1a[:, a * TB:(a + 1) * TB],
                                   in_=ln2_b[:, a * TB1 + 1:a * TB1 + 1 + TB],
                                   func=AF.Copy, scale=mu[:, a:a + 1])
                eng.maybe_drain_then_inc((ss.sem, 1))
            sec("scalar", f_scal)
            ss.v += 1
            ssw = ss.v

            def f_vecm(eng, imu=imu, dst=dst, ssw=ssw, stw=stw, sgw=sgw):
                eng.wait_ge(ss.sem, ssw)
                eng.wait_ge(st.sem, stw)
                eng.wait_ge(sg.sem, sgw)
                for a in range(ND):
                    eng.scalar_tensor_tensor(
                        out=dst[:, a * TB:(a + 1) * TB],
                        in0=ln2_b[:, a * TB1:a * TB1 + TB],
                        scalar=imu[:, a:a + 1],
                        in1=t1a[:, a * TB:(a + 1) * TB],
                        op0=ALU.mult, op1=ALU.add)
                eng.maybe_drain_then_inc((sv.sem, 1))
            sec("vector", f_vecm)
            sv.v += 1
            if pi == 0:
                LS["vecmr_sv"][i] = sv.v
            else:
                LS["vecmk_sv"][i] = sv.v

        def rmm(i):
            def mmsr(eng, o, bi):
                for (off, cw) in nchm:
                    for a in range(ND):
                        eng.matmul(out=fb[bi][:, off:off + cw],
                                   lhsT=w_r_bf[:, a * D + o * 128:a * D + (o + 1) * 128],
                                   rhs=xr_s[:, a * TB + off:a * TB + off + cw],
                                   start=(a == 0), stop=(a == ND - 1))

            def evr(eng, o, bi):
                eng.activation(out=r_f32[:, o * TB:(o + 1) * TB],
                               in_=fb[bi][:, 0:TB], func=AF.Sigmoid)
            cpw = ((sv, LS["evv_sv"][i - 1]),) if i > 0 else ()
            mm_rounds(ND, 3, mmsr, "scalar", evr,
                      pre_wait=((sv, LS["vecmr_sv"][i]),), cons_pre_wait=cpw)
            LS["sig_ss"][i] = ss.v
            LS["rmm_st"][i] = st.v

        def kmm(i):
            def mmsk(eng, o, bi):
                for (off, cw) in nchm:
                    for a in range(ND):
                        eng.matmul(out=gb[bi][:, off:off + cw],
                                   lhsT=w_k_bf[:, a * D + o * 128:a * D + (o + 1) * 128],
                                   rhs=xk_s[:, a * TB + off:a * TB + off + cw],
                                   start=(a == 0), stop=(a == ND - 1))

            def evk(eng, o, bi):
                eng.tensor_scalar_max(out=kr_bf[:, o * TB:(o + 1) * TB],
                                      in0=gb[bi][:, 0:TB], scalar1=0.0)
            mm_rounds(ND, 3, mmsk, "vector", evk,
                      pre_wait=((sv, LS["vecmk_sv"][i]),))
            LS["kmm_st"][i] = st.v

        def sq(i):
            def f_veck(eng):
                for o in range(ND):
                    eng.tensor_mul(out=kr_bf[:, o * TB:(o + 1) * TB],
                                   in0=kr_bf[:, o * TB:(o + 1) * TB],
                                   in1=kr_bf[:, o * TB:(o + 1) * TB])
                eng.maybe_drain_then_inc((sv.sem, 1))
            sec("vector", f_veck)
            sv.v += 1
            LS["sq_sv"][i] = sv.v

        def vmm(i):
            def mmsv(eng, o, bi):
                for (off, cw) in nchm:
                    for a in range(ND):
                        eng.matmul(out=fb[bi][:, off:off + cw],
                                   lhsT=w_v_bf[:, a * D + o * 128:a * D + (o + 1) * 128],
                                   rhs=kr_bf[:, a * TB + off:a * TB + off + cw],
                                   start=(a == 0), stop=(a == ND - 1))

            def evv(eng, o, bi):
                eng.tensor_mul(out=t1v[:, o * TB:(o + 1) * TB],
                               in0=r_f32[:, o * TB:(o + 1) * TB],
                               in1=fb[bi][:, 0:TB])
            cpw = [(ss, LS["sig_ss"][i])]
            if i > 0:
                cpw.append((qg, LS["qs_out"][i - 1]))
            mm_rounds(ND, 3, mmsv, "vector", evv,
                      pre_wait=((sv, LS["sq_sv"][i]),),
                      cons_pre_wait=tuple(cpw))
            LS["evv_sv"][i] = sv.v

        def fin(i):
            A = i % 3
            svw = LS["evv_sv"][i]

            def f_gp(eng, A=A, svw=svw, c0=i * TB):
                eng.wait_ge(sv.sem, svw)
                for a in range(ND):
                    eng.tensor_add(out=t1v[:, a * TB:(a + 1) * TB],
                                   in0=t1v[:, a * TB:(a + 1) * TB],
                                   in1=xm3[A][:, a * TB:(a + 1) * TB])
                eng.drain()
                eng.dma_start(out=bass.AP(out_d, c0, [[CT, 128],
                                                      [128 * CT, ND], [1, TB]]),
                              in_=t1v[:, :]).then_inc(qg.sem, 16)
                qg.v += 16
                eng.sem_inc(sg.sem, 1)
            sec("gpsimd", f_gp)
            sg.v += 1
            LS["gadd_sg"][i] = sg.v
            LS["qs_out"][i] = qg.v

        load_blk(0)
        vsq(0)
        ln2_s2(0)
        ln2_s3(0)
        ln2_s4(0)
        vec3copy(0)
        ln2_s5b(0)
        load_blk(1)
        vsq(1)
        ln2_s2(1)
        ln2_s3(1)
        ln2_s4(1)
        vec3copy(1)
        stt_half(0, 0)
        stt_half(0, 1)
        vech(0)
        ln2_s5b(1)
        for i in range(NB):
            if i + 2 < NB:
                load_blk(i + 2)
            rmm(i)
            kmm(i)
            if i + 2 < NB:
                vsq(i + 2)
                ln2_s2(i + 2)
                ln2_s3(i + 2)
                ln2_s4(i + 2)
                vec3copy(i + 2)
            sq(i)
            vmm(i)
            if i + 1 < NB:
                stt_half(i + 1, 0)
                stt_half(i + 1, 1)
                vech(i + 1)
            if i + 2 < NB:
                ln2_s5b(i + 2)
            fin(i)
        barrier()
        ph.close()

    upto = p.get("upto", 99)
    emit_init()
    if upto >= 1:
        emit_phase1a()
    if upto >= 2:
        emit_collective()
        coll_es.close()
    if upto >= 3:
        emit_phase1b()
    if upto >= 4:
        scan_es.close()
        emit_phase2()

    if DBG:
        def f_dbg2(eng):
            eng.dma_start(out=d_ln[:, :], in_=ln1_sp[:, :]).then_inc(qs.sem, 16)
            qs.v += 16
            eng.dma_start(out=d_xc[:, :], in_=xconv_sp[:, :]).then_inc(qs.sem, 16)
            qs.v += 16
            eng.dma_start(out=d_xmid[:, :], in_=xmid_sp[:, :]).then_inc(qs.sem, 16)
            qs.v += 16
            eng.wait_ge(qs.sem, qs.v)
        sec("sync", f_dbg2)

    def f_final(eng):
        eng.wait_ge(qs.sem, qs.v)
    sec("sync", f_final)

    def f_finalg(eng):
        eng.wait_ge(qg.sem, qg.v)
    sec("gpsimd", f_finalg)

    es.close()
    return nc



def make_consts():
    tri = np.zeros((128, 128), np.float32)
    for s in range(128):
        tri[s, s:] = 1.0
    mask = np.zeros((128, 128), np.float32)
    for s in range(128):
        mask[s, :s] = BIG
    return dict(
        c_id_bf=np.eye(128, dtype=np.float32).astype(ml_dtypes.bfloat16),
        c_id_f32=np.eye(128, dtype=np.float32),
        c_tri=tri,
        c_mask=mask,
        c_ones=np.ones((1, 128), np.float32),
        c_allones=np.ones((128, 128), np.float32),
    )


def prep_inputs(p, inp):
    n_cores = p["n_cores"]
    CPB = p["cores_per_batch"]
    D = p["d_model"]
    E = p["d_inner"]
    CT = p["ct"]
    ND = D // 128
    NE = E // 128

    x = np.asarray(inp["L0_attn_x"], np.float32)
    conv_w = np.asarray(inp["L0_attn_conv_w"], np.float32)[:, 0, :]
    w_cd = np.zeros((E, 512), np.float32)
    for dt in range(NE):
        for k in range(4):
            blk = np.zeros((128, 128), np.float32)
            np.fill_diagonal(blk, conv_w[dt * 128:(dt + 1) * 128, k])
            w_cd[dt * 128:(dt + 1) * 128, k * 128:(k + 1) * 128] = blk

    def colt(v, nt):
        return np.ascontiguousarray(np.asarray(v, np.float32).reshape(nt, 128).T)

    shared = dict(
        w_inx=np.ascontiguousarray(
            np.asarray(inp["L0_attn_in_proj_w"], np.float32)[:E].T),
        w_inz=np.ascontiguousarray(
            np.asarray(inp["L0_attn_in_proj_w"], np.float32)[E:].T),
        w_xp=np.ascontiguousarray(
            np.asarray(inp["L0_attn_x_proj_w"], np.float32).T),
        w_out=np.ascontiguousarray(
            np.asarray(inp["L0_attn_out_proj_w"], np.float32).T),
        w_r=np.ascontiguousarray(np.asarray(inp["L0_ffn_W_r"], np.float32).T),
        w_k=np.ascontiguousarray(np.asarray(inp["L0_ffn_W_k"], np.float32).T),
        w_v=np.ascontiguousarray(np.asarray(inp["L0_ffn_W_v"], np.float32).T),
        w_cd=w_cd,
        cb_col=colt(inp["L0_attn_conv_b"], NE),
        d_col=colt(inp["L0_attn_D"], NE),
        mur_col=colt(inp["L0_ffn_mu_r"], ND),
        muk_col=colt(inp["L0_ffn_mu_k"], ND),
        alog_row=np.ascontiguousarray(
            np.asarray(inp["L0_attn_A_log"], np.float32)[None, :]),
    )
    shared.update(make_consts())

    in_maps = []
    for c in range(n_cores):
        b = c // CPB
        q0 = (c % CPB) * CT
        xs = np.zeros((D, CT + 4), np.float32)
        lo = q0 - 4
        src_lo = max(lo, 0)
        xs[:, src_lo - lo:] = np.ascontiguousarray(x[b, src_lo:q0 + CT, :].T)
        maskrow = np.zeros((1, n_cores), np.float32)
        predT = np.zeros((n_cores, n_cores), np.float32)
        for j in range(n_cores):
            if j // CPB == b and j < c:
                maskrow[0, j] = 1.0
                for u in range(n_cores):
                    if u // CPB == b and j < u < c:
                        predT[u, j] = 1.0
        m = dict(shared)
        m["x"] = np.ascontiguousarray(xs)
        m["c_maskrow"] = maskrow
        m["c_predT"] = predT
        m["c_hscale"] = np.array([[0.0 if c % CPB == 0 else 1.0]], np.float32)
        in_maps.append(m)
    return in_maps


_CACHE = {}


def kernel(**inputs):
    p = dict(REAL)
    if "real" not in _CACHE:
        _CACHE["real"] = build_program(p)
    nc = _CACHE["real"]
    in_maps = prep_inputs(p, inputs)
    res = run_bass_kernel_spmd(nc, in_maps, core_ids=list(range(p["n_cores"])))
    B, T, D, CT, CPB = (p["batch"], p["seq"], p["d_model"], p["ct"],
                        p["cores_per_batch"])
    out = np.empty((B, T, D), np.float32)
    for c in range(p["n_cores"]):
        b = c // CPB
        q0 = (c % CPB) * CT
        out[b, q0:q0 + CT, :] = res.results[c]["out"].T
    return out
